# version 18
# baseline (speedup 1.0000x reference)
import numpy as np
import ml_dtypes

import concourse.bass as bass
import concourse.mybir as mybir
from concourse.bass_utils import run_bass_kernel_spmd

F32 = mybir.dt.float32
BF16 = mybir.dt.bfloat16
I32 = mybir.dt.int32
I8 = mybir.dt.int8
AF = mybir.ActivationFunctionType
ALU = mybir.AluOpType
AX = mybir.AxisListType
BF16_NP = ml_dtypes.bfloat16

B, S, D, HDIM, INNER = 8, 4096, 1024, 128, 4096
NT = S // 128
QI = S // 256
NWA = 48
NWB = 64
INV_SQRT_HD = 0.08838834764831845
N_ELEM_W = float(INNER * D)

ENGINES = ("sync", "tensor", "vector", "scalar", "gpsimd")


class Prog:
    def __init__(self):
        self.streams = {e: [] for e in ENGINES}
        self.counts = {}
        self._h = None

    def sem(self, name):
        self.counts[name] = 0

    def emit(self, eng, fn):
        self.streams[eng].append(fn)

    def wait(self, eng, sem_name, val):
        if val <= 0:
            return
        self.emit(eng, lambda E, s=sem_name, v=val: E.wait_ge(self._h[s], v))

    def op(self, eng, builder, inc=None, n=1):
        if inc is None:
            self.emit(eng, builder)
        else:
            self.counts[inc] += n

            def fn(E, b=builder, s=inc, k=n):
                b(E).then_inc(self._h[s], k)
            self.emit(eng, fn)

    def c(self, name):
        return self.counts[name]

    def opf(self, eng, builder):
        s = "chain_" + eng
        self.op(eng, builder, inc=s)
        self.wait(eng, s, self.counts[s])


def build_nc(dbg=False):
    nc = bass.Bass("TRN2", target_bir_lowering=False,
                   detect_race_conditions=False)

    x_d = nc.dram_tensor("x", [S, D], F32, kind="ExternalInput").ap()
    qkvT_d = nc.dram_tensor("qkvT", [D, 384], F32, kind="ExternalInput").ap()
    woT_d = nc.dram_tensor("woT", [HDIM, D], F32, kind="ExternalInput").ap()
    wgT_d = nc.dram_tensor("wgT", [D, INNER], F32, kind="ExternalInput").ap()
    wuT_d = nc.dram_tensor("wuT", [D, INNER], F32, kind="ExternalInput").ap()
    wdT_d = nc.dram_tensor("wdT", [INNER, D], F32, kind="ExternalInput").ap()
    trig_d = nc.dram_tensor("trig", [S, 128], F32, kind="ExternalInput").ap()
    idb_d = nc.dram_tensor("idb", [128, 128], BF16, kind="ExternalInput").ap()
    idf_d = nc.dram_tensor("idf", [128, 128], F32, kind="ExternalInput").ap()
    onesb_d = nc.dram_tensor("onesb", [128, 1], BF16, kind="ExternalInput").ap()
    onesf_d = nc.dram_tensor("onesf", [128, 1], F32, kind="ExternalInput").ap()
    onesrow_d = nc.dram_tensor("onesrow", [1, 128], F32, kind="ExternalInput").ap()
    out_d = nc.dram_tensor("out", [S, D], F32, kind="ExternalOutput").ap()
    wdq_d = nc.dram_tensor("wdq_scratch", [128, 32768], BF16, kind="Internal").ap()

    BASE = 16512
    TOP = 229376
    A = {}
    def sb(name, shape, dtype, off):
        assert off % 32 == 0, (name, off)
        A[name] = nc.alloc_sbuf_tensor_at(name, shape, dtype, offset=off)
        return A[name]

    o = BASE
    sb("WG", [128, 8 * INNER], BF16, o); o += 65536
    sb("WU", [128, 8 * INNER], BF16, o); o += 65536
    sb("ATT", [128, S], BF16, o); o += 8192
    sb("WO", [128, D], BF16, o); o += 2048
    for nm, shp, dt, szb in [
        ("MEAN", [128, NT], F32, 128), ("RS", [128, NT], F32, 128),
        ("RDEN", [128, NT], F32, 128),
        ("IDB", [128, 128], BF16, 256), ("IDF", [128, 128], F32, 512),
        ("ONESB", [128, 1], BF16, 32), ("ONESF", [128, 1], F32, 32),
        ("ONESROW", [1, 128], F32, 512),
        ("NEGB", [128, 1], F32, 32),
        ("SWG", [128, 1], F32, 32), ("SWU", [128, 1], F32, 32), ("SWD", [128, 1], F32, 32),
        ("MWG127n", [128, 1], F32, 32), ("MWU127n", [128, 1], F32, 32),
        ("MWD127n", [128, 1], F32, 32),
        ("ACCG", [128, 1], F32, 32), ("ACCU", [128, 1], F32, 32), ("ACCD", [128, 1], F32, 32),
        ("PAP", [128, 2], F32, 32),
        ("SC1", [128, 8], F32, 32),
        ("SC2", [1, 8], F32, 32),
        ("DEN", [1, 256], F32, 1024),
    ]:
        sb(nm, shp, dt, o); o += szb
    PERS_END = o
    OVA = (o + 31) // 32 * 32
    sb("KF", [128, S], BF16, OVA)
    sb("QF", [128, S], BF16, OVA + 8192)
    sb("VT", [128, S], BF16, OVA + 16384)
    OVB = OVA + 24576
    sb("QKVW", [128, 3072], BF16, OVB)
    sb("XB", [128, 2048], F32, OVB + 6144)
    sb("XN", [128, 1024], F32, OVB + 14336)
    sb("XNT", [128, 1024], BF16, OVB + 18432)
    sb("RT", [128, 384], F32, OVB + 20480)
    sb("TRG", [128, 256], F32, OVB + 22016)
    sb("WLD", [128, 4096], F32, OVB + 23040)
    sb("SQS", [128, 256], BF16, OVB + 39424)
    assert OVB + 39936 <= TOP
    sb("PT", [128, 8192], BF16, OVB)
    sb("WLB", [128, 4096], F32, OVB + 16384)
    sb("WDQA0", [128, 1024], BF16, OVB + 16384)
    sb("WDQA1", [128, 1024], BF16, OVB + 24576)
    sb("TL", [128, 1024], F32, OVB + 32768)
    sb("TG", [128, 1024], F32, OVB + 36864)
    assert OVB + 40960 <= TOP
    o = OVA
    sb("H", [128, 4096], F32, o); o += 16384
    sb("HQT", [128, 8192], BF16, o); o += 16384
    sb("WD", [128, 2048], BF16, o); o += 4096
    sb("XN2", [128, 1024], F32, o); o += 4096
    sb("ATX", [128, 1024], BF16, o); o += 2048
    sb("QS1", [128, 1024], F32, o); o += 4096
    sb("QS2", [128, 1024], I32, o); o += 4096
    sb("YO", [128, 4096], F32, o); o += 16384
    sb("CT", [128, 16], F32, o); o += 64
    assert o <= TOP, o

    _psg = nc.psum_tensor("PS", [128, 4096], F32)
    PS = _psg.__enter__()

    P = Prog()
    sem_names = [
        "sCstG", "sCstS", "sXld0", "sXld1", "sTRG0", "sTRG1", "sWA0", "sWA1",
        "sSabs", "sVacc",
        "sSsq", "sVrec", "sSrs0", "sVxn",
        "sTtr", "sSxnt", "sTqkv", "sVrope", "sTqk2", "sSqk", "sSv",
        "sVnorm", "sTnorm", "sVbp", "sSbp", "sTbp",
        "sTsc", "sSexp", "sTpv", "sTden", "sSatt", "sSden", "sTrd", "sVrd",
        "sWB0", "sWB1", "sWBdone", "sDQ0", "sDQ1",
        "sX2ld", "sWDl0", "sWDl1", "sWDl2", "sWDl3",
        "sVaq", "sTat2", "sSat2", "sTgu", "sSgl", "sVh", "sVq",
        "sThq", "sShq", "sTpj", "sTdn", "sVout", "sOut0", "sOut1",
        "chain_vector", "chain_scalar", "chain_tensor", "chain_sync", "chain_gpsimd",
        "sVln", "sVsh", "sSqp", "sVqd",
    ]
    sems = {}
    for nm in sem_names:
        sems[nm] = nc.semaphore(nm).__enter__()
        P.sem(nm)
    P._h = sems

    KF, QF, VT = A["KF"], A["QF"], A["VT"]
    ATT, WO, QKVW = A["ATT"], A["WO"], A["QKVW"]
    XB, XN, XNT, RT, TRG, WLD, SQS = (A["XB"], A["XN"], A["XNT"], A["RT"],
                                      A["TRG"], A["WLD"], A["SQS"])
    MEAN, RS, RDEN = A["MEAN"], A["RS"], A["RDEN"]
    IDB, IDF = A["IDB"], A["IDF"]
    ONESB, ONESF, ONESROW = A["ONESB"], A["ONESF"], A["ONESROW"]
    NEGB, SC1, SC2, DEN, PAP = A["NEGB"], A["SC1"], A["SC2"], A["DEN"], A["PAP"]
    PT, WLB, TL, TG = A["PT"], A["WLB"], A["TL"], A["TG"]
    WDQA = (A["WDQA0"], A["WDQA1"])
    WG, WU = A["WG"], A["WU"]
    H, HQT, WD = A["H"], A["HQT"], A["WD"]
    XN2, ATX = A["XN2"], A["ATX"]
    QS1, QS2, YO, CT = A["QS1"], A["QS2"], A["YO"], A["CT"]

    for d in range(8):
        P.op("gpsimd", lambda E, d=d: E.dma_start(
            out=QKVW[:, 384 * d:384 * (d + 1)],
            in_=qkvT_d[128 * d:128 * (d + 1), :]), inc="sCstG", n=16)
    P.op("gpsimd", lambda E: E.dma_start(out=WO[:, :], in_=woT_d), inc="sCstG", n=16)
    for dst, src in [(IDB, idb_d), (IDF, idf_d), (ONESB, onesb_d),
                     (ONESF, onesf_d), (ONESROW, onesrow_d)]:
        P.op("sync", lambda E, d=dst, s=src: E.dma_start(out=d[:, :], in_=s),
             inc="sCstS", n=16)

    def p1_sync_tile(t):
        P.wait("sync", "sVxn", t - 1)
        P.op("sync", lambda E, t=t: E.dma_start(
            out=XB[:, 1024 * (t % 2):1024 * (t % 2 + 1)],
            in_=x_d[128 * t:128 * (t + 1), :]), inc="sXld%d" % (t % 2), n=16)
        P.wait("sync", "sVrope", t - 1)
        P.op("sync", lambda E, t=t: E.dma_start(
            out=TRG[:, 128 * (t % 2):128 * (t % 2 + 1)],
            in_=trig_d[128 * t:128 * (t + 1), :]), inc="sTRG%d" % (t % 2), n=16)

    def wa_src(j):
        m, jj = j // 16, j % 16
        rb, h = jj // 2, jj % 2
        if m == 0:
            return wgT_d[128 * rb:128 * (rb + 1), 2048 * h:2048 * (h + 1)]
        if m == 1:
            return wuT_d[128 * rb:128 * (rb + 1), 2048 * h:2048 * (h + 1)]
        return wdT_d.rearrange("(c p two) n -> c p (two n)", two=2, p=128)[jj]

    def p1_sync_passA(j):
        P.wait("sync", "sSabs", j - 1)
        P.op("sync", lambda E, j=j: E.dma_start(
            out=WLD[:, 2048 * (j % 2):2048 * (j % 2 + 1)], in_=wa_src(j)),
            inc="sWA%d" % (j % 2), n=16)

    def p1_vector(t):
        sl = XB[:, 1024 * (t % 2):1024 * (t % 2 + 1)]
        P.wait("vector", "sXld%d" % (t % 2), 16 * (t // 2 + 1))
        P.opf("vector", lambda E, sl=sl: E.reduce_sum(SC1[:, 0:1], sl, axis=AX.X))
        P.opf("vector", lambda E: E.tensor_scalar(SC1[:, 0:1], SC1[:, 0:1],
                                                  1.0 / 1024.0, None, ALU.mult))
        P.wait("vector", "sSsq", t + 1)
        P.opf("vector", lambda E: E.tensor_scalar(SC1[:, 1:2], SC1[:, 2:3],
                                                  1.0 / 1024.0, None, ALU.mult))
        P.opf("vector", lambda E: E.tensor_tensor(SC1[:, 4:5], SC1[:, 0:1],
                                                  SC1[:, 0:1], ALU.mult))
        P.opf("vector", lambda E: E.tensor_sub(SC1[:, 1:2], SC1[:, 1:2], SC1[:, 4:5]))
        P.opf("vector", lambda E: E.tensor_scalar(SC1[:, 1:2], SC1[:, 1:2],
                                                  1e-5, None, ALU.add))
        P.op("vector", lambda E: E.reciprocal(SC1[:, 2:3], SC1[:, 1:2]), inc="sVrec")
        P.opf("vector", lambda E, t=t: E.tensor_copy(MEAN[:, t:t + 1], SC1[:, 0:1]))
        P.wait("vector", "sSrs0", t + 1)
        P.opf("vector", lambda E: E.tensor_tensor(SC1[:, 4:5], SC1[:, 3:4],
                                                  SC1[:, 3:4], ALU.mult))
        P.opf("vector", lambda E: E.tensor_tensor(SC1[:, 4:5], SC1[:, 4:5],
                                                  SC1[:, 1:2], ALU.mult))
        P.opf("vector", lambda E: E.tensor_scalar(SC1[:, 4:5], SC1[:, 4:5],
                                                  -0.5, 1.5, ALU.mult, ALU.add))
        P.opf("vector", lambda E, t=t: E.tensor_tensor(RS[:, t:t + 1], SC1[:, 3:4],
                                                       SC1[:, 4:5], ALU.mult))
        P.op("vector", lambda E, sl=sl, t=t: E.tensor_scalar(
            XN[:, :], sl, MEAN[:, t:t + 1], RS[:, t:t + 1],
            ALU.subtract, ALU.mult), inc="sVxn")

    def p1_rope(t):
        P.wait("vector", "sTqkv", t + 1)
        P.wait("vector", "sTRG%d" % (t % 2), 16 * (t // 2 + 1))
        cosv = TRG[:, 128 * (t % 2):128 * (t % 2) + 64]
        sinv = TRG[:, 128 * (t % 2) + 64:128 * (t % 2) + 128]
        for base, off in ((0, 0), (512, 128)):
            ev = PS[:, base:base + 127:2]
            od = PS[:, base + 1:base + 128:2]
            dst = RT[:, off:off + 128]
            scr = RT[:, 256:320]
            last = (base == 512)
            P.opf("vector", lambda E, ev=ev, cosv=cosv, dst=dst: E.tensor_tensor(
                dst[:, 0:127:2], ev, cosv, ALU.mult))
            P.opf("vector", lambda E, od=od, sinv=sinv, scr=scr: E.tensor_tensor(
                scr, od, sinv, ALU.mult))
            P.op("vector", lambda E, dst=dst, scr=scr: E.tensor_sub(
                dst[:, 0:127:2], dst[:, 0:127:2], scr))
            P.opf("vector", lambda E, ev=ev, sinv=sinv, dst=dst: E.tensor_tensor(
                dst[:, 1:128:2], ev, sinv, ALU.mult))
            P.opf("vector", lambda E, od=od, cosv=cosv, scr=scr: E.tensor_tensor(
                scr, od, cosv, ALU.mult))
            P.op("vector", lambda E, dst=dst, scr=scr: E.tensor_add(
                dst[:, 1:128:2], dst[:, 1:128:2], scr),
                inc=("sVrope" if last else None))

    def p1_scalar(t):
        sl = XB[:, 1024 * (t % 2):1024 * (t % 2 + 1)]
        P.wait("scalar", "sXld%d" % (t % 2), 16 * (t // 2 + 1))
        P.wait("scalar", "sVrec", t)
        P.wait("scalar", "sTtr", t)
        P.op("scalar", lambda E, sl=sl: E.activation(
            XN[:, :], sl, AF.Square, accum_out=SC1[:, 2:3]), inc="sSsq")
        P.wait("scalar", "sVrec", t + 1)
        P.wait("scalar", "sVxn", t)
        P.op("scalar", lambda E: E.activation(SC1[:, 3:4], SC1[:, 2:3], AF.Sqrt),
             inc="sSrs0")
        P.wait("scalar", "sTtr", t + 1)
        P.op("scalar", lambda E: E.copy(XNT[:, 0:512], PS[:, 3072:3584]))
        P.op("scalar", lambda E: E.copy(XNT[:, 512:1024], PS[:, 3584:4096]),
             inc="sSxnt")
        P.wait("scalar", "sTqkv", t + 1)
        P.op("scalar", lambda E, t=t: E.copy(VT[:, 128 * t:128 * (t + 1)],
                                             PS[:, 1024:1152]), inc="sSv")
        P.wait("scalar", "sTqk2", t + 1)
        P.op("scalar", lambda E, t=t: E.copy(QF[:, 128 * t:128 * (t + 1)],
                                             PS[:, 1536:1664]))
        P.op("scalar", lambda E, t=t: E.copy(KF[:, 128 * t:128 * (t + 1)],
                                             PS[:, 1664:1792]), inc="sSqk")

    def passA_chunk(j):
        P.wait("scalar", "sWA%d" % (j % 2), 16 * (j // 2 + 1))
        P.wait("scalar", "sVacc", j - 1)
        P.op("scalar", lambda E, j=j: E.activation(
            WLD[:, 2048 * (j % 2):2048 * (j % 2 + 1)],
            WLD[:, 2048 * (j % 2):2048 * (j % 2 + 1)], AF.Abs,
            accum_out=PAP[:, j % 2:j % 2 + 1]), inc="sSabs")
        acc = A["ACCG"] if j < 16 else (A["ACCU"] if j < 32 else A["ACCD"])
        P.wait("vector", "sSabs", j + 1)
        if j % 16 == 0:
            P.op("vector", lambda E, acc=acc, j=j: E.tensor_copy(
                acc[:, :], PAP[:, j % 2:j % 2 + 1]), inc="sVacc")
        else:
            P.op("vector", lambda E, acc=acc, j=j: E.tensor_add(
                acc[:, :], acc[:, :], PAP[:, j % 2:j % 2 + 1]), inc="sVacc")
        P.wait("vector", "sVacc", P.c("sVacc"))

    def p1_tensor(t):
        P.wait("tensor", "sVxn", t + 1)
        P.wait("tensor", "sSxnt", t)
        for d in range(8):
            base = 3072 + 128 * d
            P.op("tensor", lambda E, d=d, base=base: E.transpose(
                PS[:, base:base + 128], XN[:, 128 * d:128 * (d + 1)], IDF[:, :]),
                inc=("sTtr" if d == 7 else None))
        P.wait("tensor", "sSxnt", t + 1)
        P.wait("tensor", "sVrope", t)
        P.wait("tensor", "sSv", t)
        for d in range(8):
            lhsT = XNT[:, 128 * d:128 * (d + 1)]
            st, sp = (d == 0), (d == 7)
            P.op("tensor", lambda E, lhsT=lhsT, d=d, st=st, sp=sp: E.matmul(
                PS[:, 0:128], lhsT, QKVW[:, 384 * d:384 * d + 128],
                start=st, stop=sp))
            P.op("tensor", lambda E, lhsT=lhsT, d=d, st=st, sp=sp: E.matmul(
                PS[:, 512:640], lhsT, QKVW[:, 384 * d + 128:384 * d + 256],
                start=st, stop=sp))
            P.op("tensor", lambda E, lhsT=lhsT, d=d, st=st, sp=sp: E.matmul(
                PS[:, 1024:1152], lhsT, QKVW[:, 384 * d + 256:384 * d + 384],
                start=st, stop=sp), inc=("sTqkv" if sp else None))
        P.wait("tensor", "sVrope", t + 1)
        P.wait("tensor", "sSqk", t)
        P.op("tensor", lambda E: E.transpose(PS[:, 1536:1664], RT[:, 0:128], IDF[:, :]))
        P.op("tensor", lambda E: E.transpose(PS[:, 1664:1792], RT[:, 128:256],
                                             IDF[:, :]), inc="sTqk2")

    P.wait("tensor", "sCstG", P.c("sCstG"))
    P.wait("tensor", "sCstS", P.c("sCstS"))
    for t in range(NT):
        p1_sync_tile(t)
        p1_vector(t)
        p1_rope(t)
        p1_scalar(t)
        p1_tensor(t)
        for j in range(min(NWA, (3 * t) // 2), min(NWA, (3 * (t + 1)) // 2)):
            p1_sync_passA(j)
            passA_chunk(j)
    for j in range((3 * NT) // 2, NWA):
        p1_sync_passA(j)
        passA_chunk(j)

    P.wait("vector", "sSqk", NT)
    P.wait("tensor", "sSqk", NT)
    P.wait("tensor", "sSv", NT)
    P.wait("tensor", "sSxnt", NT)
    for w, src in ((0, KF), (1, QF)):
        for c in range(NT):
            P.wait("vector", "sTnorm", 32 * w + c - 1)
            P.op("vector", lambda E, src=src, c=c: E.tensor_tensor(
                SQS[:, 128 * (c % 2):128 * (c % 2 + 1)],
                src[:, 128 * c:128 * (c + 1)], src[:, 128 * c:128 * (c + 1)],
                ALU.mult), inc="sVnorm")
            P.wait("tensor", "sVnorm", 32 * w + c + 1)
            if w == 1 and c == 0:
                P.wait("tensor", "sVbp", 1)
            P.op("tensor", lambda E, c=c: E.matmul(
                PS[0:1, 128 * c:128 * (c + 1)], ONESB[:, :],
                SQS[:, 128 * (c % 2):128 * (c % 2 + 1)], start=True, stop=True),
                inc="sTnorm")
        P.wait("vector", "sTnorm", 32 * (w + 1))
        P.op("vector", lambda E, w=w: E.reduce_max(
            SC2[0:1, w:w + 1], PS[0:1, 0:4096], axis=AX.X), inc="sVbp")
        P.wait("vector", "sVbp", P.c("sVbp"))
    P.op("vector", lambda E: E.tensor_tensor(SC2[0:1, 2:3], SC2[0:1, 0:1],
                                             SC2[0:1, 1:2], ALU.mult), inc="sVbp")
    P.wait("vector", "sVbp", P.c("sVbp"))
    P.wait("scalar", "sVbp", P.c("sVbp"))
    P.op("scalar", lambda E: E.activation(SC2[0:1, 3:4], SC2[0:1, 2:3], AF.Sqrt),
         inc="sSbp")
    P.wait("vector", "sSbp", P.c("sSbp"))
    P.opf("vector", lambda E: E.tensor_scalar(SC2[0:1, 4:5], SC2[0:1, 3:4],
                                              INV_SQRT_HD, 1.0, ALU.mult, ALU.add))
    P.op("vector", lambda E: E.tensor_scalar(SC2[0:1, 4:5], SC2[0:1, 4:5],
                                             -1.0, None, ALU.mult), inc="sVbp")
    P.wait("vector", "sVbp", P.c("sVbp"))
    VBP1 = P.c("sVbp")
    P.wait("tensor", "sVacc", NWA)
    P.wait("tensor", "sVbp", VBP1)
    for i, acc in enumerate(("ACCG", "ACCU", "ACCD")):
        P.op("tensor", lambda E, i=i, acc=acc: E.matmul(
            PS[0:1, 2048 + 16 * i:2049 + 16 * i], ONESF[:, :], A[acc][:, :],
            start=True, stop=True), inc="sTbp")
    P.wait("vector", "sTbp", P.c("sTbp"))
    for i in range(3):
        P.op("vector", lambda E, i=i: E.tensor_scalar(
            SC2[0:1, 5 + i:6 + i], PS[0:1, 2048 + 16 * i:2049 + 16 * i],
            1.0 / N_ELEM_W, 1e-5, ALU.mult, ALU.max), inc="sVbp")
    P.wait("tensor", "sVbp", P.c("sVbp"))
    for i, col in enumerate((4, 5, 6, 7)):
        P.op("tensor", lambda E, col=col, i=i: E.matmul(
            PS[:, 2560 + 16 * i:2561 + 16 * i], ONESROW[:, :],
            SC2[0:1, col:col + 1], start=True, stop=True), inc="sTbp")
    P.wait("vector", "sTbp", P.c("sTbp"))
    P.op("vector", lambda E: E.tensor_copy(NEGB[:, :], PS[:, 2560:2561]))
    for i, (sw, mw) in enumerate((("SWG", "MWG127n"), ("SWU", "MWU127n"),
                                  ("SWD", "MWD127n"))):
        src = PS[:, 2576 + 16 * i:2577 + 16 * i]
        P.opf("vector", lambda E, sw=sw, src=src: E.reciprocal(A[sw][:, :], src))
        P.opf("vector", lambda E, mw=mw, src=src: E.tensor_scalar(
            A[mw][:, :], src, -1.0 / 127.0, None, ALU.mult))
    P.op("vector", lambda E: E.tensor_copy(SC1[:, 0:1], SC1[:, 0:1]), inc="sVbp")
    BP_DONE = P.c("sVbp")

    P.wait("sync", "sVbp", BP_DONE)

    def wb_src(u):
        if u < 16:
            rb, h = (u % 16) // 2, u % 2
            return wgT_d[128 * rb:128 * (rb + 1), 2048 * h:2048 * (h + 1)]
        if u < 32:
            uu = u - 16
            rb, h = uu // 2, uu % 2
            return wuT_d[128 * rb:128 * (rb + 1), 2048 * h:2048 * (h + 1)]
        c = u - 32
        return wdT_d[128 * c:128 * (c + 1), :]

    for u in range(NWB):
        P.wait("sync", "sWBdone", u - 1)
        if u >= 34:
            c_prev = u - 34
            P.wait("sync", "sDQ%d" % (c_prev % 2), 16 * (c_prev // 2 + 1))
        ncols = 2048 if u < 32 else 1024
        P.op("sync", lambda E, u=u, ncols=ncols: E.dma_start(
            out=WLB[:, 2048 * (u % 2):2048 * (u % 2) + ncols], in_=wb_src(u)),
            inc="sWB%d" % (u % 2), n=16)
        if u >= 32:
            c = u - 32
            P.wait("sync", "sWBdone", u + 1)
            P.op("sync", lambda E, c=c: E.dma_start(
                out=wdq_d[:, 1024 * c:1024 * (c + 1)], in_=WDQA[c % 2][:, :]),
                inc="sDQ%d" % (c % 2), n=16)

    def passB_unit(u):
        ncols = 2048 if u < 32 else 1024
        sl = WLB[:, 2048 * (u % 2):2048 * (u % 2) + ncols]
        sw = A["SWG"] if u < 16 else (A["SWU"] if u < 32 else A["SWD"])
        P.wait("vector", "sWB%d" % (u % 2), 16 * (u // 2 + 1))
        for half in range(ncols // 1024):
            hsl = sl[:, 1024 * half:1024 * (half + 1)]
            P.op("vector", lambda E, hsl=hsl, sw=sw: E.tensor_scalar(
                TL[:, :], hsl, sw[:, :], -0.5, ALU.mult, ALU.is_lt))
            P.op("vector", lambda E, hsl=hsl, sw=sw: E.tensor_scalar(
                TG[:, :], hsl, sw[:, :], 0.5, ALU.mult, ALU.is_gt))
            if u < 32:
                dstW = WG if u < 16 else WU
                cw = u % 16
                rb, hh = cw // 2, cw % 2
                dst = dstW[:, 4096 * rb + 2048 * hh + 1024 * half:
                           4096 * rb + 2048 * hh + 1024 * (half + 1)]
                last = (half == ncols // 1024 - 1)
                P.op("vector", lambda E, dst=dst: E.tensor_sub(dst, TG[:, :], TL[:, :]),
                     inc=("sWBdone" if last else None))
            else:
                c = u - 32
                if c >= 2:
                    P.wait("vector", "sDQ%d" % (c % 2), 16 * (c // 2))
                P.op("vector", lambda E, c=c: E.tensor_sub(
                    WDQA[c % 2][:, :], TG[:, :], TL[:, :]), inc="sWBdone")

    P.wait("tensor", "sVbp", BP_DONE)
    P.wait("scalar", "sVbp", BP_DONE)

    def a2_tensor_scores(i, g):
        P.wait("tensor", "sSexp", 8 * i + g - 1)
        for k in range(4):
            c = 4 * g + k
            bank = 2 * (g % 2) + k // 2
            off = 512 * bank + 256 * (k % 2)
            P.op("tensor", lambda E, c=c, off=off, i=i: E.matmul(
                PS[:, off:off + 256], KF[:, 128 * c:128 * (c + 1)],
                QF[:, 256 * i:256 * (i + 1)], start=True, stop=True),
                inc=("sTsc" if k == 3 else None))

    def a2_tensor_pv(i, g):
        P.wait("tensor", "sSexp", 8 * i + g + 1)
        if g == 0:
            P.wait("tensor", "sSatt", i)
        for k in range(4):
            c = 4 * g + k
            P.op("tensor", lambda E, c=c: E.matmul(
                PS[:, 2048:2304], VT[:, 128 * c:128 * (c + 1)],
                PT[:, 256 * c:256 * (c + 1)], start=(c == 0), stop=(c == 31)),
                inc=("sTpv" if k == 3 else None))

    for i in range(QI):
        for g in range(8):
            a2_tensor_scores(i, g)
            if g >= 1:
                a2_tensor_pv(i, g - 1)
        a2_tensor_pv(i, 7)
        P.wait("tensor", "sSden", i)
        for c in range(32):
            P.op("tensor", lambda E, c=c: E.matmul(
                PS[0:1, 2560:2816], ONESB[:, :], PT[:, 256 * c:256 * (c + 1)],
                start=(c == 0), stop=(c == 31)), inc=("sTden" if c == 31 else None))
        P.wait("tensor", "sSden", i + 1)
        P.wait("tensor", "sVrd", i)
        P.op("tensor", lambda E: E.transpose(PS[:, 3072:3073], DEN[0:1, 0:128],
                                             IDF[0:1, 0:1]))
        P.op("tensor", lambda E: E.transpose(PS[:, 3073:3074], DEN[0:1, 128:256],
                                             IDF[0:1, 0:1]), inc="sTrd")
        for g in range(8):
            if g == 0:
                P.wait("scalar", "sTden", i)
            P.wait("scalar", "sTsc", 8 * i + g + 1)
            src = PS[:, 1024 * (g % 2):1024 * (g % 2) + 1024]
            P.op("scalar", lambda E, src=src, g=g: E.activation(
                PT[:, 1024 * g:1024 * (g + 1)], src, AF.Exp,
                bias=NEGB[:, :], scale=INV_SQRT_HD), inc="sSexp")
        P.wait("scalar", "sTpv", 8 * (i + 1))
        P.op("scalar", lambda E, i=i: E.copy(ATT[:, 256 * i:256 * (i + 1)],
                                             PS[:, 2048:2304]), inc="sSatt")
        P.wait("scalar", "sTden", i + 1)
        P.op("scalar", lambda E: E.copy(DEN[0:1, :], PS[0:1, 2560:2816]),
             inc="sSden")
        P.wait("vector", "sTrd", i + 1)
        P.op("vector", lambda E, i=i: E.reciprocal(RDEN[:, 2 * i:2 * i + 2],
                                                   PS[:, 3072:3074]), inc="sVrd")
        P.wait("vector", "sVrd", P.c("sVrd"))
        for u in range(4 * i, 4 * i + 4):
            passB_unit(u)

    DQ_ALL = 16 * 32
    P.wait("sync", "sTden", QI)
    P.wait("sync", "sDQ0", 16 * 16)
    P.wait("sync", "sDQ1", 16 * 16)
    P.wait("sync", "sWBdone", NWB)
    P.wait("tensor", "sVrd", QI)
    P.wait("tensor", "sSatt", QI)
    P.wait("tensor", "sWBdone", NWB)
    P.wait("vector", "sDQ0", 16 * 16)
    P.wait("vector", "sDQ1", 16 * 16)
    P.wait("scalar", "sVrd", QI)

    for t in range(NT + 2):
        if t < NT:
            if t >= 4:
                P.wait("sync", "sOut%d" % ((t - 4) % 2), 16 * ((t - 4) // 2 + 1))
            P.op("sync", lambda E, t=t: E.dma_start(
                out=YO[:, 1024 * (t % 4):1024 * (t % 4 + 1)],
                in_=x_d[128 * t:128 * (t + 1), :]), inc="sX2ld", n=16)
        if t >= 3 and t % 2 == 1 and t <= NT + 1:
            G = (t - 3) // 2
            for c in range(32):
                P.wait("sync", "sTdn", 32 * G + c - 1)
                P.op("sync", lambda E, c=c: E.dma_start(
                    out=WD[:, 1024 * (c % 2):1024 * (c % 2 + 1)],
                    in_=wdq_d[:, 1024 * c:1024 * (c + 1)]),
                    inc="sWDl%d" % (c % 2), n=16)
        if t >= 2:
            to = t - 2
            P.wait("gpsimd", "sVout", 4 * (to // 2) + 3 + (to % 2))
            P.op("gpsimd", lambda E, to=to: E.dma_start(
                out=out_d[128 * to:128 * (to + 1), :],
                in_=YO[:, 1024 * (to % 4):1024 * (to % 4 + 1)]),
                inc="sOut%d" % (to % 2), n=16)
    P.wait("sync", "sOut0", 16 * (NT // 2))
    P.wait("sync", "sOut1", 16 * (NT // 2))
    if dbg:
        dbg_specs = [
            ("NEGB", [128, 1], F32), ("RDEN", [128, NT], F32),
            ("MEAN", [128, NT], F32), ("RS", [128, NT], F32),
            ("SWG", [128, 1], F32), ("SWU", [128, 1], F32), ("SWD", [128, 1], F32),
            ("MWG127n", [128, 1], F32), ("MWD127n", [128, 1], F32),
            ("CT", [128, 8], F32), ("DEN", [1, 256], F32),
            ("ATT", [128, S], BF16), ("KF", [128, S], BF16),
            ("QF", [128, S], BF16), ("VT", [128, S], BF16),
            ("WG", [128, 8 * INNER], BF16),
            ("HQT", [128, 8192], BF16), ("ATX", [128, 1024], BF16),
            ("H", [128, 4096], F32), ("XN2", [128, 1024], F32),
            ("YO", [128, 4096], F32), ("SC2", [1, 8], F32),
        ]
        for nm, shp, dt in dbg_specs:
            od = nc.dram_tensor("dbg_" + nm, shp, dt, kind="ExternalOutput").ap()
            P.op("sync", lambda E, od=od, nm=nm: E.dma_start(out=od, in_=A[nm][:, :]),
                 inc="sOut0", n=16)
        P.wait("sync", "sOut0", 16 * (NT // 2) + 16 * len(dbg_specs))

    def b2_front_tensor(t):
        if t % 2 == 1 and t >= 3:
            P.wait("tensor", "sVout", 2 * t - 3)
        if t % 2 == 0 and t >= 2:
            P.wait("tensor", "sShq", 8 * (t - 1))
        for b in range(2):
            P.wait("tensor", "sVaq", 2 * t + b + 1)
            if b == 1:
                P.wait("tensor", "sSat2", 2 * t + 1)
            for k in range(4):
                d = 4 * b + k
                P.op("tensor", lambda E, d=d, k=k: E.transpose(
                    PS[:, 3072 + 128 * k:3072 + 128 * (k + 1)],
                    XN2[:, 128 * d:128 * (d + 1)], IDF[:, :]),
                    inc=("sTat2" if k == 3 else None))
        P.wait("tensor", "sSat2", 2 * (t + 1))
        gp = (t - 1) // 2 - 1
        stt_done3 = {1: 1, 2: 2, 0: 4}
        gu_prev = {0: 6, 1: 7, 2: 8}
        for e in range(8):
            se = (e + 1) % 3
            if t >= 1:
                P.wait("tensor", "sVq", 8 * (t - 1) + e + 1)
                P.wait("tensor", "sShq", 8 * (t - 1) + e)
                for k in range(4):
                    c = 4 * e + k
                    P.op("tensor", lambda E, c=c, k=k: E.transpose(
                        PS[:, 3072 + 128 * k:3072 + 128 * (k + 1)],
                        H[:, 128 * c:128 * (c + 1)], IDF[:, :]),
                        inc=("sThq" if k == 3 else None))
            if e >= 3:
                P.wait("tensor", "sVh", 8 * t + e - 2)
                P.wait("tensor", "sSgl", 8 * t + e - 2)
            elif t % 2 == 1 and gp >= 0:
                P.wait("tensor", "sVout", 4 * gp + stt_done3[se])
            elif t >= 1:
                P.wait("tensor", "sVh", 8 * (t - 1) + gu_prev[se])
                P.wait("tensor", "sSgl", 8 * (t - 1) + gu_prev[se])
            gcol, ucol = 1024 * se, 1024 * se + 512
            wcol = 2048 * (e // 4) + 512 * (e % 4)
            for d in range(8):
                st, sp = (d == 0), (d == 7)
                P.op("tensor", lambda E, d=d, st=st, sp=sp, gcol=gcol, wcol=wcol:
                     E.matmul(PS[:, gcol:gcol + 512], ATX[:, 128 * d:128 * (d + 1)],
                              WG[:, 4096 * d + wcol:4096 * d + wcol + 512],
                              start=st, stop=sp))
                P.op("tensor", lambda E, d=d, st=st, sp=sp, ucol=ucol, wcol=wcol:
                     E.matmul(PS[:, ucol:ucol + 512], ATX[:, 128 * d:128 * (d + 1)],
                              WU[:, 4096 * d + wcol:4096 * d + wcol + 512],
                              start=st, stop=sp), inc=("sTgu" if sp else None))

    def b2_last_hqt():
        t = NT
        P.wait("tensor", "sShq", 8 * (NT - 1))
        for b in range(8):
            P.wait("tensor", "sVq", 8 * (t - 1) + b + 1)
            if b >= 1:
                P.wait("tensor", "sShq", 8 * (t - 1) + b)
            for k in range(4):
                c = 4 * b + k
                P.op("tensor", lambda E, c=c, k=k: E.transpose(
                    PS[:, 3072 + 128 * k:3072 + 128 * (k + 1)],
                    H[:, 128 * c:128 * (c + 1)], IDF[:, :]),
                    inc=("sThq" if k == 3 else None))

    def b2_tail_tensor_group(G):
        ta, tb = 2 * G, 2 * G + 1
        lim = 8 * NT
        P.wait("tensor", "sVh", min(8 * (2 * G + 2) + 7, lim))
        P.wait("tensor", "sSgl", min(8 * (2 * G + 2) + 7, lim))
        P.op("tensor", lambda E, ta=ta: E.matmul(
            PS[:, 1024:1536], ATT[:, 128 * ta:128 * (ta + 1)], WO[:, 0:512],
            start=True, stop=True))
        P.op("tensor", lambda E, ta=ta: E.matmul(
            PS[:, 1536:2048], ATT[:, 128 * ta:128 * (ta + 1)], WO[:, 512:1024],
            start=True, stop=True), inc="sTpj")
        P.wait("tensor", "sVh", min(8 * (2 * G + 2) + 8, lim))
        P.wait("tensor", "sSgl", min(8 * (2 * G + 2) + 8, lim))
        P.op("tensor", lambda E, tb=tb: E.matmul(
            PS[:, 2048:2560], ATT[:, 128 * tb:128 * (tb + 1)], WO[:, 0:512],
            start=True, stop=True))
        P.op("tensor", lambda E, tb=tb: E.matmul(
            PS[:, 2560:3072], ATT[:, 128 * tb:128 * (tb + 1)], WO[:, 512:1024],
            start=True, stop=True), inc="sTpj")
        P.wait("tensor", "sShq", min(8 * (2 * G + 2), lim))
        ha, hb = 4096 * (ta % 2), 4096 * (tb % 2)
        for c in range(32):
            P.wait("tensor", "sWDl%d" % (c % 2), 16 * (16 * G + c // 2 + 1))
            st, sp = (c == 0), (c == 31)
            wlo, whi = 1024 * (c % 2), 1024 * (c % 2) + 512
            P.op("tensor", lambda E, c=c, st=st, sp=sp, ha=ha, wlo=wlo: E.matmul(
                PS[:, 3072:3584], HQT[:, ha + 128 * c:ha + 128 * (c + 1)],
                WD[:, wlo:wlo + 512], start=st, stop=sp))
            P.op("tensor", lambda E, c=c, st=st, sp=sp, ha=ha, whi=whi: E.matmul(
                PS[:, 3584:4096], HQT[:, ha + 128 * c:ha + 128 * (c + 1)],
                WD[:, whi:whi + 512], start=st, stop=sp))
            P.op("tensor", lambda E, c=c, st=st, sp=sp, hb=hb, wlo=wlo: E.matmul(
                PS[:, 0:512], HQT[:, hb + 128 * c:hb + 128 * (c + 1)],
                WD[:, wlo:wlo + 512], start=st, stop=sp))
            P.op("tensor", lambda E, c=c, st=st, sp=sp, hb=hb, whi=whi: E.matmul(
                PS[:, 512:1024], HQT[:, hb + 128 * c:hb + 128 * (c + 1)],
                WD[:, whi:whi + 512], start=st, stop=sp), inc="sTdn")

    qs_cnt = {"s": 0, "v": 0}

    def quant_scalar_sub(z_slice, scale_ap, wait_sem, wait_val):
        g = qs_cnt["s"]; qs_cnt["s"] += 1
        sl = 512 * (g % 2)
        P.wait("scalar", wait_sem, wait_val)
        P.wait("scalar", "sVqd", g - 1)
        P.op("scalar", lambda E, z_slice=z_slice, scale_ap=scale_ap, sl=sl:
             E.activation(QS1[:, sl:sl + 512], z_slice, AF.Identity, scale=scale_ap))
        P.op("scalar", lambda E, z_slice=z_slice, scale_ap=scale_ap, sl=sl:
             E.activation(QS2[:, sl:sl + 512], z_slice, AF.Identity, scale=scale_ap),
             inc="sSqp")

    def quant_vector_sub(z_slice, inc_last=None):
        g = qs_cnt["v"]; qs_cnt["v"] += 1
        sl = 512 * (g % 2)
        P.wait("vector", "sSqp", g + 1)
        P.op("vector", lambda E, sl=sl: E.scalar_tensor_tensor(
            QS1[:, sl:sl + 512], QS2[:, sl:sl + 512], -1.0, QS1[:, sl:sl + 512],
            ALU.mult, ALU.add))
        P.op("vector", lambda E, z_slice=z_slice, sl=sl: E.scalar_tensor_tensor(
            z_slice, QS1[:, sl:sl + 512], 0.5, QS2[:, sl:sl + 512],
            ALU.is_ge, ALU.add))
        P.op("vector", lambda E, z_slice=z_slice, sl=sl: E.scalar_tensor_tensor(
            z_slice, QS1[:, sl:sl + 512], -0.5, z_slice,
            ALU.is_lt, ALU.subtract), inc=inc_last)
        P.op("vector", lambda E: E.tensor_copy(SC1[:, 5:6], SC1[:, 5:6]), inc="sVqd")

    def b2_prep_vector(t):
        s3 = t % 4
        P.wait("vector", "sX2ld", 16 * (t + 1))
        P.wait("vector", "sTat2", 2 * t)
        P.op("vector", lambda E, t=t, s3=s3: E.tensor_scalar(
            XN2[:, :], YO[:, 1024 * s3:1024 * (s3 + 1)],
            MEAN[:, t:t + 1], RS[:, t:t + 1], ALU.subtract, ALU.mult))
        P.opf("vector", lambda E: E.reduce_max(SC1[:, 0:1], XN2[:, :], axis=AX.X,
                                               apply_absolute_value=True))
        P.opf("vector", lambda E: E.tensor_scalar(SC1[:, 0:1], SC1[:, 0:1],
                                                  1e-5, None, ALU.max))
        P.opf("vector", lambda E: E.reciprocal(SC1[:, 1:2], SC1[:, 0:1]))
        P.opf("vector", lambda E, s3=s3: E.tensor_scalar(
            CT[:, 4 * s3:4 * s3 + 1], SC1[:, 1:2], 127.0, None, ALU.mult))
        P.opf("vector", lambda E, s3=s3: E.tensor_tensor(
            CT[:, 4 * s3 + 1:4 * s3 + 2], SC1[:, 0:1], A["MWG127n"][:, :], ALU.mult))
        P.opf("vector", lambda E, s3=s3: E.tensor_tensor(
            CT[:, 4 * s3 + 2:4 * s3 + 3], SC1[:, 0:1], A["MWU127n"][:, :], ALU.mult))
        P.op("vector", lambda E: E.tensor_copy(SC1[:, 6:7], SC1[:, 6:7]), inc="sVln")
        for k in range(2):
            quant_vector_sub(XN2[:, 512 * k:512 * (k + 1)], inc_last="sVaq")

    def b2_eighth_one(t, e):
        s3 = t % 4
        se = (e + 1) % 3
        P.wait("vector", "sSgl", 8 * t + e + 1)
        ucol = 1024 * se + 512
        P.op("vector", lambda E, e=e, ucol=ucol, s3=s3: E.scalar_tensor_tensor(
            H[:, 512 * e:512 * (e + 1)], PS[:, ucol:ucol + 512],
            CT[:, 4 * s3 + 2:4 * s3 + 3], H[:, 512 * e:512 * (e + 1)],
            ALU.mult, ALU.mult), inc="sVh")
        P.opf("vector", lambda E, e=e: E.reduce_max(
            SC1[:, 4:5], H[:, 512 * e:512 * (e + 1)], axis=AX.X,
            apply_absolute_value=True))
        if e == 0:
            P.opf("vector", lambda E: E.tensor_copy(SC1[:, 2:3], SC1[:, 4:5]))
        else:
            P.opf("vector", lambda E: E.tensor_tensor(
                SC1[:, 2:3], SC1[:, 2:3], SC1[:, 4:5], ALU.max))

    def b2_quant_vector_scales(t):
        s3 = t % 4
        P.opf("vector", lambda E: E.tensor_scalar(SC1[:, 2:3], SC1[:, 2:3],
                                                  1e-5, None, ALU.max))
        P.opf("vector", lambda E: E.reciprocal(SC1[:, 3:4], SC1[:, 2:3]))
        P.opf("vector", lambda E: E.tensor_scalar(SC1[:, 3:4], SC1[:, 3:4],
                                                  127.0, None, ALU.mult))
        P.opf("vector", lambda E, s3=s3: E.tensor_tensor(
            CT[:, 4 * s3 + 3:4 * s3 + 4], SC1[:, 2:3], A["MWD127n"][:, :], ALU.mult))
        P.op("vector", lambda E: E.tensor_copy(SC1[:, 7:8], SC1[:, 7:8]), inc="sVsh")

    def b2_fix_one(t, k):
        if t >= 1:
            P.wait("vector", "sThq", 8 * (t - 1) + k + 1)
        quant_vector_sub(H[:, 512 * k:512 * (k + 1)], inc_last="sVq")

    def b2_quant_vector_fixups(t):
        for k in range(8):
            b2_fix_one(t, k)

    def b2_tail_vector_stt1(G):
        for i, tt in enumerate((2 * G, 2 * G + 1)):
            s3 = tt % 4
            col = 1024 if i == 0 else 2048
            P.wait("vector", "sTpj", 2 * G + i + 1)
            P.op("vector", lambda E, tt=tt, s3=s3, col=col: E.scalar_tensor_tensor(
                YO[:, 1024 * s3:1024 * (s3 + 1)], PS[:, col:col + 1024],
                RDEN[:, tt:tt + 1], YO[:, 1024 * s3:1024 * (s3 + 1)],
                ALU.mult, ALU.add), inc="sVout")

    def b2_tail_vector_stt2(G):
        P.wait("vector", "sTdn", 32 * (G + 1))
        for i, tt in enumerate((2 * G, 2 * G + 1)):
            s3 = tt % 4
            col = 3072 if i == 0 else 0
            P.op("vector", lambda E, s3=s3, col=col: E.scalar_tensor_tensor(
                YO[:, 1024 * s3:1024 * (s3 + 1)], PS[:, col:col + 1024],
                CT[:, 4 * s3 + 3:4 * s3 + 4], YO[:, 1024 * s3:1024 * (s3 + 1)],
                ALU.mult, ALU.add), inc="sVout")

    def b2_scalar(t):
        s = t % 4
        if t % 2 == 1:
            P.wait("scalar", "sTgu", 8 * t)
            for b in range(2):
                P.wait("scalar", "sTat2", 2 * t + b + 1)
                P.op("scalar", lambda E, b=b: E.copy(
                    ATX[:, 512 * b:512 * (b + 1)], PS[:, 3072:3584]), inc="sSat2")
        if t + 1 < NT:
            s3n = (t + 1) % 4
            for k in range(2):
                quant_scalar_sub(XN2[:, 512 * k:512 * (k + 1)],
                                 CT[:, 4 * s3n:4 * s3n + 1], "sVln", t + 2)
        if t >= 3:
            P.wait("scalar", "sTdn", 32 * ((t - 3) // 2 + 1))
        hof = 4096 * ((t - 1) % 2)

        def p12h(k):
            quant_scalar_sub(H[:, 512 * k:512 * (k + 1)], SC1[:, 3:4], "sVsh", t)

        def evac(k):
            P.wait("scalar", "sThq", 8 * (t - 1) + k + 1)
            P.op("scalar", lambda E, k=k, hof=hof: E.copy(
                HQT[:, hof + 512 * k:hof + 512 * (k + 1)], PS[:, 3072:3584]),
                inc="sShq")

        def silu(j):
            se = (j + 1) % 3
            P.wait("scalar", "sTgu", 8 * t + j + 1)
            gcol = 1024 * se
            P.op("scalar", lambda E, j=j, gcol=gcol, s=s: E.activation(
                H[:, 512 * j:512 * (j + 1)], PS[:, gcol:gcol + 512], AF.Silu,
                scale=CT[:, 4 * s + 1:4 * s + 2]), inc="sSgl")

        if t >= 1:
            p12h(0)
            p12h(1)
            evac(0)
            silu(0)
            for k in range(2, 8):
                p12h(k)
                evac(k - 1)
                silu(k - 1)
            evac(7)
            silu(7)
        else:
            for j in range(8):
                silu(j)
        if t % 2 == 1 and t + 1 < NT:
            P.wait("scalar", "sTgu", 8 * (t + 1))
            for b in range(2):
                P.wait("scalar", "sTat2", 2 * (t + 1) + b + 1)
                P.op("scalar", lambda E, b=b: E.copy(
                    ATX[:, 512 * b:512 * (b + 1)], PS[:, 3072:3584]), inc="sSat2")

    def b2_last_scalar_hq():
        t = NT
        P.wait("scalar", "sTdn", 32 * ((t - 3) // 2 + 1))
        hof = 4096 * ((t - 1) % 2)
        for k in range(8):
            quant_scalar_sub(H[:, 512 * k:512 * (k + 1)], SC1[:, 3:4], "sVsh", t)
            P.wait("scalar", "sThq", 8 * (t - 1) + k + 1)
            P.op("scalar", lambda E, k=k, hof=hof: E.copy(
                HQT[:, hof + 512 * k:hof + 512 * (k + 1)], PS[:, 3072:3584]),
                inc="sShq")

    P.op("vector", lambda E: E.memset(SC1[:, 4:8], 0.0))
    for k in range(2):
        quant_scalar_sub(XN2[:, 512 * k:512 * (k + 1)], CT[:, 0:1], "sVln", 1)
    for b in range(2):
        P.wait("scalar", "sTat2", b + 1)
        P.op("scalar", lambda E, b=b: E.copy(
            ATX[:, 512 * b:512 * (b + 1)], PS[:, 3072:3584]), inc="sSat2")
    b2_prep_vector(0)
    NG = NT // 2
    for t in range(NT + 1):
        if t < NT:
            b2_front_tensor(t)
            b2_scalar(t)
            if t + 1 < NT:
                b2_prep_vector(t + 1)
            if t >= 1:
                b2_fix_one(t - 1, 0)
                b2_fix_one(t - 1, 1)
                for k in range(2, 8):
                    b2_eighth_one(t, k - 2)
                    b2_fix_one(t - 1, k)
                b2_eighth_one(t, 6)
                b2_eighth_one(t, 7)
            else:
                for e in range(8):
                    b2_eighth_one(0, e)
            b2_quant_vector_scales(t)
            if t >= 2 and t % 2 == 0:
                b2_tail_vector_stt1(t // 2 - 1)
                b2_tail_vector_stt2(t // 2 - 1)
        else:
            b2_last_hqt()
            b2_last_scalar_hq()
            b2_quant_vector_fixups(NT - 1)
            b2_tail_vector_stt1(NG - 1)
            b2_tail_vector_stt2(NG - 1)
        if t >= 2 and t % 2 == 0 and t < NT:
            b2_tail_tensor_group(t // 2 - 1)
        if t == NT:
            b2_tail_tensor_group(NG - 1)

    with nc.Block() as block:
        @block.sync
        def _(E):
            for fn in P.streams["sync"]:
                fn(E)

        @block.tensor
        def _(E):
            for fn in P.streams["tensor"]:
                fn(E)

        @block.vector
        def _(E):
            for fn in P.streams["vector"]:
                fn(E)

        @block.scalar
        def _(E):
            for fn in P.streams["scalar"]:
                fn(E)

        @block.gpsimd
        def _(E):
            for fn in P.streams["gpsimd"]:
                fn(E)

    return nc


_NC_CACHE = None


def _get_nc():
    global _NC_CACHE
    if _NC_CACHE is None:
        _NC_CACHE = build_nc()
    return _NC_CACHE


def _host_constants():
    inv_freq = (1.0 / (np.float32(10000.0) **
                       (np.arange(0, HDIM, 2, dtype=np.float32) / np.float32(HDIM))))
    inv_freq = inv_freq.astype(np.float32)
    tpos = np.arange(S - 1, dtype=np.float32)
    ang = (tpos[:, None] * inv_freq[None, :]).astype(np.float32)
    cos = np.cos(ang).astype(np.float32)
    sin = np.sin(ang).astype(np.float32)
    trig = np.zeros((S, 128), np.float32)
    trig[0, 0:64] = 1.0
    trig[1:, 0:64] = cos
    trig[1:, 64:128] = sin
    idb = np.eye(128, dtype=BF16_NP)
    idf = np.eye(128, dtype=np.float32)
    onesb = np.ones((128, 1), BF16_NP)
    onesf = np.ones((128, 1), np.float32)
    onesrow = np.ones((1, 128), np.float32)
    return trig, idb, idf, onesb, onesf, onesrow


def kernel(L0_attn_x, L0_attn_wq, L0_attn_wk, L0_attn_wv, L0_attn_wo,
           L0_ffn_w_gate, L0_ffn_w_up, L0_ffn_w_down):
    x = np.ascontiguousarray(np.asarray(L0_attn_x, dtype=np.float32))
    wq = np.asarray(L0_attn_wq, np.float32)
    wk = np.asarray(L0_attn_wk, np.float32)
    wv = np.asarray(L0_attn_wv, np.float32)
    wo = np.asarray(L0_attn_wo, np.float32)
    qkvT = np.ascontiguousarray(
        np.concatenate([wq.T, wk.T, wv.T], axis=1)).astype(np.float32)
    woT = np.ascontiguousarray(wo.T)
    wgT = np.ascontiguousarray(np.asarray(L0_ffn_w_gate, np.float32).T)
    wuT = np.ascontiguousarray(np.asarray(L0_ffn_w_up, np.float32).T)
    wdT = np.ascontiguousarray(np.asarray(L0_ffn_w_down, np.float32).T)
    trig, idb, idf, onesb, onesf, onesrow = _host_constants()

    nc = _get_nc()
    in_maps = []
    for b in range(B):
        in_maps.append({
            "x": np.ascontiguousarray(x[b]),
            "qkvT": qkvT, "woT": woT,
            "wgT": wgT, "wuT": wuT, "wdT": wdT,
            "trig": trig, "idb": idb, "idf": idf,
            "onesb": onesb, "onesf": onesf, "onesrow": onesrow,
        })
    res = run_bass_kernel_spmd(nc, in_maps, core_ids=list(range(B)))
    out = np.stack([np.asarray(res.results[b]["out"], dtype=np.float32)
                    for b in range(B)], axis=0)
    return out


# version 19
# speedup vs baseline: 1.1548x; 1.1548x over previous
import numpy as np
import ml_dtypes

import concourse.bass as bass
import concourse.mybir as mybir
from concourse.bass_utils import run_bass_kernel_spmd

F32 = mybir.dt.float32
BF16 = mybir.dt.bfloat16
I32 = mybir.dt.int32
I8 = mybir.dt.int8
AF = mybir.ActivationFunctionType
ALU = mybir.AluOpType
AX = mybir.AxisListType
BF16_NP = ml_dtypes.bfloat16

B, S, D, HDIM, INNER = 8, 4096, 1024, 128, 4096
NT = S // 128
QI = S // 256
NWA = 48
NWB = 64
INV_SQRT_HD = 0.08838834764831845
N_ELEM_W = float(INNER * D)

ENGINES = ("sync", "tensor", "vector", "scalar", "gpsimd")


class Prog:
    def __init__(self):
        self.streams = {e: [] for e in ENGINES}
        self.counts = {}
        self._h = None

    def sem(self, name):
        self.counts[name] = 0

    def emit(self, eng, fn):
        self.streams[eng].append(fn)

    def wait(self, eng, sem_name, val):
        if val <= 0:
            return
        self.emit(eng, lambda E, s=sem_name, v=val: E.wait_ge(self._h[s], v))

    def op(self, eng, builder, inc=None, n=1):
        if inc is None:
            self.emit(eng, builder)
        else:
            self.counts[inc] += n

            def fn(E, b=builder, s=inc, k=n):
                b(E).then_inc(self._h[s], k)
            self.emit(eng, fn)

    def c(self, name):
        return self.counts[name]

    def opf(self, eng, builder):
        s = "chain_" + eng
        self.op(eng, builder, inc=s)
        self.wait(eng, s, self.counts[s])


def build_nc(dbg=False):
    nc = bass.Bass("TRN2", target_bir_lowering=False,
                   detect_race_conditions=False)

    x_d = nc.dram_tensor("x", [S, D], F32, kind="ExternalInput").ap()
    qkvT_d = nc.dram_tensor("qkvT", [D, 384], F32, kind="ExternalInput").ap()
    woT_d = nc.dram_tensor("woT", [HDIM, D], F32, kind="ExternalInput").ap()
    wgT_d = nc.dram_tensor("wgT", [D, INNER], F32, kind="ExternalInput").ap()
    wuT_d = nc.dram_tensor("wuT", [D, INNER], F32, kind="ExternalInput").ap()
    wdT_d = nc.dram_tensor("wdT", [INNER, D], F32, kind="ExternalInput").ap()
    trig_d = nc.dram_tensor("trig", [S, 128], F32, kind="ExternalInput").ap()
    idb_d = nc.dram_tensor("idb", [128, 128], BF16, kind="ExternalInput").ap()
    idf_d = nc.dram_tensor("idf", [128, 128], F32, kind="ExternalInput").ap()
    onesb_d = nc.dram_tensor("onesb", [128, 1], BF16, kind="ExternalInput").ap()
    onesf_d = nc.dram_tensor("onesf", [128, 1], F32, kind="ExternalInput").ap()
    onesrow_d = nc.dram_tensor("onesrow", [1, 128], F32, kind="ExternalInput").ap()
    out_d = nc.dram_tensor("out", [S, D], F32, kind="ExternalOutput").ap()
    wdq_d = nc.dram_tensor("wdq_scratch", [INNER, D], I8, kind="Internal").ap()

    BASE = 16512
    TOP = 229376
    A = {}
    def sb(name, shape, dtype, off):
        assert off % 32 == 0, (name, off)
        A[name] = nc.alloc_sbuf_tensor_at(name, shape, dtype, offset=off)
        return A[name]

    o = BASE
    sb("WG", [128, 8 * INNER], BF16, o); o += 65536
    sb("WU", [128, 8 * INNER], BF16, o); o += 65536
    sb("ATT", [128, S], BF16, o); o += 8192
    sb("WO", [128, D], BF16, o); o += 2048
    for nm, shp, dt, szb in [
        ("MEAN", [128, NT], F32, 128), ("RS", [128, NT], F32, 128),
        ("RDEN", [128, NT], F32, 128),
        ("IDB", [128, 128], BF16, 256), ("IDF", [128, 128], F32, 512),
        ("ONESB", [128, 1], BF16, 32), ("ONESF", [128, 1], F32, 32),
        ("ONESROW", [1, 128], F32, 512),
        ("NEGB", [128, 1], F32, 32),
        ("SWG", [128, 1], F32, 32), ("SWU", [128, 1], F32, 32), ("SWD", [128, 1], F32, 32),
        ("MWG127n", [128, 1], F32, 32), ("MWU127n", [128, 1], F32, 32),
        ("MWD127n", [128, 1], F32, 32),
        ("ACCG", [128, 1], F32, 32), ("ACCU", [128, 1], F32, 32), ("ACCD", [128, 1], F32, 32),
        ("PAP", [128, 2], F32, 32),
        ("SC1", [128, 8], F32, 32),
        ("SC2", [1, 8], F32, 32),
        ("DEN", [1, 256], F32, 1024),
    ]:
        sb(nm, shp, dt, o); o += szb
    PERS_END = o
    OVA = (o + 31) // 32 * 32
    sb("KF", [128, S], BF16, OVA)
    sb("QF", [128, S], BF16, OVA + 8192)
    sb("VT", [128, S], BF16, OVA + 16384)
    OVB = OVA + 24576
    sb("QKVW", [128, 3072], BF16, OVB)
    sb("XB", [128, 2048], F32, OVB + 6144)
    sb("XN", [128, 1024], F32, OVB + 14336)
    sb("XNT", [128, 1024], BF16, OVB + 18432)
    sb("RT", [128, 384], F32, OVB + 20480)
    sb("TRG", [128, 256], F32, OVB + 22016)
    sb("WLD", [128, 4096], F32, OVB + 23040)
    sb("SQS", [128, 256], BF16, OVB + 39424)
    assert OVB + 39936 <= TOP
    sb("PT", [128, 8192], BF16, OVB)
    sb("WLB", [128, 4096], F32, OVB + 16384)
    sb("WDQA0", [128, 1024], I8, OVB + 16384)
    sb("WDQA1", [128, 1024], I8, OVB + 24576)
    sb("TL", [128, 1024], F32, OVB + 32768)
    sb("TG", [128, 1024], F32, OVB + 36864)
    assert OVB + 40960 <= TOP
    o = OVA
    sb("H", [128, 4096], F32, o); o += 16384
    sb("HQT", [128, 4096], BF16, o); o += 8192
    sb("WD", [128, 8192], BF16, o); o += 16384
    sb("XN2", [128, 1024], F32, o); o += 4096
    sb("ATX", [128, 1024], BF16, o); o += 2048
    sb("QS1", [128, 1024], F32, o); o += 4096
    sb("QS2", [128, 1024], I32, o); o += 4096
    sb("YO", [128, 3072], F32, o); o += 12288
    sb("CT", [128, 12], F32, o); o += 64
    assert o <= TOP, o

    _psg = nc.psum_tensor("PS", [128, 4096], F32)
    PS = _psg.__enter__()

    P = Prog()
    sem_names = [
        "sCstG", "sCstS", "sXld0", "sXld1", "sTRG0", "sTRG1", "sWA0", "sWA1",
        "sSabs", "sVacc",
        "sSsq", "sVrec", "sSrs0", "sVxn",
        "sTtr", "sSxnt", "sTqkv", "sVrope", "sTqk2", "sSqk", "sSv",
        "sVnorm", "sTnorm", "sVbp", "sSbp", "sTbp",
        "sTsc", "sSexp", "sTpv", "sTden", "sSatt", "sSden", "sTrd", "sVrd",
        "sWB0", "sWB1", "sWBdone", "sDQ0", "sDQ1",
        "sX2ld", "sWDl0", "sWDl1", "sWDl2", "sWDl3", "sWDl4", "sWDl5", "sWDl6", "sWDl7",
        "sVaq", "sTat2", "sSat2", "sTgu", "sSgl", "sVh", "sVq",
        "sThq", "sShq", "sTpj", "sTdn", "sVout", "sOut0", "sOut1",
        "chain_vector", "chain_scalar", "chain_tensor", "chain_sync", "chain_gpsimd",
        "sVln", "sVsh", "sSqp", "sVqd",
    ]
    sems = {}
    for nm in sem_names:
        sems[nm] = nc.semaphore(nm).__enter__()
        P.sem(nm)
    P._h = sems

    KF, QF, VT = A["KF"], A["QF"], A["VT"]
    ATT, WO, QKVW = A["ATT"], A["WO"], A["QKVW"]
    XB, XN, XNT, RT, TRG, WLD, SQS = (A["XB"], A["XN"], A["XNT"], A["RT"],
                                      A["TRG"], A["WLD"], A["SQS"])
    MEAN, RS, RDEN = A["MEAN"], A["RS"], A["RDEN"]
    IDB, IDF = A["IDB"], A["IDF"]
    ONESB, ONESF, ONESROW = A["ONESB"], A["ONESF"], A["ONESROW"]
    NEGB, SC1, SC2, DEN, PAP = A["NEGB"], A["SC1"], A["SC2"], A["DEN"], A["PAP"]
    PT, WLB, TL, TG = A["PT"], A["WLB"], A["TL"], A["TG"]
    WDQA = (A["WDQA0"], A["WDQA1"])
    WG, WU = A["WG"], A["WU"]
    H, HQT, WD = A["H"], A["HQT"], A["WD"]
    XN2, ATX = A["XN2"], A["ATX"]
    QS1, QS2, YO, CT = A["QS1"], A["QS2"], A["YO"], A["CT"]

    for d in range(8):
        P.op("gpsimd", lambda E, d=d: E.dma_start(
            out=QKVW[:, 384 * d:384 * (d + 1)],
            in_=qkvT_d[128 * d:128 * (d + 1), :]), inc="sCstG", n=16)
    P.op("gpsimd", lambda E: E.dma_start(out=WO[:, :], in_=woT_d), inc="sCstG", n=16)
    for dst, src in [(IDB, idb_d), (IDF, idf_d), (ONESB, onesb_d),
                     (ONESF, onesf_d), (ONESROW, onesrow_d)]:
        P.op("sync", lambda E, d=dst, s=src: E.dma_start(out=d[:, :], in_=s),
             inc="sCstS", n=16)

    def p1_sync_tile(t):
        P.wait("sync", "sTtr", t - 1)
        P.op("sync", lambda E, t=t: E.dma_start(
            out=XB[:, 1024 * (t % 2):1024 * (t % 2 + 1)],
            in_=x_d[128 * t:128 * (t + 1), :]), inc="sXld%d" % (t % 2), n=16)
        P.wait("sync", "sVrope", t - 1)
        P.op("sync", lambda E, t=t: E.dma_start(
            out=TRG[:, 128 * (t % 2):128 * (t % 2 + 1)],
            in_=trig_d[128 * t:128 * (t + 1), :]), inc="sTRG%d" % (t % 2), n=16)

    def wa_src(j):
        m, jj = j // 16, j % 16
        rb, h = jj // 2, jj % 2
        if m == 0:
            return wgT_d[128 * rb:128 * (rb + 1), 2048 * h:2048 * (h + 1)]
        if m == 1:
            return wuT_d[128 * rb:128 * (rb + 1), 2048 * h:2048 * (h + 1)]
        return wdT_d.rearrange("(c p two) n -> c p (two n)", two=2, p=128)[jj]

    def p1_sync_passA(j):
        P.wait("sync", "sSabs", j - 1)
        P.op("sync", lambda E, j=j: E.dma_start(
            out=WLD[:, 2048 * (j % 2):2048 * (j % 2 + 1)], in_=wa_src(j)),
            inc="sWA%d" % (j % 2), n=16)

    def p1_vector(t):
        sl = XB[:, 1024 * (t % 2):1024 * (t % 2 + 1)]
        P.wait("vector", "sXld%d" % (t % 2), 16 * (t // 2 + 1))
        P.opf("vector", lambda E, sl=sl: E.reduce_sum(SC1[:, 0:1], sl, axis=AX.X))
        P.opf("vector", lambda E: E.tensor_scalar(SC1[:, 0:1], SC1[:, 0:1],
                                                  1.0 / 1024.0, None, ALU.mult))
        P.wait("vector", "sSsq", t + 1)
        P.opf("vector", lambda E: E.tensor_scalar(SC1[:, 1:2], SC1[:, 2:3],
                                                  1.0 / 1024.0, None, ALU.mult))
        P.opf("vector", lambda E: E.tensor_tensor(SC1[:, 4:5], SC1[:, 0:1],
                                                  SC1[:, 0:1], ALU.mult))
        P.opf("vector", lambda E: E.tensor_sub(SC1[:, 1:2], SC1[:, 1:2], SC1[:, 4:5]))
        P.opf("vector", lambda E: E.tensor_scalar(SC1[:, 1:2], SC1[:, 1:2],
                                                  1e-5, None, ALU.add))
        P.op("vector", lambda E: E.reciprocal(SC1[:, 2:3], SC1[:, 1:2]), inc="sVrec")
        P.opf("vector", lambda E, t=t: E.tensor_copy(MEAN[:, t:t + 1], SC1[:, 0:1]))
        P.wait("vector", "sSrs0", t + 1)
        P.opf("vector", lambda E: E.tensor_tensor(SC1[:, 4:5], SC1[:, 3:4],
                                                  SC1[:, 3:4], ALU.mult))
        P.opf("vector", lambda E: E.tensor_tensor(SC1[:, 4:5], SC1[:, 4:5],
                                                  SC1[:, 1:2], ALU.mult))
        P.opf("vector", lambda E: E.tensor_scalar(SC1[:, 4:5], SC1[:, 4:5],
                                                  -0.5, 1.5, ALU.mult, ALU.add))
        P.opf("vector", lambda E, t=t: E.tensor_tensor(RS[:, t:t + 1], SC1[:, 3:4],
                                                       SC1[:, 4:5], ALU.mult))
        P.op("vector", lambda E, sl=sl, t=t: E.tensor_scalar(
            sl, sl, MEAN[:, t:t + 1], RS[:, t:t + 1],
            ALU.subtract, ALU.mult), inc="sVxn")

    def p1_rope(t):
        P.wait("vector", "sTqkv", t + 1)
        P.wait("vector", "sTRG%d" % (t % 2), 16 * (t // 2 + 1))
        cosv = TRG[:, 128 * (t % 2):128 * (t % 2) + 64]
        sinv = TRG[:, 128 * (t % 2) + 64:128 * (t % 2) + 128]
        for base, off in ((0, 0), (512, 128)):
            ev = PS[:, base:base + 127:2]
            od = PS[:, base + 1:base + 128:2]
            dst = RT[:, off:off + 128]
            scr = RT[:, 256:320]
            last = (base == 512)
            P.opf("vector", lambda E, ev=ev, cosv=cosv, dst=dst: E.tensor_tensor(
                dst[:, 0:127:2], ev, cosv, ALU.mult))
            P.opf("vector", lambda E, od=od, sinv=sinv, scr=scr: E.tensor_tensor(
                scr, od, sinv, ALU.mult))
            P.op("vector", lambda E, dst=dst, scr=scr: E.tensor_sub(
                dst[:, 0:127:2], dst[:, 0:127:2], scr))
            P.opf("vector", lambda E, ev=ev, sinv=sinv, dst=dst: E.tensor_tensor(
                dst[:, 1:128:2], ev, sinv, ALU.mult))
            P.opf("vector", lambda E, od=od, cosv=cosv, scr=scr: E.tensor_tensor(
                scr, od, cosv, ALU.mult))
            P.op("vector", lambda E, dst=dst, scr=scr: E.tensor_add(
                dst[:, 1:128:2], dst[:, 1:128:2], scr),
                inc=("sVrope" if last else None))

    def p1_scalar(t):
        sl = XB[:, 1024 * (t % 2):1024 * (t % 2 + 1)]
        P.wait("scalar", "sXld%d" % (t % 2), 16 * (t // 2 + 1))
        P.wait("scalar", "sVrec", t)
        P.op("scalar", lambda E, sl=sl: E.activation(
            XN[:, :], sl, AF.Square, accum_out=SC1[:, 2:3]), inc="sSsq")
        P.wait("scalar", "sVrec", t + 1)
        P.wait("scalar", "sVxn", t)
        P.op("scalar", lambda E: E.activation(SC1[:, 3:4], SC1[:, 2:3], AF.Sqrt),
             inc="sSrs0")
        P.wait("scalar", "sTtr", t + 1)
        P.op("scalar", lambda E: E.copy(XNT[:, 0:512], PS[:, 3072:3584]))
        P.op("scalar", lambda E: E.copy(XNT[:, 512:1024], PS[:, 3584:4096]),
             inc="sSxnt")
        P.wait("scalar", "sTqkv", t + 1)
        P.op("scalar", lambda E, t=t: E.copy(VT[:, 128 * t:128 * (t + 1)],
                                             PS[:, 1024:1152]), inc="sSv")
        P.wait("scalar", "sTqk2", t + 1)
        P.op("scalar", lambda E, t=t: E.copy(QF[:, 128 * t:128 * (t + 1)],
                                             PS[:, 1536:1664]))
        P.op("scalar", lambda E, t=t: E.copy(KF[:, 128 * t:128 * (t + 1)],
                                             PS[:, 1664:1792]), inc="sSqk")

    def passA_chunk(j):
        P.wait("scalar", "sWA%d" % (j % 2), 16 * (j // 2 + 1))
        P.wait("scalar", "sVacc", j - 1)
        P.op("scalar", lambda E, j=j: E.activation(
            WLD[:, 2048 * (j % 2):2048 * (j % 2 + 1)],
            WLD[:, 2048 * (j % 2):2048 * (j % 2 + 1)], AF.Abs,
            accum_out=PAP[:, j % 2:j % 2 + 1]), inc="sSabs")
        acc = A["ACCG"] if j < 16 else (A["ACCU"] if j < 32 else A["ACCD"])
        P.wait("vector", "sSabs", j + 1)
        if j % 16 == 0:
            P.op("vector", lambda E, acc=acc, j=j: E.tensor_copy(
                acc[:, :], PAP[:, j % 2:j % 2 + 1]), inc="sVacc")
        else:
            P.op("vector", lambda E, acc=acc, j=j: E.tensor_add(
                acc[:, :], acc[:, :], PAP[:, j % 2:j % 2 + 1]), inc="sVacc")
        P.wait("vector", "sVacc", P.c("sVacc"))

    def p1_tensor(t):
        P.wait("tensor", "sVxn", t + 1)
        P.wait("tensor", "sSxnt", t)
        xo = 1024 * (t % 2)
        for d in range(8):
            base = 3072 + 128 * d
            P.op("tensor", lambda E, d=d, base=base, xo=xo: E.transpose(
                PS[:, base:base + 128], XB[:, xo + 128 * d:xo + 128 * (d + 1)],
                IDF[:, :]), inc=("sTtr" if d == 7 else None))
        P.wait("tensor", "sSxnt", t + 1)
        P.wait("tensor", "sVrope", t)
        P.wait("tensor", "sSv", t)
        for d in range(8):
            lhsT = XNT[:, 128 * d:128 * (d + 1)]
            st, sp = (d == 0), (d == 7)
            P.op("tensor", lambda E, lhsT=lhsT, d=d, st=st, sp=sp: E.matmul(
                PS[:, 0:128], lhsT, QKVW[:, 384 * d:384 * d + 128],
                start=st, stop=sp))
            P.op("tensor", lambda E, lhsT=lhsT, d=d, st=st, sp=sp: E.matmul(
                PS[:, 512:640], lhsT, QKVW[:, 384 * d + 128:384 * d + 256],
                start=st, stop=sp))
            P.op("tensor", lambda E, lhsT=lhsT, d=d, st=st, sp=sp: E.matmul(
                PS[:, 1024:1152], lhsT, QKVW[:, 384 * d + 256:384 * d + 384],
                start=st, stop=sp), inc=("sTqkv" if sp else None))
        P.wait("tensor", "sVrope", t + 1)
        P.wait("tensor", "sSqk", t)
        P.op("tensor", lambda E: E.transpose(PS[:, 1536:1664], RT[:, 0:128], IDF[:, :]))
        P.op("tensor", lambda E: E.transpose(PS[:, 1664:1792], RT[:, 128:256],
                                             IDF[:, :]), inc="sTqk2")

    P.wait("tensor", "sCstG", P.c("sCstG"))
    P.wait("tensor", "sCstS", P.c("sCstS"))
    for t in range(NT):
        p1_sync_tile(t)
        p1_vector(t)
        p1_rope(t)
        p1_scalar(t)
        p1_tensor(t)
        for j in range(min(NWA, (3 * t) // 2), min(NWA, (3 * (t + 1)) // 2)):
            p1_sync_passA(j)
            passA_chunk(j)
    for j in range((3 * NT) // 2, NWA):
        p1_sync_passA(j)
        passA_chunk(j)

    P.wait("vector", "sSqk", NT)
    P.wait("tensor", "sSqk", NT)
    P.wait("tensor", "sSv", NT)
    P.wait("tensor", "sSxnt", NT)
    for w, src in ((0, KF), (1, QF)):
        for c in range(NT):
            P.wait("vector", "sTnorm", 32 * w + c - 1)
            P.op("vector", lambda E, src=src, c=c: E.tensor_tensor(
                SQS[:, 128 * (c % 2):128 * (c % 2 + 1)],
                src[:, 128 * c:128 * (c + 1)], src[:, 128 * c:128 * (c + 1)],
                ALU.mult), inc="sVnorm")
            P.wait("tensor", "sVnorm", 32 * w + c + 1)
            if w == 1 and c == 0:
                P.wait("tensor", "sVbp", 1)
            P.op("tensor", lambda E, c=c: E.matmul(
                PS[0:1, 128 * c:128 * (c + 1)], ONESB[:, :],
                SQS[:, 128 * (c % 2):128 * (c % 2 + 1)], start=True, stop=True),
                inc="sTnorm")
        P.wait("vector", "sTnorm", 32 * (w + 1))
        P.op("vector", lambda E, w=w: E.reduce_max(
            SC2[0:1, w:w + 1], PS[0:1, 0:4096], axis=AX.X), inc="sVbp")
        P.wait("vector", "sVbp", P.c("sVbp"))
    P.op("vector", lambda E: E.tensor_tensor(SC2[0:1, 2:3], SC2[0:1, 0:1],
                                             SC2[0:1, 1:2], ALU.mult), inc="sVbp")
    P.wait("vector", "sVbp", P.c("sVbp"))
    P.wait("scalar", "sVbp", P.c("sVbp"))
    P.op("scalar", lambda E: E.activation(SC2[0:1, 3:4], SC2[0:1, 2:3], AF.Sqrt),
         inc="sSbp")
    P.wait("vector", "sSbp", P.c("sSbp"))
    P.opf("vector", lambda E: E.tensor_scalar(SC2[0:1, 4:5], SC2[0:1, 3:4],
                                              INV_SQRT_HD, 1.0, ALU.mult, ALU.add))
    P.op("vector", lambda E: E.tensor_scalar(SC2[0:1, 4:5], SC2[0:1, 4:5],
                                             -1.0, None, ALU.mult), inc="sVbp")
    P.wait("vector", "sVbp", P.c("sVbp"))
    VBP1 = P.c("sVbp")
    P.wait("tensor", "sVacc", NWA)
    P.wait("tensor", "sVbp", VBP1)
    for i, acc in enumerate(("ACCG", "ACCU", "ACCD")):
        P.op("tensor", lambda E, i=i, acc=acc: E.matmul(
            PS[0:1, 2048 + 16 * i:2049 + 16 * i], ONESF[:, :], A[acc][:, :],
            start=True, stop=True), inc="sTbp")
    P.wait("vector", "sTbp", P.c("sTbp"))
    for i in range(3):
        P.op("vector", lambda E, i=i: E.tensor_scalar(
            SC2[0:1, 5 + i:6 + i], PS[0:1, 2048 + 16 * i:2049 + 16 * i],
            1.0 / N_ELEM_W, 1e-5, ALU.mult, ALU.max), inc="sVbp")
    P.wait("tensor", "sVbp", P.c("sVbp"))
    for i, col in enumerate((4, 5, 6, 7)):
        P.op("tensor", lambda E, col=col, i=i: E.matmul(
            PS[:, 2560 + 16 * i:2561 + 16 * i], ONESROW[:, :],
            SC2[0:1, col:col + 1], start=True, stop=True), inc="sTbp")
    P.wait("vector", "sTbp", P.c("sTbp"))
    P.op("vector", lambda E: E.tensor_copy(NEGB[:, :], PS[:, 2560:2561]))
    for i, (sw, mw) in enumerate((("SWG", "MWG127n"), ("SWU", "MWU127n"),
                                  ("SWD", "MWD127n"))):
        src = PS[:, 2576 + 16 * i:2577 + 16 * i]
        P.opf("vector", lambda E, sw=sw, src=src: E.reciprocal(A[sw][:, :], src))
        P.opf("vector", lambda E, mw=mw, src=src: E.tensor_scalar(
            A[mw][:, :], src, -1.0 / 127.0, None, ALU.mult))
    P.op("vector", lambda E: E.tensor_copy(SC1[:, 0:1], SC1[:, 0:1]), inc="sVbp")
    BP_DONE = P.c("sVbp")

    P.wait("sync", "sVbp", BP_DONE)

    def wb_src(u):
        if u < 16:
            rb, h = (u % 16) // 2, u % 2
            return wgT_d[128 * rb:128 * (rb + 1), 2048 * h:2048 * (h + 1)]
        if u < 32:
            uu = u - 16
            rb, h = uu // 2, uu % 2
            return wuT_d[128 * rb:128 * (rb + 1), 2048 * h:2048 * (h + 1)]
        c = u - 32
        return wdT_d[128 * c:128 * (c + 1), :]

    for u in range(NWB):
        P.wait("sync", "sWBdone", u - 1)
        if u >= 34:
            c_prev = u - 34
            P.wait("sync", "sDQ%d" % (c_prev % 2), 16 * (c_prev // 2 + 1))
        ncols = 2048 if u < 32 else 1024
        P.op("sync", lambda E, u=u, ncols=ncols: E.dma_start(
            out=WLB[:, 2048 * (u % 2):2048 * (u % 2) + ncols], in_=wb_src(u)),
            inc="sWB%d" % (u % 2), n=16)
        if u >= 32:
            c = u - 32
            P.wait("sync", "sWBdone", u + 1)
            P.op("sync", lambda E, c=c: E.dma_start(
                out=wdq_d[128 * c:128 * (c + 1), :], in_=WDQA[c % 2][:, :]),
                inc="sDQ%d" % (c % 2), n=16)

    def passB_unit(u):
        ncols = 2048 if u < 32 else 1024
        sl = WLB[:, 2048 * (u % 2):2048 * (u % 2) + ncols]
        sw = A["SWG"] if u < 16 else (A["SWU"] if u < 32 else A["SWD"])
        P.wait("vector", "sWB%d" % (u % 2), 16 * (u // 2 + 1))
        for half in range(ncols // 1024):
            hsl = sl[:, 1024 * half:1024 * (half + 1)]
            P.op("vector", lambda E, hsl=hsl, sw=sw: E.tensor_scalar(
                TL[:, :], hsl, sw[:, :], -0.5, ALU.mult, ALU.is_lt))
            P.op("vector", lambda E, hsl=hsl, sw=sw: E.tensor_scalar(
                TG[:, :], hsl, sw[:, :], 0.5, ALU.mult, ALU.is_gt))
            if u < 32:
                dstW = WG if u < 16 else WU
                cw = u % 16
                rb, hh = cw // 2, cw % 2
                dst = dstW[:, 4096 * rb + 2048 * hh + 1024 * half:
                           4096 * rb + 2048 * hh + 1024 * (half + 1)]
                last = (half == ncols // 1024 - 1)
                P.op("vector", lambda E, dst=dst: E.tensor_sub(dst, TG[:, :], TL[:, :]),
                     inc=("sWBdone" if last else None))
            else:
                c = u - 32
                if c >= 2:
                    P.wait("vector", "sDQ%d" % (c % 2), 16 * (c // 2))
                P.op("vector", lambda E, c=c: E.tensor_sub(
                    WDQA[c % 2][:, :], TG[:, :], TL[:, :]), inc="sWBdone")

    P.wait("tensor", "sVbp", BP_DONE)
    P.wait("scalar", "sVbp", BP_DONE)

    def a2_tensor_scores(i, g):
        P.wait("tensor", "sSexp", 8 * i + g - 1)
        for k in range(4):
            c = 4 * g + k
            bank = 2 * (g % 2) + k // 2
            off = 512 * bank + 256 * (k % 2)
            P.op("tensor", lambda E, c=c, off=off, i=i: E.matmul(
                PS[:, off:off + 256], KF[:, 128 * c:128 * (c + 1)],
                QF[:, 256 * i:256 * (i + 1)], start=True, stop=True),
                inc=("sTsc" if k == 3 else None))

    def a2_tensor_pv(i, g):
        P.wait("tensor", "sSexp", 8 * i + g + 1)
        if g == 0:
            P.wait("tensor", "sSatt", i)
        for k in range(4):
            c = 4 * g + k
            P.op("tensor", lambda E, c=c: E.matmul(
                PS[:, 2048:2304], VT[:, 128 * c:128 * (c + 1)],
                PT[:, 256 * c:256 * (c + 1)], start=(c == 0), stop=(c == 31)),
                inc=("sTpv" if k == 3 else None))

    for i in range(QI):
        for g in range(8):
            a2_tensor_scores(i, g)
            if g >= 1:
                a2_tensor_pv(i, g - 1)
        a2_tensor_pv(i, 7)
        P.wait("tensor", "sSden", i)
        for c in range(32):
            P.op("tensor", lambda E, c=c: E.matmul(
                PS[0:1, 2560:2816], ONESB[:, :], PT[:, 256 * c:256 * (c + 1)],
                start=(c == 0), stop=(c == 31)), inc=("sTden" if c == 31 else None))
        P.wait("tensor", "sSden", i + 1)
        P.wait("tensor", "sVrd", i)
        P.op("tensor", lambda E: E.transpose(PS[:, 3072:3073], DEN[0:1, 0:128],
                                             IDF[0:1, 0:1]))
        P.op("tensor", lambda E: E.transpose(PS[:, 3073:3074], DEN[0:1, 128:256],
                                             IDF[0:1, 0:1]), inc="sTrd")
        for g in range(8):
            if g == 0:
                P.wait("scalar", "sTden", i)
            P.wait("scalar", "sTsc", 8 * i + g + 1)
            src = PS[:, 1024 * (g % 2):1024 * (g % 2) + 1024]
            P.op("scalar", lambda E, src=src, g=g: E.activation(
                PT[:, 1024 * g:1024 * (g + 1)], src, AF.Exp,
                bias=NEGB[:, :], scale=INV_SQRT_HD), inc="sSexp")
        P.wait("scalar", "sTpv", 8 * (i + 1))
        P.op("scalar", lambda E, i=i: E.copy(ATT[:, 256 * i:256 * (i + 1)],
                                             PS[:, 2048:2304]), inc="sSatt")
        P.wait("scalar", "sTden", i + 1)
        P.op("scalar", lambda E: E.copy(DEN[0:1, :], PS[0:1, 2560:2816]),
             inc="sSden")
        P.wait("vector", "sTrd", i + 1)
        P.op("vector", lambda E, i=i: E.reciprocal(RDEN[:, 2 * i:2 * i + 2],
                                                   PS[:, 3072:3074]), inc="sVrd")
        P.wait("vector", "sVrd", P.c("sVrd"))
        for u in range(4 * i, 4 * i + 4):
            passB_unit(u)

    DQ_ALL = 16 * 32
    P.wait("sync", "sTden", QI)
    P.wait("sync", "sDQ0", 16 * 16)
    P.wait("sync", "sDQ1", 16 * 16)
    P.wait("sync", "sWBdone", NWB)
    P.wait("tensor", "sVrd", QI)
    P.wait("tensor", "sSatt", QI)
    P.wait("tensor", "sWBdone", NWB)
    P.wait("vector", "sDQ0", 16 * 16)
    P.wait("vector", "sDQ1", 16 * 16)
    P.wait("scalar", "sVrd", QI)

    P.wait("gpsimd", "sDQ0", 16 * 16)
    P.wait("gpsimd", "sDQ1", 16 * 16)
    P.wait("gpsimd", "sTden", QI)
    for t in range(NT + 2):
        if t < NT:
            if t >= 3:
                P.wait("sync", "sOut%d" % ((t - 3) % 2), 16 * ((t - 3) // 2 + 1))
            P.op("sync", lambda E, t=t: E.dma_start(
                out=YO[:, 1024 * (t % 3):1024 * (t % 3 + 1)],
                in_=x_d[128 * t:128 * (t + 1), :]), inc="sX2ld", n=16)
        if t >= 2:
            to = t - 2
            P.wait("sync", "sVout", 2 * to + 2)
            P.op("sync", lambda E, to=to: E.dma_start(
                out=out_d[128 * to:128 * (to + 1), :],
                in_=YO[:, 1024 * (to % 3):1024 * (to % 3 + 1)]),
                inc="sOut%d" % (to % 2), n=16)
        if 1 <= t <= NT:
            td = t - 1
            for c in range(32):
                P.wait("gpsimd", "sTdn", 32 * td + c - 7)
                P.op("gpsimd", lambda E, c=c: E.dma_start(
                    out=WD[:, 1024 * (c % 8):1024 * (c % 8 + 1)],
                    in_=wdq_d[128 * c:128 * (c + 1), :]),
                    inc="sWDl%d" % (c % 8), n=16)
    P.wait("sync", "sOut0", 16 * (NT // 2))
    P.wait("sync", "sOut1", 16 * (NT // 2))
    if dbg:
        dbg_specs = [
            ("NEGB", [128, 1], F32), ("RDEN", [128, NT], F32),
            ("MEAN", [128, NT], F32), ("RS", [128, NT], F32),
            ("SWG", [128, 1], F32), ("SWU", [128, 1], F32), ("SWD", [128, 1], F32),
            ("MWG127n", [128, 1], F32), ("MWD127n", [128, 1], F32),
            ("CT", [128, 8], F32), ("DEN", [1, 256], F32),
            ("ATT", [128, S], BF16), ("KF", [128, S], BF16),
            ("QF", [128, S], BF16), ("VT", [128, S], BF16),
            ("WG", [128, 8 * INNER], BF16),
            ("HQT", [128, 4096], BF16), ("ATX", [128, 1024], BF16),
            ("H", [128, 4096], F32), ("XN2", [128, 1024], F32),
            ("YO", [128, 2048], F32), ("SC2", [1, 8], F32),
        ]
        for nm, shp, dt in dbg_specs:
            od = nc.dram_tensor("dbg_" + nm, shp, dt, kind="ExternalOutput").ap()
            P.op("sync", lambda E, od=od, nm=nm: E.dma_start(out=od, in_=A[nm][:, :]),
                 inc="sOut0", n=16)
        P.wait("sync", "sOut0", 16 * (NT // 2) + 16 * len(dbg_specs))

    def b2_front_tensor(t):
        s = t % 2
        P.wait("tensor", "sVout", 2 * (t - 1))
        if t >= 1:
            P.wait("tensor", "sVh", 8 * (t - 1) + 6)
            P.wait("tensor", "sSgl", 8 * (t - 1) + 6)
        for b in range(2):
            P.wait("tensor", "sVaq", 2 * t + b + 1)
            if b == 1:
                P.wait("tensor", "sSat2", 2 * t + 1)
            for k in range(4):
                d = 4 * b + k
                P.op("tensor", lambda E, d=d, k=k: E.transpose(
                    PS[:, 3072 + 128 * k:3072 + 128 * (k + 1)],
                    XN2[:, 128 * d:128 * (d + 1)], IDF[:, :]),
                    inc=("sTat2" if k == 3 else None))
        if t >= 1:
            P.wait("tensor", "sSat2", 2 * t + 2)
            for b in range(8):
                P.wait("tensor", "sVq", 8 * (t - 1) + b + 1)
                if b >= 1:
                    P.wait("tensor", "sShq", 8 * (t - 1) + b)
                for k in range(4):
                    c = 4 * b + k
                    P.op("tensor", lambda E, c=c, k=k: E.transpose(
                        PS[:, 3072 + 128 * k:3072 + 128 * (k + 1)],
                        H[:, 128 * c:128 * (c + 1)], IDF[:, :]),
                        inc=("sThq" if k == 3 else None))
        P.wait("tensor", "sSat2", 2 * (t + 1))
        for e in range(8):
            se = (e + 2) % 4
            P.wait("tensor", "sVh", 8 * t + e - 3)
            P.wait("tensor", "sSgl", 8 * t + e - 3)
            gcol, ucol = 1024 * se, 1024 * se + 512
            wcol = 2048 * (e // 4) + 512 * (e % 4)
            for d in range(8):
                st, sp = (d == 0), (d == 7)
                P.op("tensor", lambda E, d=d, st=st, sp=sp, gcol=gcol, wcol=wcol:
                     E.matmul(PS[:, gcol:gcol + 512], ATX[:, 128 * d:128 * (d + 1)],
                              WG[:, 4096 * d + wcol:4096 * d + wcol + 512],
                              start=st, stop=sp))
                P.op("tensor", lambda E, d=d, st=st, sp=sp, ucol=ucol, wcol=wcol:
                     E.matmul(PS[:, ucol:ucol + 512], ATX[:, 128 * d:128 * (d + 1)],
                              WU[:, 4096 * d + wcol:4096 * d + wcol + 512],
                              start=st, stop=sp), inc=("sTgu" if sp else None))

    def b2_last_hqt():
        t = NT
        P.wait("tensor", "sSat2", 2 * NT)
        P.wait("tensor", "sVout", 2 * NT - 2)
        for b in range(8):
            P.wait("tensor", "sVq", 8 * (t - 1) + b + 1)
            if b >= 1:
                P.wait("tensor", "sShq", 8 * (t - 1) + b)
            for k in range(4):
                c = 4 * b + k
                P.op("tensor", lambda E, c=c, k=k: E.transpose(
                    PS[:, 3072 + 128 * k:3072 + 128 * (k + 1)],
                    H[:, 128 * c:128 * (c + 1)], IDF[:, :]),
                    inc=("sThq" if k == 3 else None))

    def b2_tail_tensor(t):
        s = t % 2
        P.wait("tensor", "sVh", min(8 * (t + 1) + 6, 8 * NT))
        P.wait("tensor", "sSgl", min(8 * (t + 1) + 6, 8 * NT))
        P.wait("tensor", "sVout", 2 * t - 1)
        P.op("tensor", lambda E, t=t: E.matmul(
            PS[:, 2048:2560], ATT[:, 128 * t:128 * (t + 1)], WO[:, 0:512],
            start=True, stop=True))
        P.op("tensor", lambda E, t=t: E.matmul(
            PS[:, 2560:3072], ATT[:, 128 * t:128 * (t + 1)], WO[:, 512:1024],
            start=True, stop=True), inc="sTpj")
        P.wait("tensor", "sShq", 8 * (t + 1))
        P.wait("tensor", "sVout", 2 * t)
        for c in range(32):
            P.wait("tensor", "sWDl%d" % (c % 8), 16 * (4 * t + c // 8 + 1))
            st, sp = (c == 0), (c == 31)
            P.op("tensor", lambda E, c=c, st=st, sp=sp: E.matmul(
                PS[:, 3072:3584], HQT[:, 128 * c:128 * (c + 1)],
                WD[:, 1024 * (c % 8):1024 * (c % 8) + 512], start=st, stop=sp))
            P.op("tensor", lambda E, c=c, st=st, sp=sp: E.matmul(
                PS[:, 3584:4096], HQT[:, 128 * c:128 * (c + 1)],
                WD[:, 1024 * (c % 8) + 512:1024 * (c % 8 + 1)], start=st, stop=sp),
                inc="sTdn")

    qs_cnt = {"s": 0, "v": 0}

    def quant_scalar_sub(z_slice, scale_ap, wait_sem, wait_val):
        g = qs_cnt["s"]; qs_cnt["s"] += 1
        sl = 512 * (g % 2)
        P.wait("scalar", wait_sem, wait_val)
        P.wait("scalar", "sVqd", g - 1)
        P.op("scalar", lambda E, z_slice=z_slice, scale_ap=scale_ap, sl=sl:
             E.activation(QS1[:, sl:sl + 512], z_slice, AF.Identity, scale=scale_ap))
        P.op("scalar", lambda E, z_slice=z_slice, scale_ap=scale_ap, sl=sl:
             E.activation(QS2[:, sl:sl + 512], z_slice, AF.Identity, scale=scale_ap),
             inc="sSqp")

    def quant_vector_sub(z_slice, inc_last=None):
        g = qs_cnt["v"]; qs_cnt["v"] += 1
        sl = 512 * (g % 2)
        P.wait("vector", "sSqp", g + 1)
        P.op("vector", lambda E, sl=sl: E.scalar_tensor_tensor(
            QS1[:, sl:sl + 512], QS2[:, sl:sl + 512], -1.0, QS1[:, sl:sl + 512],
            ALU.mult, ALU.add))
        P.op("vector", lambda E, z_slice=z_slice, sl=sl: E.scalar_tensor_tensor(
            z_slice, QS1[:, sl:sl + 512], 0.5, QS2[:, sl:sl + 512],
            ALU.is_ge, ALU.add))
        P.op("vector", lambda E, z_slice=z_slice, sl=sl: E.scalar_tensor_tensor(
            z_slice, QS1[:, sl:sl + 512], -0.5, z_slice,
            ALU.is_lt, ALU.subtract), inc=inc_last)
        P.op("vector", lambda E: E.tensor_copy(SC1[:, 5:6], SC1[:, 5:6]), inc="sVqd")

    def b2_prep_vector(t):
        s3 = t % 3
        P.wait("vector", "sX2ld", 16 * (t + 1))
        P.wait("vector", "sTat2", 2 * t)
        P.op("vector", lambda E, t=t, s3=s3: E.tensor_scalar(
            XN2[:, :], YO[:, 1024 * s3:1024 * (s3 + 1)],
            MEAN[:, t:t + 1], RS[:, t:t + 1], ALU.subtract, ALU.mult))
        P.opf("vector", lambda E: E.reduce_max(SC1[:, 0:1], XN2[:, :], axis=AX.X,
                                               apply_absolute_value=True))
        P.opf("vector", lambda E: E.tensor_scalar(SC1[:, 0:1], SC1[:, 0:1],
                                                  1e-5, None, ALU.max))
        P.opf("vector", lambda E: E.reciprocal(SC1[:, 1:2], SC1[:, 0:1]))
        P.opf("vector", lambda E, s3=s3: E.tensor_scalar(
            CT[:, 4 * s3:4 * s3 + 1], SC1[:, 1:2], 127.0, None, ALU.mult))
        P.opf("vector", lambda E, s3=s3: E.tensor_tensor(
            CT[:, 4 * s3 + 1:4 * s3 + 2], SC1[:, 0:1], A["MWG127n"][:, :], ALU.mult))
        P.opf("vector", lambda E, s3=s3: E.tensor_tensor(
            CT[:, 4 * s3 + 2:4 * s3 + 3], SC1[:, 0:1], A["MWU127n"][:, :], ALU.mult))
        P.op("vector", lambda E: E.tensor_copy(SC1[:, 6:7], SC1[:, 6:7]), inc="sVln")
        for k in range(2):
            quant_vector_sub(XN2[:, 512 * k:512 * (k + 1)], inc_last="sVaq")

    def b2_eighths_vector(t):
        s3 = t % 3
        for e in range(8):
            se = (e + 2) % 4
            P.wait("vector", "sSgl", 8 * t + e + 1)
            ucol = 1024 * se + 512
            P.op("vector", lambda E, e=e, ucol=ucol, s3=s3: E.scalar_tensor_tensor(
                H[:, 512 * e:512 * (e + 1)], PS[:, ucol:ucol + 512],
                CT[:, 4 * s3 + 2:4 * s3 + 3], H[:, 512 * e:512 * (e + 1)],
                ALU.mult, ALU.mult), inc="sVh")
            P.opf("vector", lambda E, e=e: E.reduce_max(
                SC1[:, 4:5], H[:, 512 * e:512 * (e + 1)], axis=AX.X,
                apply_absolute_value=True))
            if e == 0:
                P.opf("vector", lambda E: E.tensor_copy(SC1[:, 2:3], SC1[:, 4:5]))
            else:
                P.opf("vector", lambda E: E.tensor_tensor(
                    SC1[:, 2:3], SC1[:, 2:3], SC1[:, 4:5], ALU.max))

    def b2_quant_vector_scales(t):
        s3 = t % 3
        P.opf("vector", lambda E: E.tensor_scalar(SC1[:, 2:3], SC1[:, 2:3],
                                                  1e-5, None, ALU.max))
        P.opf("vector", lambda E: E.reciprocal(SC1[:, 3:4], SC1[:, 2:3]))
        P.opf("vector", lambda E: E.tensor_scalar(SC1[:, 3:4], SC1[:, 3:4],
                                                  127.0, None, ALU.mult))
        P.opf("vector", lambda E, s3=s3: E.tensor_tensor(
            CT[:, 4 * s3 + 3:4 * s3 + 4], SC1[:, 2:3], A["MWD127n"][:, :], ALU.mult))
        P.op("vector", lambda E: E.tensor_copy(SC1[:, 7:8], SC1[:, 7:8]), inc="sVsh")

    def b2_quant_vector_fixups(t):
        P.wait("vector", "sThq", 8 * t)
        for k in range(8):
            quant_vector_sub(H[:, 512 * k:512 * (k + 1)], inc_last="sVq")

    def b2_tail_vector(t):
        s3 = t % 3
        P.wait("vector", "sTpj", t + 1)
        P.op("vector", lambda E, t=t, s3=s3: E.scalar_tensor_tensor(
            YO[:, 1024 * s3:1024 * (s3 + 1)], PS[:, 2048:3072],
            RDEN[:, t:t + 1], YO[:, 1024 * s3:1024 * (s3 + 1)],
            ALU.mult, ALU.add), inc="sVout")
        P.wait("vector", "sTdn", 32 * (t + 1))
        P.op("vector", lambda E, t=t, s3=s3: E.scalar_tensor_tensor(
            YO[:, 1024 * s3:1024 * (s3 + 1)], PS[:, 3072:4096],
            CT[:, 4 * s3 + 3:4 * s3 + 4], YO[:, 1024 * s3:1024 * (s3 + 1)],
            ALU.mult, ALU.add), inc="sVout")

    def b2_scalar(t):
        s = t % 3
        P.wait("scalar", "sTgu", 8 * t)
        for b in range(2):
            P.wait("scalar", "sTat2", 2 * t + b + 1)
            P.op("scalar", lambda E, b=b: E.copy(
                ATX[:, 512 * b:512 * (b + 1)], PS[:, 3072:3584]), inc="sSat2")
        if t >= 1:
            P.wait("scalar", "sTdn", 32 * (t - 1))
            for b in range(8):
                P.wait("scalar", "sThq", 8 * (t - 1) + b + 1)
                P.op("scalar", lambda E, b=b: E.copy(
                    HQT[:, 512 * b:512 * (b + 1)], PS[:, 3072:3584]), inc="sShq")
        P.wait("scalar", "sVq", 8 * t)
        for e in range(8):
            se = (e + 2) % 4
            P.wait("scalar", "sTgu", 8 * t + e + 1)
            gcol = 1024 * se
            P.op("scalar", lambda E, e=e, gcol=gcol, s=s: E.activation(
                H[:, 512 * e:512 * (e + 1)], PS[:, gcol:gcol + 512], AF.Silu,
                scale=CT[:, 4 * s + 1:4 * s + 2]), inc="sSgl")
        if t + 1 < NT:
            s3n = (t + 1) % 3
            for k in range(2):
                quant_scalar_sub(XN2[:, 512 * k:512 * (k + 1)],
                                 CT[:, 4 * s3n:4 * s3n + 1], "sVln", t + 2)
        for k in range(8):
            quant_scalar_sub(H[:, 512 * k:512 * (k + 1)], SC1[:, 3:4],
                             "sVsh", t + 1)

    def b2_last_scalar_hq():
        t = NT
        P.wait("scalar", "sTdn", 32 * (t - 1))
        for b in range(8):
            P.wait("scalar", "sThq", 8 * (t - 1) + b + 1)
            P.op("scalar", lambda E, b=b: E.copy(
                HQT[:, 512 * b:512 * (b + 1)], PS[:, 3072:3584]), inc="sShq")

    P.op("vector", lambda E: E.memset(SC1[:, 4:8], 0.0))
    for k in range(2):
        quant_scalar_sub(XN2[:, 512 * k:512 * (k + 1)], CT[:, 0:1], "sVln", 1)
    b2_prep_vector(0)
    for t in range(NT + 1):
        if t < NT:
            b2_front_tensor(t)
            b2_scalar(t)
            b2_eighths_vector(t)
            b2_quant_vector_scales(t)
            if t + 1 < NT:
                b2_prep_vector(t + 1)
            b2_quant_vector_fixups(t)
            if t >= 1:
                b2_tail_vector(t - 1)
        else:
            b2_last_hqt()
            b2_last_scalar_hq()
            b2_tail_vector(NT - 1)
        if t >= 1:
            b2_tail_tensor(t - 1)

    with nc.Block() as block:
        @block.sync
        def _(E):
            for fn in P.streams["sync"]:
                fn(E)

        @block.tensor
        def _(E):
            for fn in P.streams["tensor"]:
                fn(E)

        @block.vector
        def _(E):
            for fn in P.streams["vector"]:
                fn(E)

        @block.scalar
        def _(E):
            for fn in P.streams["scalar"]:
                fn(E)

        @block.gpsimd
        def _(E):
            for fn in P.streams["gpsimd"]:
                fn(E)

    return nc


_NC_CACHE = None


def _get_nc():
    global _NC_CACHE
    if _NC_CACHE is None:
        _NC_CACHE = build_nc()
    return _NC_CACHE


def _host_constants():
    inv_freq = (1.0 / (np.float32(10000.0) **
                       (np.arange(0, HDIM, 2, dtype=np.float32) / np.float32(HDIM))))
    inv_freq = inv_freq.astype(np.float32)
    tpos = np.arange(S - 1, dtype=np.float32)
    ang = (tpos[:, None] * inv_freq[None, :]).astype(np.float32)
    cos = np.cos(ang).astype(np.float32)
    sin = np.sin(ang).astype(np.float32)
    trig = np.zeros((S, 128), np.float32)
    trig[0, 0:64] = 1.0
    trig[1:, 0:64] = cos
    trig[1:, 64:128] = sin
    idb = np.eye(128, dtype=BF16_NP)
    idf = np.eye(128, dtype=np.float32)
    onesb = np.ones((128, 1), BF16_NP)
    onesf = np.ones((128, 1), np.float32)
    onesrow = np.ones((1, 128), np.float32)
    return trig, idb, idf, onesb, onesf, onesrow


def kernel(L0_attn_x, L0_attn_wq, L0_attn_wk, L0_attn_wv, L0_attn_wo,
           L0_ffn_w_gate, L0_ffn_w_up, L0_ffn_w_down):
    x = np.ascontiguousarray(np.asarray(L0_attn_x, dtype=np.float32))
    wq = np.asarray(L0_attn_wq, np.float32)
    wk = np.asarray(L0_attn_wk, np.float32)
    wv = np.asarray(L0_attn_wv, np.float32)
    wo = np.asarray(L0_attn_wo, np.float32)
    qkvT = np.ascontiguousarray(
        np.concatenate([wq.T, wk.T, wv.T], axis=1)).astype(np.float32)
    woT = np.ascontiguousarray(wo.T)
    wgT = np.ascontiguousarray(np.asarray(L0_ffn_w_gate, np.float32).T)
    wuT = np.ascontiguousarray(np.asarray(L0_ffn_w_up, np.float32).T)
    wdT = np.ascontiguousarray(np.asarray(L0_ffn_w_down, np.float32).T)
    trig, idb, idf, onesb, onesf, onesrow = _host_constants()

    nc = _get_nc()
    in_maps = []
    for b in range(B):
        in_maps.append({
            "x": np.ascontiguousarray(x[b]),
            "qkvT": qkvT, "woT": woT,
            "wgT": wgT, "wuT": wuT, "wdT": wdT,
            "trig": trig, "idb": idb, "idf": idf,
            "onesb": onesb, "onesf": onesf, "onesrow": onesrow,
        })
    res = run_bass_kernel_spmd(nc, in_maps, core_ids=list(range(B)))
    out = np.stack([np.asarray(res.results[b]["out"], dtype=np.float32)
                    for b in range(B)], axis=0)
    return out


# version 20
# speedup vs baseline: 1.1611x; 1.0054x over previous
import numpy as np
import ml_dtypes

import concourse.bass as bass
import concourse.mybir as mybir
from concourse.bass_utils import run_bass_kernel_spmd

F32 = mybir.dt.float32
BF16 = mybir.dt.bfloat16
I32 = mybir.dt.int32
I8 = mybir.dt.int8
AF = mybir.ActivationFunctionType
ALU = mybir.AluOpType
AX = mybir.AxisListType
BF16_NP = ml_dtypes.bfloat16

B, S, D, HDIM, INNER = 8, 4096, 1024, 128, 4096
NT = S // 128
QI = S // 256
NWA = 48
NWB = 64
INV_SQRT_HD = 0.08838834764831845
N_ELEM_W = float(INNER * D)

ENGINES = ("sync", "tensor", "vector", "scalar", "gpsimd")


class Prog:
    def __init__(self):
        self.streams = {e: [] for e in ENGINES}
        self.counts = {}
        self._h = None

    def sem(self, name):
        self.counts[name] = 0

    def emit(self, eng, fn):
        self.streams[eng].append(fn)

    def wait(self, eng, sem_name, val):
        if val <= 0:
            return
        self.emit(eng, lambda E, s=sem_name, v=val: E.wait_ge(self._h[s], v))

    def op(self, eng, builder, inc=None, n=1):
        if inc is None:
            self.emit(eng, builder)
        else:
            self.counts[inc] += n

            def fn(E, b=builder, s=inc, k=n):
                b(E).then_inc(self._h[s], k)
            self.emit(eng, fn)

    def c(self, name):
        return self.counts[name]

    def opf(self, eng, builder):
        s = "chain_" + eng
        self.op(eng, builder, inc=s)
        self.wait(eng, s, self.counts[s])


def build_nc(dbg=False):
    nc = bass.Bass("TRN2", target_bir_lowering=False,
                   detect_race_conditions=False)

    x_d = nc.dram_tensor("x", [S, D], F32, kind="ExternalInput").ap()
    qkvT_d = nc.dram_tensor("qkvT", [D, 384], F32, kind="ExternalInput").ap()
    woT_d = nc.dram_tensor("woT", [HDIM, D], F32, kind="ExternalInput").ap()
    wgT_d = nc.dram_tensor("wgT", [D, INNER], F32, kind="ExternalInput").ap()
    wuT_d = nc.dram_tensor("wuT", [D, INNER], F32, kind="ExternalInput").ap()
    wdT_d = nc.dram_tensor("wdT", [INNER, D], F32, kind="ExternalInput").ap()
    trig_d = nc.dram_tensor("trig", [S, 128], F32, kind="ExternalInput").ap()
    idb_d = nc.dram_tensor("idb", [128, 128], BF16, kind="ExternalInput").ap()
    idf_d = nc.dram_tensor("idf", [128, 128], F32, kind="ExternalInput").ap()
    onesb_d = nc.dram_tensor("onesb", [128, 1], BF16, kind="ExternalInput").ap()
    onesf_d = nc.dram_tensor("onesf", [128, 1], F32, kind="ExternalInput").ap()
    onesrow_d = nc.dram_tensor("onesrow", [1, 128], F32, kind="ExternalInput").ap()
    out_d = nc.dram_tensor("out", [S, D], F32, kind="ExternalOutput").ap()
    wdq_d = nc.dram_tensor("wdq_scratch", [INNER, D], I8, kind="Internal").ap()

    BASE = 16512
    TOP = 229376
    A = {}
    def sb(name, shape, dtype, off):
        assert off % 32 == 0, (name, off)
        A[name] = nc.alloc_sbuf_tensor_at(name, shape, dtype, offset=off)
        return A[name]

    o = BASE
    sb("WG", [128, 8 * INNER], BF16, o); o += 65536
    sb("WU", [128, 8 * INNER], BF16, o); o += 65536
    sb("ATT", [128, S], BF16, o); o += 8192
    sb("WO", [128, D], BF16, o); o += 2048
    for nm, shp, dt, szb in [
        ("MEAN", [128, NT], F32, 128), ("RS", [128, NT], F32, 128),
        ("RDEN", [128, NT], F32, 128),
        ("IDB", [128, 128], BF16, 256), ("IDF", [128, 128], F32, 512),
        ("ONESB", [128, 1], BF16, 32), ("ONESF", [128, 1], F32, 32),
        ("ONESROW", [1, 128], F32, 512),
        ("NEGB", [128, 1], F32, 32),
        ("SWG", [128, 1], F32, 32), ("SWU", [128, 1], F32, 32), ("SWD", [128, 1], F32, 32),
        ("MWG127n", [128, 1], F32, 32), ("MWU127n", [128, 1], F32, 32),
        ("MWD127n", [128, 1], F32, 32),
        ("ACCG", [128, 1], F32, 32), ("ACCU", [128, 1], F32, 32), ("ACCD", [128, 1], F32, 32),
        ("PAP", [128, 2], F32, 32),
        ("SC1", [128, 8], F32, 32),
        ("SC2", [1, 8], F32, 32),
        ("DEN", [1, 256], F32, 1024),
    ]:
        sb(nm, shp, dt, o); o += szb
    PERS_END = o
    OVA = (o + 31) // 32 * 32
    sb("KF", [128, S], BF16, OVA)
    sb("QF", [128, S], BF16, OVA + 8192)
    sb("VT", [128, S], BF16, OVA + 16384)
    OVB = OVA + 24576
    sb("QKVW", [128, 3072], BF16, OVB)
    sb("XB", [128, 2048], F32, OVB + 6144)
    sb("XN", [128, 1024], F32, OVB + 14336)
    sb("XNT", [128, 1024], BF16, OVB + 18432)
    sb("RT", [128, 384], F32, OVB + 20480)
    sb("TRG", [128, 256], F32, OVB + 22016)
    sb("WLD", [128, 4096], F32, OVB + 23040)
    sb("SQS", [128, 256], BF16, OVB + 39424)
    assert OVB + 39936 <= TOP
    sb("PT", [128, 8192], BF16, OVB)
    sb("WLB", [128, 4096], F32, OVB + 16384)
    sb("WDQA0", [128, 1024], I8, OVB + 16384)
    sb("WDQA1", [128, 1024], I8, OVB + 24576)
    sb("TL", [128, 1024], F32, OVB + 32768)
    sb("TG", [128, 1024], F32, OVB + 36864)
    assert OVB + 40960 <= TOP
    o = OVA
    sb("H", [128, 4096], F32, o); o += 16384
    sb("HQT", [128, 4096], BF16, o); o += 8192
    sb("WD", [128, 8192], BF16, o); o += 16384
    sb("XN2", [128, 1024], F32, o); o += 4096
    sb("ATX", [128, 1024], BF16, o); o += 2048
    sb("QS1", [128, 1024], F32, o); o += 4096
    sb("QS2", [128, 1024], I32, o); o += 4096
    sb("YO", [128, 3072], F32, o); o += 12288
    sb("CT", [128, 12], F32, o); o += 64
    assert o <= TOP, o

    _psg = nc.psum_tensor("PS", [128, 4096], F32)
    PS = _psg.__enter__()

    P = Prog()
    sem_names = [
        "sCstG", "sCstS", "sXld0", "sXld1", "sTRG0", "sTRG1", "sWA0", "sWA1",
        "sSabs", "sVacc",
        "sSsq", "sVrec", "sSrs0", "sVxn",
        "sTtr", "sSxnt", "sTqkv", "sVrope", "sTqk2", "sSqk", "sSv",
        "sVnorm", "sTnorm", "sVbp", "sSbp", "sTbp",
        "sTsc", "sSexp", "sTpv", "sTden", "sSatt", "sSden", "sTrd", "sVrd",
        "sWB0", "sWB1", "sWBdone", "sDQ0", "sDQ1",
        "sX2ld", "sWDl0", "sWDl1", "sWDl2", "sWDl3", "sWDl4", "sWDl5", "sWDl6", "sWDl7",
        "sVaq", "sTat2", "sSat2", "sTgu", "sSgl", "sVh", "sVq",
        "sThq", "sShq", "sTpj", "sTdn", "sVout", "sOut0", "sOut1",
        "chain_vector", "chain_scalar", "chain_tensor", "chain_sync", "chain_gpsimd",
        "sVln", "sVsh", "sSqp", "sVqd",
    ]
    sems = {}
    for nm in sem_names:
        sems[nm] = nc.semaphore(nm).__enter__()
        P.sem(nm)
    P._h = sems

    KF, QF, VT = A["KF"], A["QF"], A["VT"]
    ATT, WO, QKVW = A["ATT"], A["WO"], A["QKVW"]
    XB, XN, XNT, RT, TRG, WLD, SQS = (A["XB"], A["XN"], A["XNT"], A["RT"],
                                      A["TRG"], A["WLD"], A["SQS"])
    MEAN, RS, RDEN = A["MEAN"], A["RS"], A["RDEN"]
    IDB, IDF = A["IDB"], A["IDF"]
    ONESB, ONESF, ONESROW = A["ONESB"], A["ONESF"], A["ONESROW"]
    NEGB, SC1, SC2, DEN, PAP = A["NEGB"], A["SC1"], A["SC2"], A["DEN"], A["PAP"]
    PT, WLB, TL, TG = A["PT"], A["WLB"], A["TL"], A["TG"]
    WDQA = (A["WDQA0"], A["WDQA1"])
    WG, WU = A["WG"], A["WU"]
    H, HQT, WD = A["H"], A["HQT"], A["WD"]
    XN2, ATX = A["XN2"], A["ATX"]
    QS1, QS2, YO, CT = A["QS1"], A["QS2"], A["YO"], A["CT"]

    for d in range(8):
        P.op("gpsimd", lambda E, d=d: E.dma_start(
            out=QKVW[:, 384 * d:384 * (d + 1)],
            in_=qkvT_d[128 * d:128 * (d + 1), :]), inc="sCstG", n=16)
    P.op("gpsimd", lambda E: E.dma_start(out=WO[:, :], in_=woT_d), inc="sCstG", n=16)
    for dst, src in [(IDB, idb_d), (IDF, idf_d), (ONESB, onesb_d),
                     (ONESF, onesf_d), (ONESROW, onesrow_d)]:
        P.op("sync", lambda E, d=dst, s=src: E.dma_start(out=d[:, :], in_=s),
             inc="sCstS", n=16)

    def p1_sync_tile(t):
        P.wait("sync", "sTtr", t - 1)
        P.op("sync", lambda E, t=t: E.dma_start(
            out=XB[:, 1024 * (t % 2):1024 * (t % 2 + 1)],
            in_=x_d[128 * t:128 * (t + 1), :]), inc="sXld%d" % (t % 2), n=16)
        P.wait("sync", "sVrope", t - 1)
        P.op("sync", lambda E, t=t: E.dma_start(
            out=TRG[:, 128 * (t % 2):128 * (t % 2 + 1)],
            in_=trig_d[128 * t:128 * (t + 1), :]), inc="sTRG%d" % (t % 2), n=16)

    def wa_src(j):
        m, jj = j // 16, j % 16
        rb, h = jj // 2, jj % 2
        if m == 0:
            return wgT_d[128 * rb:128 * (rb + 1), 2048 * h:2048 * (h + 1)]
        if m == 1:
            return wuT_d[128 * rb:128 * (rb + 1), 2048 * h:2048 * (h + 1)]
        return wdT_d.rearrange("(c p two) n -> c p (two n)", two=2, p=128)[jj]

    def p1_sync_passA(j):
        P.wait("sync", "sSabs", j - 1)
        P.op("sync", lambda E, j=j: E.dma_start(
            out=WLD[:, 2048 * (j % 2):2048 * (j % 2 + 1)], in_=wa_src(j)),
            inc="sWA%d" % (j % 2), n=16)

    def p1_vector(t):
        sl = XB[:, 1024 * (t % 2):1024 * (t % 2 + 1)]
        P.wait("vector", "sXld%d" % (t % 2), 16 * (t // 2 + 1))
        P.opf("vector", lambda E, sl=sl: E.reduce_sum(SC1[:, 0:1], sl, axis=AX.X))
        P.opf("vector", lambda E: E.tensor_scalar(SC1[:, 0:1], SC1[:, 0:1],
                                                  1.0 / 1024.0, None, ALU.mult))
        P.wait("vector", "sSsq", t + 1)
        P.opf("vector", lambda E: E.tensor_scalar(SC1[:, 1:2], SC1[:, 2:3],
                                                  1.0 / 1024.0, None, ALU.mult))
        P.opf("vector", lambda E: E.tensor_tensor(SC1[:, 4:5], SC1[:, 0:1],
                                                  SC1[:, 0:1], ALU.mult))
        P.opf("vector", lambda E: E.tensor_sub(SC1[:, 1:2], SC1[:, 1:2], SC1[:, 4:5]))
        P.opf("vector", lambda E: E.tensor_scalar(SC1[:, 1:2], SC1[:, 1:2],
                                                  1e-5, None, ALU.add))
        P.op("vector", lambda E: E.reciprocal(SC1[:, 2:3], SC1[:, 1:2]), inc="sVrec")
        P.opf("vector", lambda E, t=t: E.tensor_copy(MEAN[:, t:t + 1], SC1[:, 0:1]))
        P.wait("vector", "sSrs0", t + 1)
        P.opf("vector", lambda E, t=t: E.tensor_copy(RS[:, t:t + 1], SC1[:, 3:4]))
        P.op("vector", lambda E, sl=sl, t=t: E.tensor_scalar(
            sl, sl, MEAN[:, t:t + 1], RS[:, t:t + 1],
            ALU.subtract, ALU.mult), inc="sVxn")

    def p1_rope(t):
        P.wait("vector", "sTqkv", t + 1)
        P.wait("vector", "sTRG%d" % (t % 2), 16 * (t // 2 + 1))
        cosv = TRG[:, 128 * (t % 2):128 * (t % 2) + 64]
        sinv = TRG[:, 128 * (t % 2) + 64:128 * (t % 2) + 128]
        for base, off in ((0, 0), (512, 128)):
            ev = PS[:, base:base + 127:2]
            od = PS[:, base + 1:base + 128:2]
            dst = RT[:, off:off + 128]
            scr = RT[:, 256:320]
            last = (base == 512)
            P.opf("vector", lambda E, ev=ev, cosv=cosv, dst=dst: E.tensor_tensor(
                dst[:, 0:127:2], ev, cosv, ALU.mult))
            P.opf("vector", lambda E, od=od, sinv=sinv, scr=scr: E.tensor_tensor(
                scr, od, sinv, ALU.mult))
            P.op("vector", lambda E, dst=dst, scr=scr: E.tensor_sub(
                dst[:, 0:127:2], dst[:, 0:127:2], scr))
            P.opf("vector", lambda E, ev=ev, sinv=sinv, dst=dst: E.tensor_tensor(
                dst[:, 1:128:2], ev, sinv, ALU.mult))
            P.opf("vector", lambda E, od=od, cosv=cosv, scr=scr: E.tensor_tensor(
                scr, od, cosv, ALU.mult))
            P.op("vector", lambda E, dst=dst, scr=scr: E.tensor_add(
                dst[:, 1:128:2], dst[:, 1:128:2], scr),
                inc=("sVrope" if last else None))

    def p1_scalar(t):
        sl = XB[:, 1024 * (t % 2):1024 * (t % 2 + 1)]
        P.wait("scalar", "sXld%d" % (t % 2), 16 * (t // 2 + 1))
        P.wait("scalar", "sVrec", t)
        P.op("scalar", lambda E, sl=sl: E.activation(
            XN[:, :], sl, AF.Square, accum_out=SC1[:, 2:3]), inc="sSsq")
        P.wait("scalar", "sVrec", t + 1)
        P.wait("scalar", "sVxn", t)
        P.op("scalar", lambda E: E.activation(SC1[:, 3:4], SC1[:, 2:3], AF.Sqrt),
             inc="sSrs0")
        P.wait("scalar", "sTtr", t + 1)
        P.op("scalar", lambda E: E.copy(XNT[:, 0:512], PS[:, 3072:3584]))
        P.op("scalar", lambda E: E.copy(XNT[:, 512:1024], PS[:, 3584:4096]),
             inc="sSxnt")
        P.wait("scalar", "sTqkv", t + 1)
        P.op("scalar", lambda E, t=t: E.copy(VT[:, 128 * t:128 * (t + 1)],
                                             PS[:, 1024:1152]), inc="sSv")
        P.wait("scalar", "sTqk2", t + 1)
        P.op("scalar", lambda E, t=t: E.copy(QF[:, 128 * t:128 * (t + 1)],
                                             PS[:, 1536:1664]))
        P.op("scalar", lambda E, t=t: E.copy(KF[:, 128 * t:128 * (t + 1)],
                                             PS[:, 1664:1792]), inc="sSqk")

    def passA_chunk(j):
        P.wait("scalar", "sWA%d" % (j % 2), 16 * (j // 2 + 1))
        P.wait("scalar", "sVacc", j - 1)
        P.op("scalar", lambda E, j=j: E.activation(
            WLD[:, 2048 * (j % 2):2048 * (j % 2 + 1)],
            WLD[:, 2048 * (j % 2):2048 * (j % 2 + 1)], AF.Abs,
            accum_out=PAP[:, j % 2:j % 2 + 1]), inc="sSabs")
        acc = A["ACCG"] if j < 16 else (A["ACCU"] if j < 32 else A["ACCD"])
        P.wait("vector", "sSabs", j + 1)
        if j % 16 == 0:
            P.op("vector", lambda E, acc=acc, j=j: E.tensor_copy(
                acc[:, :], PAP[:, j % 2:j % 2 + 1]), inc="sVacc")
        else:
            P.op("vector", lambda E, acc=acc, j=j: E.tensor_add(
                acc[:, :], acc[:, :], PAP[:, j % 2:j % 2 + 1]), inc="sVacc")
        P.wait("vector", "sVacc", P.c("sVacc"))

    def p1_tensor(t):
        P.wait("tensor", "sVxn", t + 1)
        P.wait("tensor", "sSxnt", t)
        xo = 1024 * (t % 2)
        for d in range(8):
            base = 3072 + 128 * d
            P.op("tensor", lambda E, d=d, base=base, xo=xo: E.transpose(
                PS[:, base:base + 128], XB[:, xo + 128 * d:xo + 128 * (d + 1)],
                IDF[:, :]), inc=("sTtr" if d == 7 else None))
        P.wait("tensor", "sSxnt", t + 1)
        P.wait("tensor", "sVrope", t)
        P.wait("tensor", "sSv", t)
        for d in range(8):
            lhsT = XNT[:, 128 * d:128 * (d + 1)]
            st, sp = (d == 0), (d == 7)
            P.op("tensor", lambda E, lhsT=lhsT, d=d, st=st, sp=sp: E.matmul(
                PS[:, 0:128], lhsT, QKVW[:, 384 * d:384 * d + 128],
                start=st, stop=sp))
            P.op("tensor", lambda E, lhsT=lhsT, d=d, st=st, sp=sp: E.matmul(
                PS[:, 512:640], lhsT, QKVW[:, 384 * d + 128:384 * d + 256],
                start=st, stop=sp))
            P.op("tensor", lambda E, lhsT=lhsT, d=d, st=st, sp=sp: E.matmul(
                PS[:, 1024:1152], lhsT, QKVW[:, 384 * d + 256:384 * d + 384],
                start=st, stop=sp), inc=("sTqkv" if sp else None))
        P.wait("tensor", "sVrope", t + 1)
        P.wait("tensor", "sSqk", t)
        P.op("tensor", lambda E: E.transpose(PS[:, 1536:1664], RT[:, 0:128], IDF[:, :]))
        P.op("tensor", lambda E: E.transpose(PS[:, 1664:1792], RT[:, 128:256],
                                             IDF[:, :]), inc="sTqk2")

    P.wait("tensor", "sCstG", P.c("sCstG"))
    P.wait("tensor", "sCstS", P.c("sCstS"))
    for t in range(NT):
        p1_sync_tile(t)
        p1_vector(t)
        p1_rope(t)
        p1_scalar(t)
        p1_tensor(t)
        for j in range(min(NWA, (3 * t) // 2), min(NWA, (3 * (t + 1)) // 2)):
            p1_sync_passA(j)
            passA_chunk(j)
    for j in range((3 * NT) // 2, NWA):
        p1_sync_passA(j)
        passA_chunk(j)

    P.wait("vector", "sSqk", NT)
    P.wait("tensor", "sSqk", NT)
    P.wait("tensor", "sSv", NT)
    P.wait("tensor", "sSxnt", NT)
    for w, src in ((0, KF), (1, QF)):
        for c in range(NT):
            P.wait("vector", "sTnorm", 32 * w + c - 1)
            P.op("vector", lambda E, src=src, c=c: E.tensor_tensor(
                SQS[:, 128 * (c % 2):128 * (c % 2 + 1)],
                src[:, 128 * c:128 * (c + 1)], src[:, 128 * c:128 * (c + 1)],
                ALU.mult), inc="sVnorm")
            P.wait("tensor", "sVnorm", 32 * w + c + 1)
            if w == 1 and c == 0:
                P.wait("tensor", "sVbp", 1)
            P.op("tensor", lambda E, c=c: E.matmul(
                PS[0:1, 128 * c:128 * (c + 1)], ONESB[:, :],
                SQS[:, 128 * (c % 2):128 * (c % 2 + 1)], start=True, stop=True),
                inc="sTnorm")
        P.wait("vector", "sTnorm", 32 * (w + 1))
        P.op("vector", lambda E, w=w: E.reduce_max(
            SC2[0:1, w:w + 1], PS[0:1, 0:4096], axis=AX.X), inc="sVbp")
        P.wait("vector", "sVbp", P.c("sVbp"))
    P.op("vector", lambda E: E.tensor_tensor(SC2[0:1, 2:3], SC2[0:1, 0:1],
                                             SC2[0:1, 1:2], ALU.mult), inc="sVbp")
    P.wait("vector", "sVbp", P.c("sVbp"))
    P.wait("scalar", "sVbp", P.c("sVbp"))
    P.op("scalar", lambda E: E.activation(SC2[0:1, 3:4], SC2[0:1, 2:3], AF.Sqrt),
         inc="sSbp")
    P.wait("vector", "sSbp", P.c("sSbp"))
    P.opf("vector", lambda E: E.tensor_scalar(SC2[0:1, 4:5], SC2[0:1, 3:4],
                                              INV_SQRT_HD, 1.0, ALU.mult, ALU.add))
    P.op("vector", lambda E: E.tensor_scalar(SC2[0:1, 4:5], SC2[0:1, 4:5],
                                             -1.0, None, ALU.mult), inc="sVbp")
    P.wait("vector", "sVbp", P.c("sVbp"))
    VBP1 = P.c("sVbp")
    P.wait("tensor", "sVacc", NWA)
    P.wait("tensor", "sVbp", VBP1)
    for i, acc in enumerate(("ACCG", "ACCU", "ACCD")):
        P.op("tensor", lambda E, i=i, acc=acc: E.matmul(
            PS[0:1, 2048 + 16 * i:2049 + 16 * i], ONESF[:, :], A[acc][:, :],
            start=True, stop=True), inc="sTbp")
    P.wait("vector", "sTbp", P.c("sTbp"))
    for i in range(3):
        P.op("vector", lambda E, i=i: E.tensor_scalar(
            SC2[0:1, 5 + i:6 + i], PS[0:1, 2048 + 16 * i:2049 + 16 * i],
            1.0 / N_ELEM_W, 1e-5, ALU.mult, ALU.max), inc="sVbp")
    P.wait("tensor", "sVbp", P.c("sVbp"))
    for i, col in enumerate((4, 5, 6, 7)):
        P.op("tensor", lambda E, col=col, i=i: E.matmul(
            PS[:, 2560 + 16 * i:2561 + 16 * i], ONESROW[:, :],
            SC2[0:1, col:col + 1], start=True, stop=True), inc="sTbp")
    P.wait("vector", "sTbp", P.c("sTbp"))
    P.op("vector", lambda E: E.tensor_copy(NEGB[:, :], PS[:, 2560:2561]))
    for i, (sw, mw) in enumerate((("SWG", "MWG127n"), ("SWU", "MWU127n"),
                                  ("SWD", "MWD127n"))):
        src = PS[:, 2576 + 16 * i:2577 + 16 * i]
        P.opf("vector", lambda E, sw=sw, src=src: E.reciprocal(A[sw][:, :], src))
        P.opf("vector", lambda E, mw=mw, src=src: E.tensor_scalar(
            A[mw][:, :], src, -1.0 / 127.0, None, ALU.mult))
    P.op("vector", lambda E: E.tensor_copy(SC1[:, 0:1], SC1[:, 0:1]), inc="sVbp")
    BP_DONE = P.c("sVbp")

    P.wait("sync", "sVbp", BP_DONE)

    def wb_src(u):
        if u < 16:
            rb, h = (u % 16) // 2, u % 2
            return wgT_d[128 * rb:128 * (rb + 1), 2048 * h:2048 * (h + 1)]
        if u < 32:
            uu = u - 16
            rb, h = uu // 2, uu % 2
            return wuT_d[128 * rb:128 * (rb + 1), 2048 * h:2048 * (h + 1)]
        c = u - 32
        return wdT_d[128 * c:128 * (c + 1), :]

    for u in range(NWB):
        P.wait("sync", "sWBdone", u - 1)
        if u >= 34:
            c_prev = u - 34
            P.wait("sync", "sDQ%d" % (c_prev % 2), 16 * (c_prev // 2 + 1))
        ncols = 2048 if u < 32 else 1024
        P.op("sync", lambda E, u=u, ncols=ncols: E.dma_start(
            out=WLB[:, 2048 * (u % 2):2048 * (u % 2) + ncols], in_=wb_src(u)),
            inc="sWB%d" % (u % 2), n=16)
        if u >= 32:
            c = u - 32
            P.wait("sync", "sWBdone", u + 1)
            P.op("sync", lambda E, c=c: E.dma_start(
                out=wdq_d[128 * c:128 * (c + 1), :], in_=WDQA[c % 2][:, :]),
                inc="sDQ%d" % (c % 2), n=16)

    def passB_unit(u):
        ncols = 2048 if u < 32 else 1024
        sl = WLB[:, 2048 * (u % 2):2048 * (u % 2) + ncols]
        sw = A["SWG"] if u < 16 else (A["SWU"] if u < 32 else A["SWD"])
        P.wait("vector", "sWB%d" % (u % 2), 16 * (u // 2 + 1))
        for half in range(ncols // 1024):
            hsl = sl[:, 1024 * half:1024 * (half + 1)]
            P.op("vector", lambda E, hsl=hsl, sw=sw: E.tensor_scalar(
                TL[:, :], hsl, sw[:, :], -0.5, ALU.mult, ALU.is_lt))
            P.op("vector", lambda E, hsl=hsl, sw=sw: E.tensor_scalar(
                TG[:, :], hsl, sw[:, :], 0.5, ALU.mult, ALU.is_gt))
            if u < 32:
                dstW = WG if u < 16 else WU
                cw = u % 16
                rb, hh = cw // 2, cw % 2
                dst = dstW[:, 4096 * rb + 2048 * hh + 1024 * half:
                           4096 * rb + 2048 * hh + 1024 * (half + 1)]
                last = (half == ncols // 1024 - 1)
                P.op("vector", lambda E, dst=dst: E.tensor_sub(dst, TG[:, :], TL[:, :]),
                     inc=("sWBdone" if last else None))
            else:
                c = u - 32
                if c >= 2:
                    P.wait("vector", "sDQ%d" % (c % 2), 16 * (c // 2))
                P.op("vector", lambda E, c=c: E.tensor_sub(
                    WDQA[c % 2][:, :], TG[:, :], TL[:, :]), inc="sWBdone")

    P.wait("tensor", "sVbp", BP_DONE)
    P.wait("scalar", "sVbp", BP_DONE)

    def a2_tensor_scores(i, g):
        P.wait("tensor", "sSexp", 8 * i + g - 1)
        for k in range(4):
            c = 4 * g + k
            bank = 2 * (g % 2) + k // 2
            off = 512 * bank + 256 * (k % 2)
            P.op("tensor", lambda E, c=c, off=off, i=i: E.matmul(
                PS[:, off:off + 256], KF[:, 128 * c:128 * (c + 1)],
                QF[:, 256 * i:256 * (i + 1)], start=True, stop=True),
                inc=("sTsc" if k == 3 else None))

    def a2_tensor_pv(i, g):
        P.wait("tensor", "sSexp", 8 * i + g + 1)
        if g == 0:
            P.wait("tensor", "sSatt", i)
        for k in range(4):
            c = 4 * g + k
            P.op("tensor", lambda E, c=c: E.matmul(
                PS[:, 2048:2304], VT[:, 128 * c:128 * (c + 1)],
                PT[:, 256 * c:256 * (c + 1)], start=(c == 0), stop=(c == 31)),
                inc=("sTpv" if k == 3 else None))

    for i in range(QI):
        for g in range(8):
            a2_tensor_scores(i, g)
            if g >= 1:
                a2_tensor_pv(i, g - 1)
        a2_tensor_pv(i, 7)
        P.wait("tensor", "sSden", i)
        for c in range(32):
            P.op("tensor", lambda E, c=c: E.matmul(
                PS[0:1, 2560:2816], ONESB[:, :], PT[:, 256 * c:256 * (c + 1)],
                start=(c == 0), stop=(c == 31)), inc=("sTden" if c == 31 else None))
        P.wait("tensor", "sSden", i + 1)
        P.wait("tensor", "sVrd", i)
        P.op("tensor", lambda E: E.transpose(PS[:, 3072:3073], DEN[0:1, 0:128],
                                             IDF[0:1, 0:1]))
        P.op("tensor", lambda E: E.transpose(PS[:, 3073:3074], DEN[0:1, 128:256],
                                             IDF[0:1, 0:1]), inc="sTrd")
        for g in range(8):
            if g == 0:
                P.wait("scalar", "sTden", i)
            P.wait("scalar", "sTsc", 8 * i + g + 1)
            src = PS[:, 1024 * (g % 2):1024 * (g % 2) + 1024]
            P.op("scalar", lambda E, src=src, g=g: E.activation(
                PT[:, 1024 * g:1024 * (g + 1)], src, AF.Exp,
                bias=NEGB[:, :], scale=INV_SQRT_HD), inc="sSexp")
        P.wait("scalar", "sTpv", 8 * (i + 1))
        P.op("scalar", lambda E, i=i: E.copy(ATT[:, 256 * i:256 * (i + 1)],
                                             PS[:, 2048:2304]), inc="sSatt")
        P.wait("scalar", "sTden", i + 1)
        P.op("scalar", lambda E: E.copy(DEN[0:1, :], PS[0:1, 2560:2816]),
             inc="sSden")
        P.wait("vector", "sTrd", i + 1)
        P.op("vector", lambda E, i=i: E.reciprocal(RDEN[:, 2 * i:2 * i + 2],
                                                   PS[:, 3072:3074]), inc="sVrd")
        P.wait("vector", "sVrd", P.c("sVrd"))
        for u in range(4 * i, 4 * i + 4):
            passB_unit(u)

    DQ_ALL = 16 * 32
    P.wait("sync", "sTden", QI)
    P.wait("sync", "sDQ0", 16 * 16)
    P.wait("sync", "sDQ1", 16 * 16)
    P.wait("sync", "sWBdone", NWB)
    P.wait("tensor", "sVrd", QI)
    P.wait("tensor", "sSatt", QI)
    P.wait("tensor", "sWBdone", NWB)
    P.wait("vector", "sDQ0", 16 * 16)
    P.wait("vector", "sDQ1", 16 * 16)
    P.wait("scalar", "sVrd", QI)

    P.wait("gpsimd", "sDQ0", 16 * 16)
    P.wait("gpsimd", "sDQ1", 16 * 16)
    P.wait("gpsimd", "sTden", QI)
    for t in range(NT + 2):
        if t < NT:
            if t >= 3:
                P.wait("sync", "sOut%d" % ((t - 3) % 2), 16 * ((t - 3) // 2 + 1))
            P.op("sync", lambda E, t=t: E.dma_start(
                out=YO[:, 1024 * (t % 3):1024 * (t % 3 + 1)],
                in_=x_d[128 * t:128 * (t + 1), :]), inc="sX2ld", n=16)
        if t >= 2:
            to = t - 2
            P.wait("sync", "sVout", 2 * to + 2)
            P.op("sync", lambda E, to=to: E.dma_start(
                out=out_d[128 * to:128 * (to + 1), :],
                in_=YO[:, 1024 * (to % 3):1024 * (to % 3 + 1)]),
                inc="sOut%d" % (to % 2), n=16)
        if 1 <= t <= NT:
            td = t - 1
            for c in range(32):
                P.wait("gpsimd", "sTdn", 32 * td + c - 7)
                P.op("gpsimd", lambda E, c=c: E.dma_start(
                    out=WD[:, 1024 * (c % 8):1024 * (c % 8 + 1)],
                    in_=wdq_d[128 * c:128 * (c + 1), :]),
                    inc="sWDl%d" % (c % 8), n=16)
    P.wait("sync", "sOut0", 16 * (NT // 2))
    P.wait("sync", "sOut1", 16 * (NT // 2))
    if dbg:
        dbg_specs = [
            ("NEGB", [128, 1], F32), ("RDEN", [128, NT], F32),
            ("MEAN", [128, NT], F32), ("RS", [128, NT], F32),
            ("SWG", [128, 1], F32), ("SWU", [128, 1], F32), ("SWD", [128, 1], F32),
            ("MWG127n", [128, 1], F32), ("MWD127n", [128, 1], F32),
            ("CT", [128, 8], F32), ("DEN", [1, 256], F32),
            ("ATT", [128, S], BF16), ("KF", [128, S], BF16),
            ("QF", [128, S], BF16), ("VT", [128, S], BF16),
            ("WG", [128, 8 * INNER], BF16),
            ("HQT", [128, 4096], BF16), ("ATX", [128, 1024], BF16),
            ("H", [128, 4096], F32), ("XN2", [128, 1024], F32),
            ("YO", [128, 2048], F32), ("SC2", [1, 8], F32),
        ]
        for nm, shp, dt in dbg_specs:
            od = nc.dram_tensor("dbg_" + nm, shp, dt, kind="ExternalOutput").ap()
            P.op("sync", lambda E, od=od, nm=nm: E.dma_start(out=od, in_=A[nm][:, :]),
                 inc="sOut0", n=16)
        P.wait("sync", "sOut0", 16 * (NT // 2) + 16 * len(dbg_specs))

    def b2_front_tensor(t):
        s = t % 2
        P.wait("tensor", "sVout", 2 * (t - 1))
        if t >= 1:
            P.wait("tensor", "sVh", 8 * (t - 1) + 6)
            P.wait("tensor", "sSgl", 8 * (t - 1) + 6)
        for b in range(2):
            P.wait("tensor", "sVaq", 2 * t + b + 1)
            if b == 1:
                P.wait("tensor", "sSat2", 2 * t + 1)
            for k in range(4):
                d = 4 * b + k
                P.op("tensor", lambda E, d=d, k=k: E.transpose(
                    PS[:, 3072 + 128 * k:3072 + 128 * (k + 1)],
                    XN2[:, 128 * d:128 * (d + 1)], IDF[:, :]),
                    inc=("sTat2" if k == 3 else None))
        if t >= 1:
            P.wait("tensor", "sSat2", 2 * t + 2)
            for b in range(8):
                P.wait("tensor", "sVq", 8 * (t - 1) + b + 1)
                if b >= 1:
                    P.wait("tensor", "sShq", 8 * (t - 1) + b)
                for k in range(4):
                    c = 4 * b + k
                    P.op("tensor", lambda E, c=c, k=k: E.transpose(
                        PS[:, 3072 + 128 * k:3072 + 128 * (k + 1)],
                        H[:, 128 * c:128 * (c + 1)], IDF[:, :]),
                        inc=("sThq" if k == 3 else None))
        P.wait("tensor", "sSat2", 2 * (t + 1))
        for e in range(8):
            se = (e + 2) % 4
            P.wait("tensor", "sVh", 8 * t + e - 3)
            P.wait("tensor", "sSgl", 8 * t + e - 3)
            gcol, ucol = 1024 * se, 1024 * se + 512
            wcol = 2048 * (e // 4) + 512 * (e % 4)
            for d in range(8):
                st, sp = (d == 0), (d == 7)
                P.op("tensor", lambda E, d=d, st=st, sp=sp, gcol=gcol, wcol=wcol:
                     E.matmul(PS[:, gcol:gcol + 512], ATX[:, 128 * d:128 * (d + 1)],
                              WG[:, 4096 * d + wcol:4096 * d + wcol + 512],
                              start=st, stop=sp))
                P.op("tensor", lambda E, d=d, st=st, sp=sp, ucol=ucol, wcol=wcol:
                     E.matmul(PS[:, ucol:ucol + 512], ATX[:, 128 * d:128 * (d + 1)],
                              WU[:, 4096 * d + wcol:4096 * d + wcol + 512],
                              start=st, stop=sp), inc=("sTgu" if sp else None))

    def b2_last_hqt():
        t = NT
        P.wait("tensor", "sSat2", 2 * NT)
        P.wait("tensor", "sVout", 2 * NT - 2)
        for b in range(8):
            P.wait("tensor", "sVq", 8 * (t - 1) + b + 1)
            if b >= 1:
                P.wait("tensor", "sShq", 8 * (t - 1) + b)
            for k in range(4):
                c = 4 * b + k
                P.op("tensor", lambda E, c=c, k=k: E.transpose(
                    PS[:, 3072 + 128 * k:3072 + 128 * (k + 1)],
                    H[:, 128 * c:128 * (c + 1)], IDF[:, :]),
                    inc=("sThq" if k == 3 else None))

    def b2_tail_tensor(t):
        s = t % 2
        P.wait("tensor", "sVh", min(8 * (t + 1) + 6, 8 * NT))
        P.wait("tensor", "sSgl", min(8 * (t + 1) + 6, 8 * NT))
        P.wait("tensor", "sVout", 2 * t - 1)
        P.op("tensor", lambda E, t=t: E.matmul(
            PS[:, 2048:2560], ATT[:, 128 * t:128 * (t + 1)], WO[:, 0:512],
            start=True, stop=True))
        P.op("tensor", lambda E, t=t: E.matmul(
            PS[:, 2560:3072], ATT[:, 128 * t:128 * (t + 1)], WO[:, 512:1024],
            start=True, stop=True), inc="sTpj")
        P.wait("tensor", "sShq", 8 * (t + 1))
        P.wait("tensor", "sVout", 2 * t)
        for c in range(32):
            P.wait("tensor", "sWDl%d" % (c % 8), 16 * (4 * t + c // 8 + 1))
            st, sp = (c == 0), (c == 31)
            P.op("tensor", lambda E, c=c, st=st, sp=sp: E.matmul(
                PS[:, 3072:3584], HQT[:, 128 * c:128 * (c + 1)],
                WD[:, 1024 * (c % 8):1024 * (c % 8) + 512], start=st, stop=sp))
            P.op("tensor", lambda E, c=c, st=st, sp=sp: E.matmul(
                PS[:, 3584:4096], HQT[:, 128 * c:128 * (c + 1)],
                WD[:, 1024 * (c % 8) + 512:1024 * (c % 8 + 1)], start=st, stop=sp),
                inc="sTdn")

    qs_cnt = {"s": 0, "v": 0}

    def quant_scalar_sub(z_slice, scale_ap, wait_sem, wait_val):
        g = qs_cnt["s"]; qs_cnt["s"] += 1
        sl = 512 * (g % 2)
        P.wait("scalar", wait_sem, wait_val)
        P.wait("scalar", "sVqd", g - 1)
        P.op("scalar", lambda E, z_slice=z_slice, scale_ap=scale_ap, sl=sl:
             E.activation(QS1[:, sl:sl + 512], z_slice, AF.Identity, scale=scale_ap))
        P.op("scalar", lambda E, z_slice=z_slice, scale_ap=scale_ap, sl=sl:
             E.activation(QS2[:, sl:sl + 512], z_slice, AF.Identity, scale=scale_ap),
             inc="sSqp")

    def quant_vector_sub(z_slice, inc_last=None):
        g = qs_cnt["v"]; qs_cnt["v"] += 1
        sl = 512 * (g % 2)
        P.wait("vector", "sSqp", g + 1)
        P.op("vector", lambda E, sl=sl: E.scalar_tensor_tensor(
            QS1[:, sl:sl + 512], QS2[:, sl:sl + 512], -1.0, QS1[:, sl:sl + 512],
            ALU.mult, ALU.add))
        P.op("vector", lambda E, z_slice=z_slice, sl=sl: E.scalar_tensor_tensor(
            z_slice, QS1[:, sl:sl + 512], 0.5, QS2[:, sl:sl + 512],
            ALU.is_ge, ALU.add))
        P.op("vector", lambda E, z_slice=z_slice, sl=sl: E.scalar_tensor_tensor(
            z_slice, QS1[:, sl:sl + 512], -0.5, z_slice,
            ALU.is_lt, ALU.subtract), inc=inc_last)
        P.op("vector", lambda E: E.tensor_copy(SC1[:, 5:6], SC1[:, 5:6]), inc="sVqd")

    def b2_prep_vector(t):
        s3 = t % 3
        P.wait("vector", "sX2ld", 16 * (t + 1))
        P.wait("vector", "sTat2", 2 * t)
        P.op("vector", lambda E, t=t, s3=s3: E.tensor_scalar(
            XN2[:, :], YO[:, 1024 * s3:1024 * (s3 + 1)],
            MEAN[:, t:t + 1], RS[:, t:t + 1], ALU.subtract, ALU.mult))
        P.opf("vector", lambda E: E.reduce_max(SC1[:, 0:1], XN2[:, :], axis=AX.X,
                                               apply_absolute_value=True))
        P.opf("vector", lambda E: E.tensor_scalar(SC1[:, 0:1], SC1[:, 0:1],
                                                  1e-5, None, ALU.max))
        P.opf("vector", lambda E: E.reciprocal(SC1[:, 1:2], SC1[:, 0:1]))
        P.opf("vector", lambda E, s3=s3: E.tensor_scalar(
            CT[:, 4 * s3:4 * s3 + 1], SC1[:, 1:2], 127.0, None, ALU.mult))
        P.opf("vector", lambda E, s3=s3: E.tensor_tensor(
            CT[:, 4 * s3 + 1:4 * s3 + 2], SC1[:, 0:1], A["MWG127n"][:, :], ALU.mult))
        P.opf("vector", lambda E, s3=s3: E.tensor_tensor(
            CT[:, 4 * s3 + 2:4 * s3 + 3], SC1[:, 0:1], A["MWU127n"][:, :], ALU.mult))
        P.op("vector", lambda E: E.tensor_copy(SC1[:, 6:7], SC1[:, 6:7]), inc="sVln")
        for k in range(2):
            quant_vector_sub(XN2[:, 512 * k:512 * (k + 1)], inc_last="sVaq")

    def b2_eighths_vector(t):
        s3 = t % 3
        for e in range(8):
            se = (e + 2) % 4
            P.wait("vector", "sSgl", 8 * t + e + 1)
            ucol = 1024 * se + 512
            P.op("vector", lambda E, e=e, ucol=ucol, s3=s3: E.scalar_tensor_tensor(
                H[:, 512 * e:512 * (e + 1)], PS[:, ucol:ucol + 512],
                CT[:, 4 * s3 + 2:4 * s3 + 3], H[:, 512 * e:512 * (e + 1)],
                ALU.mult, ALU.mult), inc="sVh")
            P.opf("vector", lambda E, e=e: E.reduce_max(
                SC1[:, 4:5], H[:, 512 * e:512 * (e + 1)], axis=AX.X,
                apply_absolute_value=True))
            if e == 0:
                P.opf("vector", lambda E: E.tensor_copy(SC1[:, 2:3], SC1[:, 4:5]))
            else:
                P.opf("vector", lambda E: E.tensor_tensor(
                    SC1[:, 2:3], SC1[:, 2:3], SC1[:, 4:5], ALU.max))

    def b2_quant_vector_scales(t):
        s3 = t % 3
        P.opf("vector", lambda E: E.tensor_scalar(SC1[:, 2:3], SC1[:, 2:3],
                                                  1e-5, None, ALU.max))
        P.opf("vector", lambda E: E.reciprocal(SC1[:, 3:4], SC1[:, 2:3]))
        P.opf("vector", lambda E: E.tensor_scalar(SC1[:, 3:4], SC1[:, 3:4],
                                                  127.0, None, ALU.mult))
        P.opf("vector", lambda E, s3=s3: E.tensor_tensor(
            CT[:, 4 * s3 + 3:4 * s3 + 4], SC1[:, 2:3], A["MWD127n"][:, :], ALU.mult))
        P.op("vector", lambda E: E.tensor_copy(SC1[:, 7:8], SC1[:, 7:8]), inc="sVsh")

    def b2_quant_vector_fixups(t):
        P.wait("vector", "sThq", 8 * t)
        for k in range(8):
            quant_vector_sub(H[:, 512 * k:512 * (k + 1)], inc_last="sVq")

    def b2_tail_vector(t):
        s3 = t % 3
        P.wait("vector", "sTpj", t + 1)
        P.op("vector", lambda E, t=t, s3=s3: E.scalar_tensor_tensor(
            YO[:, 1024 * s3:1024 * (s3 + 1)], PS[:, 2048:3072],
            RDEN[:, t:t + 1], YO[:, 1024 * s3:1024 * (s3 + 1)],
            ALU.mult, ALU.add), inc="sVout")
        P.wait("vector", "sTdn", 32 * (t + 1))
        P.op("vector", lambda E, t=t, s3=s3: E.scalar_tensor_tensor(
            YO[:, 1024 * s3:1024 * (s3 + 1)], PS[:, 3072:4096],
            CT[:, 4 * s3 + 3:4 * s3 + 4], YO[:, 1024 * s3:1024 * (s3 + 1)],
            ALU.mult, ALU.add), inc="sVout")

    def b2_scalar(t):
        s = t % 3
        P.wait("scalar", "sTgu", 8 * t)
        for b in range(2):
            P.wait("scalar", "sTat2", 2 * t + b + 1)
            P.op("scalar", lambda E, b=b: E.copy(
                ATX[:, 512 * b:512 * (b + 1)], PS[:, 3072:3584]), inc="sSat2")
        if t >= 1:
            P.wait("scalar", "sTdn", 32 * (t - 1))
            for b in range(8):
                P.wait("scalar", "sThq", 8 * (t - 1) + b + 1)
                P.op("scalar", lambda E, b=b: E.copy(
                    HQT[:, 512 * b:512 * (b + 1)], PS[:, 3072:3584]), inc="sShq")
        P.wait("scalar", "sVq", 8 * t)
        for e in range(8):
            se = (e + 2) % 4
            P.wait("scalar", "sTgu", 8 * t + e + 1)
            gcol = 1024 * se
            P.op("scalar", lambda E, e=e, gcol=gcol, s=s: E.activation(
                H[:, 512 * e:512 * (e + 1)], PS[:, gcol:gcol + 512], AF.Silu,
                scale=CT[:, 4 * s + 1:4 * s + 2]), inc="sSgl")
        if t + 1 < NT:
            s3n = (t + 1) % 3
            for k in range(2):
                quant_scalar_sub(XN2[:, 512 * k:512 * (k + 1)],
                                 CT[:, 4 * s3n:4 * s3n + 1], "sVln", t + 2)
        for k in range(8):
            quant_scalar_sub(H[:, 512 * k:512 * (k + 1)], SC1[:, 3:4],
                             "sVsh", t + 1)

    def b2_last_scalar_hq():
        t = NT
        P.wait("scalar", "sTdn", 32 * (t - 1))
        for b in range(8):
            P.wait("scalar", "sThq", 8 * (t - 1) + b + 1)
            P.op("scalar", lambda E, b=b: E.copy(
                HQT[:, 512 * b:512 * (b + 1)], PS[:, 3072:3584]), inc="sShq")

    P.op("vector", lambda E: E.memset(SC1[:, 4:8], 0.0))
    for k in range(2):
        quant_scalar_sub(XN2[:, 512 * k:512 * (k + 1)], CT[:, 0:1], "sVln", 1)
    b2_prep_vector(0)
    for t in range(NT + 1):
        if t < NT:
            b2_front_tensor(t)
            b2_scalar(t)
            b2_eighths_vector(t)
            b2_quant_vector_scales(t)
            if t + 1 < NT:
                b2_prep_vector(t + 1)
            b2_quant_vector_fixups(t)
            if t >= 1:
                b2_tail_vector(t - 1)
        else:
            b2_last_hqt()
            b2_last_scalar_hq()
            b2_tail_vector(NT - 1)
        if t >= 1:
            b2_tail_tensor(t - 1)

    with nc.Block() as block:
        @block.sync
        def _(E):
            for fn in P.streams["sync"]:
                fn(E)

        @block.tensor
        def _(E):
            for fn in P.streams["tensor"]:
                fn(E)

        @block.vector
        def _(E):
            for fn in P.streams["vector"]:
                fn(E)

        @block.scalar
        def _(E):
            for fn in P.streams["scalar"]:
                fn(E)

        @block.gpsimd
        def _(E):
            for fn in P.streams["gpsimd"]:
                fn(E)

    return nc


_NC_CACHE = None


def _get_nc():
    global _NC_CACHE
    if _NC_CACHE is None:
        _NC_CACHE = build_nc()
    return _NC_CACHE


def _host_constants():
    inv_freq = (1.0 / (np.float32(10000.0) **
                       (np.arange(0, HDIM, 2, dtype=np.float32) / np.float32(HDIM))))
    inv_freq = inv_freq.astype(np.float32)
    tpos = np.arange(S - 1, dtype=np.float32)
    ang = (tpos[:, None] * inv_freq[None, :]).astype(np.float32)
    cos = np.cos(ang).astype(np.float32)
    sin = np.sin(ang).astype(np.float32)
    trig = np.zeros((S, 128), np.float32)
    trig[0, 0:64] = 1.0
    trig[1:, 0:64] = cos
    trig[1:, 64:128] = sin
    idb = np.eye(128, dtype=BF16_NP)
    idf = np.eye(128, dtype=np.float32)
    onesb = np.ones((128, 1), BF16_NP)
    onesf = np.ones((128, 1), np.float32)
    onesrow = np.ones((1, 128), np.float32)
    return trig, idb, idf, onesb, onesf, onesrow


def kernel(L0_attn_x, L0_attn_wq, L0_attn_wk, L0_attn_wv, L0_attn_wo,
           L0_ffn_w_gate, L0_ffn_w_up, L0_ffn_w_down):
    x = np.ascontiguousarray(np.asarray(L0_attn_x, dtype=np.float32))
    wq = np.asarray(L0_attn_wq, np.float32)
    wk = np.asarray(L0_attn_wk, np.float32)
    wv = np.asarray(L0_attn_wv, np.float32)
    wo = np.asarray(L0_attn_wo, np.float32)
    qkvT = np.ascontiguousarray(
        np.concatenate([wq.T, wk.T, wv.T], axis=1)).astype(np.float32)
    woT = np.ascontiguousarray(wo.T)
    wgT = np.ascontiguousarray(np.asarray(L0_ffn_w_gate, np.float32).T)
    wuT = np.ascontiguousarray(np.asarray(L0_ffn_w_up, np.float32).T)
    wdT = np.ascontiguousarray(np.asarray(L0_ffn_w_down, np.float32).T)
    trig, idb, idf, onesb, onesf, onesrow = _host_constants()

    nc = _get_nc()
    in_maps = []
    for b in range(B):
        in_maps.append({
            "x": np.ascontiguousarray(x[b]),
            "qkvT": qkvT, "woT": woT,
            "wgT": wgT, "wuT": wuT, "wdT": wdT,
            "trig": trig, "idb": idb, "idf": idf,
            "onesb": onesb, "onesf": onesf, "onesrow": onesrow,
        })
    res = run_bass_kernel_spmd(nc, in_maps, core_ids=list(range(B)))
    out = np.stack([np.asarray(res.results[b]["out"], dtype=np.float32)
                    for b in range(B)], axis=0)
    return out


# version 21
# speedup vs baseline: 1.1959x; 1.0299x over previous
import numpy as np
import ml_dtypes

import concourse.bass as bass
import concourse.mybir as mybir
from concourse.bass_utils import run_bass_kernel_spmd

F32 = mybir.dt.float32
BF16 = mybir.dt.bfloat16
I32 = mybir.dt.int32
I8 = mybir.dt.int8
AF = mybir.ActivationFunctionType
ALU = mybir.AluOpType
AX = mybir.AxisListType
BF16_NP = ml_dtypes.bfloat16

B, S, D, HDIM, INNER = 8, 4096, 1024, 128, 4096
NT = S // 128
QI = S // 256
NWA = 48
NWB = 64
INV_SQRT_HD = 0.08838834764831845
N_ELEM_W = float(INNER * D)

ENGINES = ("sync", "tensor", "vector", "scalar", "gpsimd")


class Prog:
    def __init__(self):
        self.streams = {e: [] for e in ENGINES}
        self.counts = {}
        self._h = None

    def sem(self, name):
        self.counts[name] = 0

    def emit(self, eng, fn):
        self.streams[eng].append(fn)

    def wait(self, eng, sem_name, val):
        if val <= 0:
            return
        self.emit(eng, lambda E, s=sem_name, v=val: E.wait_ge(self._h[s], v))

    def op(self, eng, builder, inc=None, n=1):
        if inc is None:
            self.emit(eng, builder)
        else:
            self.counts[inc] += n

            def fn(E, b=builder, s=inc, k=n):
                b(E).then_inc(self._h[s], k)
            self.emit(eng, fn)

    def c(self, name):
        return self.counts[name]

    def opf(self, eng, builder):
        s = "chain_" + eng
        self.op(eng, builder, inc=s)
        self.wait(eng, s, self.counts[s])


def build_nc(dbg=False):
    nc = bass.Bass("TRN2", target_bir_lowering=False,
                   detect_race_conditions=False)

    x_d = nc.dram_tensor("x", [S, D], F32, kind="ExternalInput").ap()
    qkvT_d = nc.dram_tensor("qkvT", [D, 384], F32, kind="ExternalInput").ap()
    woT_d = nc.dram_tensor("woT", [HDIM, D], F32, kind="ExternalInput").ap()
    wgT_d = nc.dram_tensor("wgT", [D, INNER], F32, kind="ExternalInput").ap()
    wuT_d = nc.dram_tensor("wuT", [D, INNER], F32, kind="ExternalInput").ap()
    wdT_d = nc.dram_tensor("wdT", [INNER, D], F32, kind="ExternalInput").ap()
    trig_d = nc.dram_tensor("trig", [S, 128], F32, kind="ExternalInput").ap()
    idb_d = nc.dram_tensor("idb", [128, 128], BF16, kind="ExternalInput").ap()
    idf_d = nc.dram_tensor("idf", [128, 128], F32, kind="ExternalInput").ap()
    onesb_d = nc.dram_tensor("onesb", [128, 1], BF16, kind="ExternalInput").ap()
    onesf_d = nc.dram_tensor("onesf", [128, 1], F32, kind="ExternalInput").ap()
    onesrow_d = nc.dram_tensor("onesrow", [1, 128], F32, kind="ExternalInput").ap()
    out_d = nc.dram_tensor("out", [S, D], F32, kind="ExternalOutput").ap()
    wdq_d = nc.dram_tensor("wdq_scratch", [INNER, D], I8, kind="Internal").ap()

    BASE = 16512
    TOP = 229376
    A = {}
    def sb(name, shape, dtype, off):
        assert off % 32 == 0, (name, off)
        A[name] = nc.alloc_sbuf_tensor_at(name, shape, dtype, offset=off)
        return A[name]

    o = BASE
    sb("WG", [128, 8 * INNER], BF16, o); o += 65536
    sb("WU", [128, 8 * INNER], BF16, o); o += 65536
    sb("ATT", [128, S], BF16, o); o += 8192
    sb("WO", [128, D], BF16, o); o += 2048
    for nm, shp, dt, szb in [
        ("MEAN", [128, NT], F32, 128), ("RS", [128, NT], F32, 128),
        ("RDEN", [128, NT], F32, 128),
        ("IDB", [128, 128], BF16, 256), ("IDF", [128, 128], F32, 512),
        ("ONESB", [128, 1], BF16, 32), ("ONESF", [128, 1], F32, 32),
        ("ONESROW", [1, 128], F32, 512),
        ("NEGB", [128, 1], F32, 32),
        ("SWG", [128, 1], F32, 32), ("SWU", [128, 1], F32, 32), ("SWD", [128, 1], F32, 32),
        ("MWG127n", [128, 1], F32, 32), ("MWU127n", [128, 1], F32, 32),
        ("MWD127n", [128, 1], F32, 32),
        ("ACCG", [128, 1], F32, 32), ("ACCU", [128, 1], F32, 32), ("ACCD", [128, 1], F32, 32),
        ("PAP", [128, 2], F32, 32),
        ("SC1", [128, 8], F32, 32),
        ("SC2", [1, 8], F32, 32),
        ("DEN", [1, 256], F32, 1024),
    ]:
        sb(nm, shp, dt, o); o += szb
    PERS_END = o
    OVA = (o + 31) // 32 * 32
    sb("KF", [128, S], BF16, OVA)
    sb("QF", [128, S], BF16, OVA + 8192)
    sb("VT", [128, S], BF16, OVA + 16384)
    OVB = OVA + 24576
    sb("QKVW", [128, 3072], BF16, OVB)
    sb("XB", [128, 2048], F32, OVB + 6144)
    sb("XN", [128, 1024], F32, OVB + 14336)
    sb("XNT", [128, 1024], BF16, OVB + 18432)
    sb("RT", [128, 384], F32, OVB + 20480)
    sb("TRG", [128, 256], F32, OVB + 22016)
    sb("WLD", [128, 4096], F32, OVB + 23040)
    sb("SQS", [128, 256], BF16, OVB + 39424)
    assert OVB + 39936 <= TOP
    sb("PT", [128, 8192], BF16, OVB)
    sb("WLB", [128, 4096], F32, OVB + 16384)
    sb("WDQA0", [128, 1024], I8, OVB + 16384)
    sb("WDQA1", [128, 1024], I8, OVB + 24576)
    sb("TL", [128, 1024], F32, OVB + 32768)
    sb("TG", [128, 1024], F32, OVB + 36864)
    assert OVB + 40960 <= TOP
    o = OVA
    sb("H", [128, 4096], F32, o); o += 16384
    sb("HQT", [128, 4096], BF16, o); o += 8192
    sb("WD", [128, 8192], BF16, o); o += 16384
    sb("XN2", [128, 1024], F32, o); o += 4096
    sb("ATX", [128, 1024], BF16, o); o += 2048
    sb("QS1", [128, 1024], F32, o); o += 4096
    sb("QS2", [128, 1024], I32, o); o += 4096
    sb("YO", [128, 3072], F32, o); o += 12288
    sb("CT", [128, 12], F32, o); o += 64
    assert o <= TOP, o

    _psg = nc.psum_tensor("PS", [128, 4096], F32)
    PS = _psg.__enter__()

    P = Prog()
    sem_names = [
        "sCstG", "sCstS", "sXld0", "sXld1", "sTRG0", "sTRG1", "sWA0", "sWA1",
        "sSabs", "sVacc",
        "sSsq", "sVrec", "sSrs0", "sVxn",
        "sTtr", "sSxnt", "sTqkv", "sVrope", "sTqk2", "sSqk", "sSv",
        "sVnorm", "sTnorm", "sVbp", "sSbp", "sTbp",
        "sTsc", "sSexp", "sTpv", "sTden", "sSatt", "sSden", "sTrd", "sVrd",
        "sWB0", "sWB1", "sWBdone", "sDQ0", "sDQ1",
        "sX2ld", "sWDl0", "sWDl1", "sWDl2", "sWDl3", "sWDl4", "sWDl5", "sWDl6", "sWDl7",
        "sVaq", "sTat2", "sSat2", "sTgu", "sSgl", "sVh", "sVq",
        "sThq", "sShq", "sTpj", "sTdn", "sVout", "sOut0", "sOut1",
        "chain_vector", "chain_scalar", "chain_tensor", "chain_sync", "chain_gpsimd",
        "sVln", "sVsh", "sSqp", "sVqd",
    ]
    sems = {}
    for nm in sem_names:
        sems[nm] = nc.semaphore(nm).__enter__()
        P.sem(nm)
    P._h = sems

    KF, QF, VT = A["KF"], A["QF"], A["VT"]
    ATT, WO, QKVW = A["ATT"], A["WO"], A["QKVW"]
    XB, XN, XNT, RT, TRG, WLD, SQS = (A["XB"], A["XN"], A["XNT"], A["RT"],
                                      A["TRG"], A["WLD"], A["SQS"])
    MEAN, RS, RDEN = A["MEAN"], A["RS"], A["RDEN"]
    IDB, IDF = A["IDB"], A["IDF"]
    ONESB, ONESF, ONESROW = A["ONESB"], A["ONESF"], A["ONESROW"]
    NEGB, SC1, SC2, DEN, PAP = A["NEGB"], A["SC1"], A["SC2"], A["DEN"], A["PAP"]
    PT, WLB, TL, TG = A["PT"], A["WLB"], A["TL"], A["TG"]
    WDQA = (A["WDQA0"], A["WDQA1"])
    WG, WU = A["WG"], A["WU"]
    H, HQT, WD = A["H"], A["HQT"], A["WD"]
    XN2, ATX = A["XN2"], A["ATX"]
    QS1, QS2, YO, CT = A["QS1"], A["QS2"], A["YO"], A["CT"]

    for d in range(8):
        P.op("gpsimd", lambda E, d=d: E.dma_start(
            out=QKVW[:, 384 * d:384 * (d + 1)],
            in_=qkvT_d[128 * d:128 * (d + 1), :]), inc="sCstG", n=16)
    P.op("gpsimd", lambda E: E.dma_start(out=WO[:, :], in_=woT_d), inc="sCstG", n=16)
    for dst, src in [(IDB, idb_d), (IDF, idf_d), (ONESB, onesb_d),
                     (ONESF, onesf_d), (ONESROW, onesrow_d)]:
        P.op("sync", lambda E, d=dst, s=src: E.dma_start(out=d[:, :], in_=s),
             inc="sCstS", n=16)

    def p1_sync_tile(t):
        P.wait("sync", "sVxn", t - 1)
        P.op("sync", lambda E, t=t: E.dma_start(
            out=XB[:, 1024 * (t % 2):1024 * (t % 2 + 1)],
            in_=x_d[128 * t:128 * (t + 1), :]), inc="sXld%d" % (t % 2), n=16)
        P.wait("sync", "sVrope", t - 1)
        P.op("sync", lambda E, t=t: E.dma_start(
            out=TRG[:, 128 * (t % 2):128 * (t % 2 + 1)],
            in_=trig_d[128 * t:128 * (t + 1), :]), inc="sTRG%d" % (t % 2), n=16)

    def wa_src(j):
        m, jj = j // 16, j % 16
        rb, h = jj // 2, jj % 2
        if m == 0:
            return wgT_d[128 * rb:128 * (rb + 1), 2048 * h:2048 * (h + 1)]
        if m == 1:
            return wuT_d[128 * rb:128 * (rb + 1), 2048 * h:2048 * (h + 1)]
        return wdT_d.rearrange("(c p two) n -> c p (two n)", two=2, p=128)[jj]

    def p1_sync_passA(j):
        P.wait("sync", "sSabs", j - 1)
        P.op("sync", lambda E, j=j: E.dma_start(
            out=WLD[:, 2048 * (j % 2):2048 * (j % 2 + 1)], in_=wa_src(j)),
            inc="sWA%d" % (j % 2), n=16)

    def p1_vector(t):
        sl = XB[:, 1024 * (t % 2):1024 * (t % 2 + 1)]
        P.wait("vector", "sXld%d" % (t % 2), 16 * (t // 2 + 1))
        P.opf("vector", lambda E, sl=sl: E.reduce_sum(SC1[:, 0:1], sl, axis=AX.X))
        P.opf("vector", lambda E: E.tensor_scalar(SC1[:, 0:1], SC1[:, 0:1],
                                                  1.0 / 1024.0, None, ALU.mult))
        P.wait("vector", "sSsq", t + 1)
        P.opf("vector", lambda E: E.tensor_scalar(SC1[:, 1:2], SC1[:, 2:3],
                                                  1.0 / 1024.0, None, ALU.mult))
        P.opf("vector", lambda E: E.tensor_tensor(SC1[:, 4:5], SC1[:, 0:1],
                                                  SC1[:, 0:1], ALU.mult))
        P.opf("vector", lambda E: E.tensor_sub(SC1[:, 1:2], SC1[:, 1:2], SC1[:, 4:5]))
        P.opf("vector", lambda E: E.tensor_scalar(SC1[:, 1:2], SC1[:, 1:2],
                                                  1e-5, None, ALU.add))
        P.op("vector", lambda E: E.reciprocal(SC1[:, 2:3], SC1[:, 1:2]), inc="sVrec")
        P.opf("vector", lambda E, t=t: E.tensor_copy(MEAN[:, t:t + 1], SC1[:, 0:1]))
        P.wait("vector", "sSrs0", t + 1)
        P.opf("vector", lambda E: E.tensor_tensor(SC1[:, 4:5], SC1[:, 3:4],
                                                  SC1[:, 3:4], ALU.mult))
        P.opf("vector", lambda E: E.tensor_tensor(SC1[:, 4:5], SC1[:, 4:5],
                                                  SC1[:, 1:2], ALU.mult))
        P.opf("vector", lambda E: E.tensor_scalar(SC1[:, 4:5], SC1[:, 4:5],
                                                  -0.5, 1.5, ALU.mult, ALU.add))
        P.opf("vector", lambda E, t=t: E.tensor_tensor(RS[:, t:t + 1], SC1[:, 3:4],
                                                       SC1[:, 4:5], ALU.mult))
        P.op("vector", lambda E, sl=sl, t=t: E.tensor_scalar(
            XN[:, :], sl, MEAN[:, t:t + 1], RS[:, t:t + 1],
            ALU.subtract, ALU.mult), inc="sVxn")

    def p1_rope(t):
        P.wait("vector", "sTqkv", t + 1)
        P.wait("vector", "sTRG%d" % (t % 2), 16 * (t // 2 + 1))
        cosv = TRG[:, 128 * (t % 2):128 * (t % 2) + 64]
        sinv = TRG[:, 128 * (t % 2) + 64:128 * (t % 2) + 128]
        for base, off in ((0, 0), (512, 128)):
            ev = PS[:, base:base + 127:2]
            od = PS[:, base + 1:base + 128:2]
            dst = RT[:, off:off + 128]
            scr = RT[:, 256:320]
            last = (base == 512)
            P.opf("vector", lambda E, ev=ev, cosv=cosv, dst=dst: E.tensor_tensor(
                dst[:, 0:127:2], ev, cosv, ALU.mult))
            P.opf("vector", lambda E, od=od, sinv=sinv, scr=scr: E.tensor_tensor(
                scr, od, sinv, ALU.mult))
            P.op("vector", lambda E, dst=dst, scr=scr: E.tensor_sub(
                dst[:, 0:127:2], dst[:, 0:127:2], scr))
            P.opf("vector", lambda E, ev=ev, sinv=sinv, dst=dst: E.tensor_tensor(
                dst[:, 1:128:2], ev, sinv, ALU.mult))
            P.opf("vector", lambda E, od=od, cosv=cosv, scr=scr: E.tensor_tensor(
                scr, od, cosv, ALU.mult))
            P.op("vector", lambda E, dst=dst, scr=scr: E.tensor_add(
                dst[:, 1:128:2], dst[:, 1:128:2], scr),
                inc=("sVrope" if last else None))

    def p1_scalar(t):
        sl = XB[:, 1024 * (t % 2):1024 * (t % 2 + 1)]
        P.wait("scalar", "sXld%d" % (t % 2), 16 * (t // 2 + 1))
        P.wait("scalar", "sVrec", t)
        P.wait("scalar", "sTtr", t)
        P.op("scalar", lambda E, sl=sl: E.activation(
            XN[:, :], sl, AF.Square, accum_out=SC1[:, 2:3]), inc="sSsq")
        P.wait("scalar", "sVrec", t + 1)
        P.wait("scalar", "sVxn", t)
        P.op("scalar", lambda E: E.activation(SC1[:, 3:4], SC1[:, 2:3], AF.Sqrt),
             inc="sSrs0")
        P.wait("scalar", "sTtr", t + 1)
        P.op("scalar", lambda E: E.copy(XNT[:, 0:512], PS[:, 3072:3584]))
        P.op("scalar", lambda E: E.copy(XNT[:, 512:1024], PS[:, 3584:4096]),
             inc="sSxnt")
        P.wait("scalar", "sTqkv", t + 1)
        P.op("scalar", lambda E, t=t: E.copy(VT[:, 128 * t:128 * (t + 1)],
                                             PS[:, 1024:1152]), inc="sSv")
        if t >= 1:
            P.wait("scalar", "sTqk2", t)
            P.op("scalar", lambda E, t=t: E.copy(
                QF[:, 128 * (t - 1):128 * t], PS[:, 1536:1664]))
            P.op("scalar", lambda E, t=t: E.copy(
                KF[:, 128 * (t - 1):128 * t], PS[:, 1664:1792]), inc="sSqk")

    def passA_chunk(j):
        P.wait("scalar", "sWA%d" % (j % 2), 16 * (j // 2 + 1))
        P.wait("scalar", "sVacc", j - 1)
        P.op("scalar", lambda E, j=j: E.activation(
            WLD[:, 2048 * (j % 2):2048 * (j % 2 + 1)],
            WLD[:, 2048 * (j % 2):2048 * (j % 2 + 1)], AF.Abs,
            accum_out=PAP[:, j % 2:j % 2 + 1]), inc="sSabs")
        acc = A["ACCG"] if j < 16 else (A["ACCU"] if j < 32 else A["ACCD"])
        P.wait("vector", "sSabs", j + 1)
        if j % 16 == 0:
            P.op("vector", lambda E, acc=acc, j=j: E.tensor_copy(
                acc[:, :], PAP[:, j % 2:j % 2 + 1]), inc="sVacc")
        else:
            P.op("vector", lambda E, acc=acc, j=j: E.tensor_add(
                acc[:, :], acc[:, :], PAP[:, j % 2:j % 2 + 1]), inc="sVacc")
        P.wait("vector", "sVacc", P.c("sVacc"))

    def p1_tensor(t):
        if t >= 1:
            P.wait("tensor", "sVrope", t)
            P.wait("tensor", "sSqk", t - 1)
            P.op("tensor", lambda E: E.transpose(PS[:, 1536:1664], RT[:, 0:128],
                                                 IDF[:, :]))
            P.op("tensor", lambda E: E.transpose(PS[:, 1664:1792], RT[:, 128:256],
                                                 IDF[:, :]), inc="sTqk2")
        P.wait("tensor", "sVxn", t + 1)
        P.wait("tensor", "sSxnt", t)
        for d in range(8):
            base = 3072 + 128 * d
            P.op("tensor", lambda E, d=d, base=base: E.transpose(
                PS[:, base:base + 128], XN[:, 128 * d:128 * (d + 1)], IDF[:, :]),
                inc=("sTtr" if d == 7 else None))
        P.wait("tensor", "sSxnt", t + 1)
        P.wait("tensor", "sVrope", t)
        P.wait("tensor", "sSv", t)
        for d in range(8):
            lhsT = XNT[:, 128 * d:128 * (d + 1)]
            st, sp = (d == 0), (d == 7)
            P.op("tensor", lambda E, lhsT=lhsT, d=d, st=st, sp=sp: E.matmul(
                PS[:, 0:128], lhsT, QKVW[:, 384 * d:384 * d + 128],
                start=st, stop=sp))
            P.op("tensor", lambda E, lhsT=lhsT, d=d, st=st, sp=sp: E.matmul(
                PS[:, 512:640], lhsT, QKVW[:, 384 * d + 128:384 * d + 256],
                start=st, stop=sp))
            P.op("tensor", lambda E, lhsT=lhsT, d=d, st=st, sp=sp: E.matmul(
                PS[:, 1024:1152], lhsT, QKVW[:, 384 * d + 256:384 * d + 384],
                start=st, stop=sp), inc=("sTqkv" if sp else None))


    P.wait("tensor", "sCstG", P.c("sCstG"))
    P.wait("tensor", "sCstS", P.c("sCstS"))
    for t in range(NT):
        p1_sync_tile(t)
        p1_vector(t)
        p1_rope(t)
        p1_scalar(t)
        p1_tensor(t)
        for j in range(min(NWA, (3 * t) // 2), min(NWA, (3 * (t + 1)) // 2)):
            p1_sync_passA(j)
            passA_chunk(j)
    P.wait("tensor", "sVrope", NT)
    P.wait("tensor", "sSqk", NT - 1)
    P.op("tensor", lambda E: E.transpose(PS[:, 1536:1664], RT[:, 0:128], IDF[:, :]))
    P.op("tensor", lambda E: E.transpose(PS[:, 1664:1792], RT[:, 128:256],
                                         IDF[:, :]), inc="sTqk2")
    P.wait("scalar", "sTqk2", NT)
    P.op("scalar", lambda E: E.copy(QF[:, 128 * (NT - 1):128 * NT],
                                    PS[:, 1536:1664]))
    P.op("scalar", lambda E: E.copy(KF[:, 128 * (NT - 1):128 * NT],
                                    PS[:, 1664:1792]), inc="sSqk")
    for j in range((3 * NT) // 2, NWA):
        p1_sync_passA(j)
        passA_chunk(j)

    P.wait("vector", "sSqk", NT)
    P.wait("tensor", "sSqk", NT)
    P.wait("tensor", "sSv", NT)
    P.wait("tensor", "sSxnt", NT)
    for w, src in ((0, KF), (1, QF)):
        for c in range(NT):
            P.wait("vector", "sTnorm", 32 * w + c - 1)
            P.op("vector", lambda E, src=src, c=c: E.tensor_tensor(
                SQS[:, 128 * (c % 2):128 * (c % 2 + 1)],
                src[:, 128 * c:128 * (c + 1)], src[:, 128 * c:128 * (c + 1)],
                ALU.mult), inc="sVnorm")
            P.wait("tensor", "sVnorm", 32 * w + c + 1)
            if w == 1 and c == 0:
                P.wait("tensor", "sVbp", 1)
            P.op("tensor", lambda E, c=c: E.matmul(
                PS[0:1, 128 * c:128 * (c + 1)], ONESB[:, :],
                SQS[:, 128 * (c % 2):128 * (c % 2 + 1)], start=True, stop=True),
                inc="sTnorm")
        P.wait("vector", "sTnorm", 32 * (w + 1))
        P.op("vector", lambda E, w=w: E.reduce_max(
            SC2[0:1, w:w + 1], PS[0:1, 0:4096], axis=AX.X), inc="sVbp")
        P.wait("vector", "sVbp", P.c("sVbp"))
    P.op("vector", lambda E: E.tensor_tensor(SC2[0:1, 2:3], SC2[0:1, 0:1],
                                             SC2[0:1, 1:2], ALU.mult), inc="sVbp")
    P.wait("vector", "sVbp", P.c("sVbp"))
    P.wait("scalar", "sVbp", P.c("sVbp"))
    P.op("scalar", lambda E: E.activation(SC2[0:1, 3:4], SC2[0:1, 2:3], AF.Sqrt),
         inc="sSbp")
    P.wait("vector", "sSbp", P.c("sSbp"))
    P.opf("vector", lambda E: E.tensor_scalar(SC2[0:1, 4:5], SC2[0:1, 3:4],
                                              INV_SQRT_HD, 1.0, ALU.mult, ALU.add))
    P.op("vector", lambda E: E.tensor_scalar(SC2[0:1, 4:5], SC2[0:1, 4:5],
                                             -1.0, None, ALU.mult), inc="sVbp")
    P.wait("vector", "sVbp", P.c("sVbp"))
    VBP1 = P.c("sVbp")
    P.wait("tensor", "sVacc", NWA)
    P.wait("tensor", "sVbp", VBP1)
    for i, acc in enumerate(("ACCG", "ACCU", "ACCD")):
        P.op("tensor", lambda E, i=i, acc=acc: E.matmul(
            PS[0:1, 2048 + 16 * i:2049 + 16 * i], ONESF[:, :], A[acc][:, :],
            start=True, stop=True), inc="sTbp")
    P.wait("vector", "sTbp", P.c("sTbp"))
    for i in range(3):
        P.op("vector", lambda E, i=i: E.tensor_scalar(
            SC2[0:1, 5 + i:6 + i], PS[0:1, 2048 + 16 * i:2049 + 16 * i],
            1.0 / N_ELEM_W, 1e-5, ALU.mult, ALU.max), inc="sVbp")
    P.wait("tensor", "sVbp", P.c("sVbp"))
    for i, col in enumerate((4, 5, 6, 7)):
        P.op("tensor", lambda E, col=col, i=i: E.matmul(
            PS[:, 2560 + 16 * i:2561 + 16 * i], ONESROW[:, :],
            SC2[0:1, col:col + 1], start=True, stop=True), inc="sTbp")
    P.wait("vector", "sTbp", P.c("sTbp"))
    P.op("vector", lambda E: E.tensor_copy(NEGB[:, :], PS[:, 2560:2561]))
    for i, (sw, mw) in enumerate((("SWG", "MWG127n"), ("SWU", "MWU127n"),
                                  ("SWD", "MWD127n"))):
        src = PS[:, 2576 + 16 * i:2577 + 16 * i]
        P.opf("vector", lambda E, sw=sw, src=src: E.reciprocal(A[sw][:, :], src))
        P.opf("vector", lambda E, mw=mw, src=src: E.tensor_scalar(
            A[mw][:, :], src, -1.0 / 127.0, None, ALU.mult))
    P.op("vector", lambda E: E.tensor_copy(SC1[:, 0:1], SC1[:, 0:1]), inc="sVbp")
    BP_DONE = P.c("sVbp")

    P.wait("sync", "sVbp", BP_DONE)

    def wb_src(u):
        if u < 16:
            rb, h = (u % 16) // 2, u % 2
            return wgT_d[128 * rb:128 * (rb + 1), 2048 * h:2048 * (h + 1)]
        if u < 32:
            uu = u - 16
            rb, h = uu // 2, uu % 2
            return wuT_d[128 * rb:128 * (rb + 1), 2048 * h:2048 * (h + 1)]
        c = u - 32
        return wdT_d[128 * c:128 * (c + 1), :]

    for u in range(NWB):
        P.wait("sync", "sWBdone", u - 1)
        if u >= 34:
            c_prev = u - 34
            P.wait("sync", "sDQ%d" % (c_prev % 2), 16 * (c_prev // 2 + 1))
        ncols = 2048 if u < 32 else 1024
        P.op("sync", lambda E, u=u, ncols=ncols: E.dma_start(
            out=WLB[:, 2048 * (u % 2):2048 * (u % 2) + ncols], in_=wb_src(u)),
            inc="sWB%d" % (u % 2), n=16)
        if u >= 32:
            c = u - 32
            P.wait("sync", "sWBdone", u + 1)
            P.op("sync", lambda E, c=c: E.dma_start(
                out=wdq_d[128 * c:128 * (c + 1), :], in_=WDQA[c % 2][:, :]),
                inc="sDQ%d" % (c % 2), n=16)

    def passB_unit(u):
        ncols = 2048 if u < 32 else 1024
        sl = WLB[:, 2048 * (u % 2):2048 * (u % 2) + ncols]
        sw = A["SWG"] if u < 16 else (A["SWU"] if u < 32 else A["SWD"])
        P.wait("vector", "sWB%d" % (u % 2), 16 * (u // 2 + 1))
        for half in range(ncols // 1024):
            hsl = sl[:, 1024 * half:1024 * (half + 1)]
            P.op("vector", lambda E, hsl=hsl, sw=sw: E.tensor_scalar(
                TL[:, :], hsl, sw[:, :], -0.5, ALU.mult, ALU.is_lt))
            P.op("vector", lambda E, hsl=hsl, sw=sw: E.tensor_scalar(
                TG[:, :], hsl, sw[:, :], 0.5, ALU.mult, ALU.is_gt))
            if u < 32:
                dstW = WG if u < 16 else WU
                cw = u % 16
                rb, hh = cw // 2, cw % 2
                dst = dstW[:, 4096 * rb + 2048 * hh + 1024 * half:
                           4096 * rb + 2048 * hh + 1024 * (half + 1)]
                last = (half == ncols // 1024 - 1)
                P.op("vector", lambda E, dst=dst: E.tensor_sub(dst, TG[:, :], TL[:, :]),
                     inc=("sWBdone" if last else None))
            else:
                c = u - 32
                if c >= 2:
                    P.wait("vector", "sDQ%d" % (c % 2), 16 * (c // 2))
                P.op("vector", lambda E, c=c: E.tensor_sub(
                    WDQA[c % 2][:, :], TG[:, :], TL[:, :]), inc="sWBdone")

    P.wait("tensor", "sVbp", BP_DONE)
    P.wait("scalar", "sVbp", BP_DONE)

    def a2_tensor_scores(i, g):
        P.wait("tensor", "sSexp", 8 * i + g - 1)
        for k in range(4):
            c = 4 * g + k
            bank = 2 * (g % 2) + k // 2
            off = 512 * bank + 256 * (k % 2)
            P.op("tensor", lambda E, c=c, off=off, i=i: E.matmul(
                PS[:, off:off + 256], KF[:, 128 * c:128 * (c + 1)],
                QF[:, 256 * i:256 * (i + 1)], start=True, stop=True),
                inc=("sTsc" if k == 3 else None))

    def a2_tensor_pv(i, g):
        P.wait("tensor", "sSexp", 8 * i + g + 1)
        if g == 0:
            P.wait("tensor", "sSatt", i)
        for k in range(4):
            c = 4 * g + k
            P.op("tensor", lambda E, c=c: E.matmul(
                PS[:, 2048:2304], VT[:, 128 * c:128 * (c + 1)],
                PT[:, 256 * c:256 * (c + 1)], start=(c == 0), stop=(c == 31)),
                inc=("sTpv" if k == 3 else None))

    for i in range(QI):
        for g in range(8):
            a2_tensor_scores(i, g)
            if g >= 1:
                a2_tensor_pv(i, g - 1)
        a2_tensor_pv(i, 7)
        P.wait("tensor", "sSden", i)
        for c in range(32):
            P.op("tensor", lambda E, c=c: E.matmul(
                PS[0:1, 2560:2816], ONESB[:, :], PT[:, 256 * c:256 * (c + 1)],
                start=(c == 0), stop=(c == 31)), inc=("sTden" if c == 31 else None))
        P.wait("tensor", "sSden", i + 1)
        P.wait("tensor", "sVrd", i)
        P.op("tensor", lambda E: E.transpose(PS[:, 3072:3073], DEN[0:1, 0:128],
                                             IDF[0:1, 0:1]))
        P.op("tensor", lambda E: E.transpose(PS[:, 3073:3074], DEN[0:1, 128:256],
                                             IDF[0:1, 0:1]), inc="sTrd")
        for g in range(8):
            if g == 0:
                P.wait("scalar", "sTden", i)
            P.wait("scalar", "sTsc", 8 * i + g + 1)
            src = PS[:, 1024 * (g % 2):1024 * (g % 2) + 1024]
            P.op("scalar", lambda E, src=src, g=g: E.activation(
                PT[:, 1024 * g:1024 * (g + 1)], src, AF.Exp,
                bias=NEGB[:, :], scale=INV_SQRT_HD), inc="sSexp")
        P.wait("scalar", "sTpv", 8 * (i + 1))
        P.op("scalar", lambda E, i=i: E.copy(ATT[:, 256 * i:256 * (i + 1)],
                                             PS[:, 2048:2304]), inc="sSatt")
        P.wait("scalar", "sTden", i + 1)
        P.op("scalar", lambda E: E.copy(DEN[0:1, :], PS[0:1, 2560:2816]),
             inc="sSden")
        P.wait("vector", "sTrd", i + 1)
        P.op("vector", lambda E, i=i: E.reciprocal(RDEN[:, 2 * i:2 * i + 2],
                                                   PS[:, 3072:3074]), inc="sVrd")
        P.wait("vector", "sVrd", P.c("sVrd"))
        for u in range(4 * i, 4 * i + 4):
            passB_unit(u)

    DQ_ALL = 16 * 32
    P.wait("sync", "sTden", QI)
    P.wait("sync", "sDQ0", 16 * 16)
    P.wait("sync", "sDQ1", 16 * 16)
    P.wait("sync", "sWBdone", NWB)
    P.wait("tensor", "sVrd", QI)
    P.wait("tensor", "sSatt", QI)
    P.wait("tensor", "sWBdone", NWB)
    P.wait("vector", "sDQ0", 16 * 16)
    P.wait("vector", "sDQ1", 16 * 16)
    P.wait("scalar", "sVrd", QI)

    P.wait("gpsimd", "sDQ0", 16 * 16)
    P.wait("gpsimd", "sDQ1", 16 * 16)
    P.wait("gpsimd", "sTden", QI)
    for t in range(NT + 2):
        if t < NT:
            if t >= 3:
                P.wait("sync", "sOut%d" % ((t - 3) % 2), 16 * ((t - 3) // 2 + 1))
            P.op("sync", lambda E, t=t: E.dma_start(
                out=YO[:, 1024 * (t % 3):1024 * (t % 3 + 1)],
                in_=x_d[128 * t:128 * (t + 1), :]), inc="sX2ld", n=16)
        if t >= 2:
            to = t - 2
            P.wait("sync", "sVout", 2 * to + 2)
            P.op("sync", lambda E, to=to: E.dma_start(
                out=out_d[128 * to:128 * (to + 1), :],
                in_=YO[:, 1024 * (to % 3):1024 * (to % 3 + 1)]),
                inc="sOut%d" % (to % 2), n=16)
        if 1 <= t <= NT:
            td = t - 1
            for c in range(32):
                P.wait("gpsimd", "sTdn", 32 * td + c - 7)
                P.op("gpsimd", lambda E, c=c: E.dma_start(
                    out=WD[:, 1024 * (c % 8):1024 * (c % 8 + 1)],
                    in_=wdq_d[128 * c:128 * (c + 1), :]),
                    inc="sWDl%d" % (c % 8), n=16)
    P.wait("sync", "sOut0", 16 * (NT // 2))
    P.wait("sync", "sOut1", 16 * (NT // 2))
    if dbg:
        dbg_specs = [
            ("NEGB", [128, 1], F32), ("RDEN", [128, NT], F32),
            ("MEAN", [128, NT], F32), ("RS", [128, NT], F32),
            ("SWG", [128, 1], F32), ("SWU", [128, 1], F32), ("SWD", [128, 1], F32),
            ("MWG127n", [128, 1], F32), ("MWD127n", [128, 1], F32),
            ("CT", [128, 8], F32), ("DEN", [1, 256], F32),
            ("ATT", [128, S], BF16), ("KF", [128, S], BF16),
            ("QF", [128, S], BF16), ("VT", [128, S], BF16),
            ("WG", [128, 8 * INNER], BF16),
            ("HQT", [128, 4096], BF16), ("ATX", [128, 1024], BF16),
            ("H", [128, 4096], F32), ("XN2", [128, 1024], F32),
            ("YO", [128, 2048], F32), ("SC2", [1, 8], F32),
        ]
        for nm, shp, dt in dbg_specs:
            od = nc.dram_tensor("dbg_" + nm, shp, dt, kind="ExternalOutput").ap()
            P.op("sync", lambda E, od=od, nm=nm: E.dma_start(out=od, in_=A[nm][:, :]),
                 inc="sOut0", n=16)
        P.wait("sync", "sOut0", 16 * (NT // 2) + 16 * len(dbg_specs))

    def b2_front_tensor(t):
        s = t % 2
        P.wait("tensor", "sVout", 2 * (t - 1))
        if t >= 1:
            P.wait("tensor", "sVh", 8 * (t - 1) + 6)
            P.wait("tensor", "sSgl", 8 * (t - 1) + 6)
        for b in range(2):
            P.wait("tensor", "sVaq", 2 * t + b + 1)
            if b == 1:
                P.wait("tensor", "sSat2", 2 * t + 1)
            for k in range(4):
                d = 4 * b + k
                P.op("tensor", lambda E, d=d, k=k: E.transpose(
                    PS[:, 3072 + 128 * k:3072 + 128 * (k + 1)],
                    XN2[:, 128 * d:128 * (d + 1)], IDF[:, :]),
                    inc=("sTat2" if k == 3 else None))
        if t >= 1:
            P.wait("tensor", "sSat2", 2 * t + 2)
            for b in range(8):
                P.wait("tensor", "sVq", 8 * (t - 1) + b + 1)
                if b >= 1:
                    P.wait("tensor", "sShq", 8 * (t - 1) + b)
                for k in range(4):
                    c = 4 * b + k
                    P.op("tensor", lambda E, c=c, k=k: E.transpose(
                        PS[:, 3072 + 128 * k:3072 + 128 * (k + 1)],
                        H[:, 128 * c:128 * (c + 1)], IDF[:, :]),
                        inc=("sThq" if k == 3 else None))
        P.wait("tensor", "sSat2", 2 * (t + 1))
        for e in range(8):
            se = (e + 2) % 4
            P.wait("tensor", "sVh", 8 * t + e - 3)
            P.wait("tensor", "sSgl", 8 * t + e - 3)
            gcol, ucol = 1024 * se, 1024 * se + 512
            wcol = 2048 * (e // 4) + 512 * (e % 4)
            for d in range(8):
                st, sp = (d == 0), (d == 7)
                P.op("tensor", lambda E, d=d, st=st, sp=sp, gcol=gcol, wcol=wcol:
                     E.matmul(PS[:, gcol:gcol + 512], ATX[:, 128 * d:128 * (d + 1)],
                              WG[:, 4096 * d + wcol:4096 * d + wcol + 512],
                              start=st, stop=sp))
                P.op("tensor", lambda E, d=d, st=st, sp=sp, ucol=ucol, wcol=wcol:
                     E.matmul(PS[:, ucol:ucol + 512], ATX[:, 128 * d:128 * (d + 1)],
                              WU[:, 4096 * d + wcol:4096 * d + wcol + 512],
                              start=st, stop=sp), inc=("sTgu" if sp else None))

    def b2_last_hqt():
        t = NT
        P.wait("tensor", "sSat2", 2 * NT)
        P.wait("tensor", "sVout", 2 * NT - 2)
        for b in range(8):
            P.wait("tensor", "sVq", 8 * (t - 1) + b + 1)
            if b >= 1:
                P.wait("tensor", "sShq", 8 * (t - 1) + b)
            for k in range(4):
                c = 4 * b + k
                P.op("tensor", lambda E, c=c, k=k: E.transpose(
                    PS[:, 3072 + 128 * k:3072 + 128 * (k + 1)],
                    H[:, 128 * c:128 * (c + 1)], IDF[:, :]),
                    inc=("sThq" if k == 3 else None))

    def b2_tail_tensor(t):
        s = t % 2
        P.wait("tensor", "sVh", min(8 * (t + 1) + 6, 8 * NT))
        P.wait("tensor", "sSgl", min(8 * (t + 1) + 6, 8 * NT))
        P.wait("tensor", "sVout", 2 * t - 1)
        P.op("tensor", lambda E, t=t: E.matmul(
            PS[:, 2048:2560], ATT[:, 128 * t:128 * (t + 1)], WO[:, 0:512],
            start=True, stop=True))
        P.op("tensor", lambda E, t=t: E.matmul(
            PS[:, 2560:3072], ATT[:, 128 * t:128 * (t + 1)], WO[:, 512:1024],
            start=True, stop=True), inc="sTpj")
        P.wait("tensor", "sShq", 8 * (t + 1))
        P.wait("tensor", "sVout", 2 * t)
        for c in range(32):
            P.wait("tensor", "sWDl%d" % (c % 8), 16 * (4 * t + c // 8 + 1))
            st, sp = (c == 0), (c == 31)
            P.op("tensor", lambda E, c=c, st=st, sp=sp: E.matmul(
                PS[:, 3072:3584], HQT[:, 128 * c:128 * (c + 1)],
                WD[:, 1024 * (c % 8):1024 * (c % 8) + 512], start=st, stop=sp))
            P.op("tensor", lambda E, c=c, st=st, sp=sp: E.matmul(
                PS[:, 3584:4096], HQT[:, 128 * c:128 * (c + 1)],
                WD[:, 1024 * (c % 8) + 512:1024 * (c % 8 + 1)], start=st, stop=sp),
                inc="sTdn")

    qs_cnt = {"s": 0, "v": 0}

    def quant_scalar_sub(z_slice, scale_ap, wait_sem, wait_val):
        g = qs_cnt["s"]; qs_cnt["s"] += 1
        sl = 512 * (g % 2)
        P.wait("scalar", wait_sem, wait_val)
        P.wait("scalar", "sVqd", g - 1)
        P.op("scalar", lambda E, z_slice=z_slice, scale_ap=scale_ap, sl=sl:
             E.activation(QS1[:, sl:sl + 512], z_slice, AF.Identity, scale=scale_ap))
        P.op("scalar", lambda E, z_slice=z_slice, scale_ap=scale_ap, sl=sl:
             E.activation(QS2[:, sl:sl + 512], z_slice, AF.Identity, scale=scale_ap),
             inc="sSqp")

    def quant_vector_sub(z_slice, inc_last=None):
        g = qs_cnt["v"]; qs_cnt["v"] += 1
        sl = 512 * (g % 2)
        P.wait("vector", "sSqp", g + 1)
        P.op("vector", lambda E, sl=sl: E.scalar_tensor_tensor(
            QS1[:, sl:sl + 512], QS2[:, sl:sl + 512], -1.0, QS1[:, sl:sl + 512],
            ALU.mult, ALU.add))
        P.op("vector", lambda E, z_slice=z_slice, sl=sl: E.scalar_tensor_tensor(
            z_slice, QS1[:, sl:sl + 512], 0.5, QS2[:, sl:sl + 512],
            ALU.is_ge, ALU.add))
        P.op("vector", lambda E, z_slice=z_slice, sl=sl: E.scalar_tensor_tensor(
            z_slice, QS1[:, sl:sl + 512], -0.5, z_slice,
            ALU.is_lt, ALU.subtract), inc=inc_last)
        P.op("vector", lambda E: E.tensor_copy(SC1[:, 5:6], SC1[:, 5:6]), inc="sVqd")

    def b2_prep_vector(t):
        s3 = t % 3
        P.wait("vector", "sX2ld", 16 * (t + 1))
        P.wait("vector", "sTat2", 2 * t)
        P.op("vector", lambda E, t=t, s3=s3: E.tensor_scalar(
            XN2[:, :], YO[:, 1024 * s3:1024 * (s3 + 1)],
            MEAN[:, t:t + 1], RS[:, t:t + 1], ALU.subtract, ALU.mult))
        P.opf("vector", lambda E: E.reduce_max(SC1[:, 0:1], XN2[:, :], axis=AX.X,
                                               apply_absolute_value=True))
        P.opf("vector", lambda E: E.tensor_scalar(SC1[:, 0:1], SC1[:, 0:1],
                                                  1e-5, None, ALU.max))
        P.opf("vector", lambda E: E.reciprocal(SC1[:, 1:2], SC1[:, 0:1]))
        P.opf("vector", lambda E, s3=s3: E.tensor_scalar(
            CT[:, 4 * s3:4 * s3 + 1], SC1[:, 1:2], 127.0, None, ALU.mult))
        P.opf("vector", lambda E, s3=s3: E.tensor_tensor(
            CT[:, 4 * s3 + 1:4 * s3 + 2], SC1[:, 0:1], A["MWG127n"][:, :], ALU.mult))
        P.opf("vector", lambda E, s3=s3: E.tensor_tensor(
            CT[:, 4 * s3 + 2:4 * s3 + 3], SC1[:, 0:1], A["MWU127n"][:, :], ALU.mult))
        P.op("vector", lambda E: E.tensor_copy(SC1[:, 6:7], SC1[:, 6:7]), inc="sVln")
        for k in range(2):
            quant_vector_sub(XN2[:, 512 * k:512 * (k + 1)], inc_last="sVaq")

    def b2_eighths_vector(t):
        s3 = t % 3
        for e in range(8):
            se = (e + 2) % 4
            P.wait("vector", "sSgl", 8 * t + e + 1)
            ucol = 1024 * se + 512
            P.op("vector", lambda E, e=e, ucol=ucol, s3=s3: E.scalar_tensor_tensor(
                H[:, 512 * e:512 * (e + 1)], PS[:, ucol:ucol + 512],
                CT[:, 4 * s3 + 2:4 * s3 + 3], H[:, 512 * e:512 * (e + 1)],
                ALU.mult, ALU.mult), inc="sVh")
            P.opf("vector", lambda E, e=e: E.reduce_max(
                SC1[:, 4:5], H[:, 512 * e:512 * (e + 1)], axis=AX.X,
                apply_absolute_value=True))
            if e == 0:
                P.opf("vector", lambda E: E.tensor_copy(SC1[:, 2:3], SC1[:, 4:5]))
            else:
                P.opf("vector", lambda E: E.tensor_tensor(
                    SC1[:, 2:3], SC1[:, 2:3], SC1[:, 4:5], ALU.max))

    def b2_quant_vector_scales(t):
        s3 = t % 3
        P.opf("vector", lambda E: E.tensor_scalar(SC1[:, 2:3], SC1[:, 2:3],
                                                  1e-5, None, ALU.max))
        P.opf("vector", lambda E: E.reciprocal(SC1[:, 3:4], SC1[:, 2:3]))
        P.opf("vector", lambda E: E.tensor_scalar(SC1[:, 3:4], SC1[:, 3:4],
                                                  127.0, None, ALU.mult))
        P.opf("vector", lambda E, s3=s3: E.tensor_tensor(
            CT[:, 4 * s3 + 3:4 * s3 + 4], SC1[:, 2:3], A["MWD127n"][:, :], ALU.mult))
        P.op("vector", lambda E: E.tensor_copy(SC1[:, 7:8], SC1[:, 7:8]), inc="sVsh")

    def b2_quant_vector_fixups(t):
        P.wait("vector", "sThq", 8 * t)
        for k in range(8):
            quant_vector_sub(H[:, 512 * k:512 * (k + 1)], inc_last="sVq")

    def b2_tail_vector(t):
        s3 = t % 3
        P.wait("vector", "sTpj", t + 1)
        P.op("vector", lambda E, t=t, s3=s3: E.scalar_tensor_tensor(
            YO[:, 1024 * s3:1024 * (s3 + 1)], PS[:, 2048:3072],
            RDEN[:, t:t + 1], YO[:, 1024 * s3:1024 * (s3 + 1)],
            ALU.mult, ALU.add), inc="sVout")
        P.wait("vector", "sTdn", 32 * (t + 1))
        P.op("vector", lambda E, t=t, s3=s3: E.scalar_tensor_tensor(
            YO[:, 1024 * s3:1024 * (s3 + 1)], PS[:, 3072:4096],
            CT[:, 4 * s3 + 3:4 * s3 + 4], YO[:, 1024 * s3:1024 * (s3 + 1)],
            ALU.mult, ALU.add), inc="sVout")

    def b2_scalar(t):
        s = t % 3
        P.wait("scalar", "sTgu", 8 * t)
        for b in range(2):
            P.wait("scalar", "sTat2", 2 * t + b + 1)
            P.op("scalar", lambda E, b=b: E.copy(
                ATX[:, 512 * b:512 * (b + 1)], PS[:, 3072:3584]), inc="sSat2")
        if t >= 1:
            P.wait("scalar", "sTdn", 32 * (t - 1))
            for b in range(8):
                P.wait("scalar", "sThq", 8 * (t - 1) + b + 1)
                P.op("scalar", lambda E, b=b: E.copy(
                    HQT[:, 512 * b:512 * (b + 1)], PS[:, 3072:3584]), inc="sShq")
        P.wait("scalar", "sVq", 8 * t)
        for e in range(8):
            se = (e + 2) % 4
            P.wait("scalar", "sTgu", 8 * t + e + 1)
            gcol = 1024 * se
            P.op("scalar", lambda E, e=e, gcol=gcol, s=s: E.activation(
                H[:, 512 * e:512 * (e + 1)], PS[:, gcol:gcol + 512], AF.Silu,
                scale=CT[:, 4 * s + 1:4 * s + 2]), inc="sSgl")
        if t + 1 < NT:
            s3n = (t + 1) % 3
            for k in range(2):
                quant_scalar_sub(XN2[:, 512 * k:512 * (k + 1)],
                                 CT[:, 4 * s3n:4 * s3n + 1], "sVln", t + 2)
        for k in range(8):
            quant_scalar_sub(H[:, 512 * k:512 * (k + 1)], SC1[:, 3:4],
                             "sVsh", t + 1)

    def b2_last_scalar_hq():
        t = NT
        P.wait("scalar", "sTdn", 32 * (t - 1))
        for b in range(8):
            P.wait("scalar", "sThq", 8 * (t - 1) + b + 1)
            P.op("scalar", lambda E, b=b: E.copy(
                HQT[:, 512 * b:512 * (b + 1)], PS[:, 3072:3584]), inc="sShq")

    P.op("vector", lambda E: E.memset(SC1[:, 4:8], 0.0))
    for k in range(2):
        quant_scalar_sub(XN2[:, 512 * k:512 * (k + 1)], CT[:, 0:1], "sVln", 1)
    b2_prep_vector(0)
    for t in range(NT + 1):
        if t < NT:
            b2_front_tensor(t)
            b2_scalar(t)
            b2_eighths_vector(t)
            b2_quant_vector_scales(t)
            if t + 1 < NT:
                b2_prep_vector(t + 1)
            b2_quant_vector_fixups(t)
            if t >= 1:
                b2_tail_vector(t - 1)
        else:
            b2_last_hqt()
            b2_last_scalar_hq()
            b2_tail_vector(NT - 1)
        if t >= 1:
            b2_tail_tensor(t - 1)

    with nc.Block() as block:
        @block.sync
        def _(E):
            for fn in P.streams["sync"]:
                fn(E)

        @block.tensor
        def _(E):
            for fn in P.streams["tensor"]:
                fn(E)

        @block.vector
        def _(E):
            for fn in P.streams["vector"]:
                fn(E)

        @block.scalar
        def _(E):
            for fn in P.streams["scalar"]:
                fn(E)

        @block.gpsimd
        def _(E):
            for fn in P.streams["gpsimd"]:
                fn(E)

    return nc


_NC_CACHE = None


def _get_nc():
    global _NC_CACHE
    if _NC_CACHE is None:
        _NC_CACHE = build_nc()
    return _NC_CACHE


def _host_constants():
    inv_freq = (1.0 / (np.float32(10000.0) **
                       (np.arange(0, HDIM, 2, dtype=np.float32) / np.float32(HDIM))))
    inv_freq = inv_freq.astype(np.float32)
    tpos = np.arange(S - 1, dtype=np.float32)
    ang = (tpos[:, None] * inv_freq[None, :]).astype(np.float32)
    cos = np.cos(ang).astype(np.float32)
    sin = np.sin(ang).astype(np.float32)
    trig = np.zeros((S, 128), np.float32)
    trig[0, 0:64] = 1.0
    trig[1:, 0:64] = cos
    trig[1:, 64:128] = sin
    idb = np.eye(128, dtype=BF16_NP)
    idf = np.eye(128, dtype=np.float32)
    onesb = np.ones((128, 1), BF16_NP)
    onesf = np.ones((128, 1), np.float32)
    onesrow = np.ones((1, 128), np.float32)
    return trig, idb, idf, onesb, onesf, onesrow


def kernel(L0_attn_x, L0_attn_wq, L0_attn_wk, L0_attn_wv, L0_attn_wo,
           L0_ffn_w_gate, L0_ffn_w_up, L0_ffn_w_down):
    x = np.ascontiguousarray(np.asarray(L0_attn_x, dtype=np.float32))
    wq = np.asarray(L0_attn_wq, np.float32)
    wk = np.asarray(L0_attn_wk, np.float32)
    wv = np.asarray(L0_attn_wv, np.float32)
    wo = np.asarray(L0_attn_wo, np.float32)
    qkvT = np.ascontiguousarray(
        np.concatenate([wq.T, wk.T, wv.T], axis=1)).astype(np.float32)
    woT = np.ascontiguousarray(wo.T)
    wgT = np.ascontiguousarray(np.asarray(L0_ffn_w_gate, np.float32).T)
    wuT = np.ascontiguousarray(np.asarray(L0_ffn_w_up, np.float32).T)
    wdT = np.ascontiguousarray(np.asarray(L0_ffn_w_down, np.float32).T)
    trig, idb, idf, onesb, onesf, onesrow = _host_constants()

    nc = _get_nc()
    in_maps = []
    for b in range(B):
        in_maps.append({
            "x": np.ascontiguousarray(x[b]),
            "qkvT": qkvT, "woT": woT,
            "wgT": wgT, "wuT": wuT, "wdT": wdT,
            "trig": trig, "idb": idb, "idf": idf,
            "onesb": onesb, "onesf": onesf, "onesrow": onesrow,
        })
    res = run_bass_kernel_spmd(nc, in_maps, core_ids=list(range(B)))
    out = np.stack([np.asarray(res.results[b]["out"], dtype=np.float32)
                    for b in range(B)], axis=0)
    return out


# version 22
# speedup vs baseline: 1.2034x; 1.0063x over previous
import numpy as np
import ml_dtypes

import concourse.bass as bass
import concourse.mybir as mybir
from concourse.bass_utils import run_bass_kernel_spmd

F32 = mybir.dt.float32
BF16 = mybir.dt.bfloat16
I32 = mybir.dt.int32
I8 = mybir.dt.int8
AF = mybir.ActivationFunctionType
ALU = mybir.AluOpType
AX = mybir.AxisListType
BF16_NP = ml_dtypes.bfloat16

B, S, D, HDIM, INNER = 8, 4096, 1024, 128, 4096
NT = S // 128
QI = S // 256
NWA = 48
NWB = 64
INV_SQRT_HD = 0.08838834764831845
N_ELEM_W = float(INNER * D)

ENGINES = ("sync", "tensor", "vector", "scalar", "gpsimd")


class Prog:
    def __init__(self):
        self.streams = {e: [] for e in ENGINES}
        self.counts = {}
        self._h = None

    def sem(self, name):
        self.counts[name] = 0

    def emit(self, eng, fn):
        self.streams[eng].append(fn)

    def wait(self, eng, sem_name, val):
        if val <= 0:
            return
        self.emit(eng, lambda E, s=sem_name, v=val: E.wait_ge(self._h[s], v))

    def op(self, eng, builder, inc=None, n=1):
        if inc is None:
            self.emit(eng, builder)
        else:
            self.counts[inc] += n

            def fn(E, b=builder, s=inc, k=n):
                b(E).then_inc(self._h[s], k)
            self.emit(eng, fn)

    def c(self, name):
        return self.counts[name]

    def opf(self, eng, builder):
        s = "chain_" + eng
        self.op(eng, builder, inc=s)
        self.wait(eng, s, self.counts[s])


def build_nc(dbg=False):
    nc = bass.Bass("TRN2", target_bir_lowering=False,
                   detect_race_conditions=False)

    x_d = nc.dram_tensor("x", [S, D], F32, kind="ExternalInput").ap()
    qkvT_d = nc.dram_tensor("qkvT", [D, 384], F32, kind="ExternalInput").ap()
    woT_d = nc.dram_tensor("woT", [HDIM, D], F32, kind="ExternalInput").ap()
    wgT_d = nc.dram_tensor("wgT", [D, INNER], F32, kind="ExternalInput").ap()
    wuT_d = nc.dram_tensor("wuT", [D, INNER], F32, kind="ExternalInput").ap()
    wdT_d = nc.dram_tensor("wdT", [INNER, D], F32, kind="ExternalInput").ap()
    trig_d = nc.dram_tensor("trig", [S, 128], F32, kind="ExternalInput").ap()
    idb_d = nc.dram_tensor("idb", [128, 128], BF16, kind="ExternalInput").ap()
    idf_d = nc.dram_tensor("idf", [128, 128], F32, kind="ExternalInput").ap()
    onesb_d = nc.dram_tensor("onesb", [128, 1], BF16, kind="ExternalInput").ap()
    onesf_d = nc.dram_tensor("onesf", [128, 1], F32, kind="ExternalInput").ap()
    onesrow_d = nc.dram_tensor("onesrow", [1, 128], F32, kind="ExternalInput").ap()
    out_d = nc.dram_tensor("out", [S, D], F32, kind="ExternalOutput").ap()
    wdq_d = nc.dram_tensor("wdq_scratch", [INNER, D], I8, kind="Internal").ap()

    BASE = 16512
    TOP = 229376
    A = {}
    def sb(name, shape, dtype, off):
        assert off % 32 == 0, (name, off)
        A[name] = nc.alloc_sbuf_tensor_at(name, shape, dtype, offset=off)
        return A[name]

    o = BASE
    sb("WG", [128, 8 * INNER], BF16, o); o += 65536
    sb("WU", [128, 8 * INNER], BF16, o); o += 65536
    sb("ATT", [128, S], BF16, o); o += 8192
    sb("WO", [128, D], BF16, o); o += 2048
    for nm, shp, dt, szb in [
        ("MEAN", [128, NT], F32, 128), ("RS", [128, NT], F32, 128),
        ("RDEN", [128, NT], F32, 128),
        ("IDB", [128, 128], BF16, 256), ("IDF", [128, 128], F32, 512),
        ("ONESB", [128, 1], BF16, 32), ("ONESF", [128, 1], F32, 32),
        ("ONESROW", [1, 128], F32, 512),
        ("NEGB", [128, 1], F32, 32),
        ("SWG", [128, 1], F32, 32), ("SWU", [128, 1], F32, 32), ("SWD", [128, 1], F32, 32),
        ("MWG127n", [128, 1], F32, 32), ("MWU127n", [128, 1], F32, 32),
        ("MWD127n", [128, 1], F32, 32),
        ("ACCG", [128, 1], F32, 32), ("ACCU", [128, 1], F32, 32), ("ACCD", [128, 1], F32, 32),
        ("PAP", [128, 2], F32, 32),
        ("SC1", [128, 8], F32, 32),
        ("SC2", [1, 8], F32, 32),
        ("DEN", [1, 256], F32, 1024),
    ]:
        sb(nm, shp, dt, o); o += szb
    PERS_END = o
    OVA = (o + 31) // 32 * 32
    sb("KF", [128, S], BF16, OVA)
    sb("QF", [128, S], BF16, OVA + 8192)
    sb("VT", [128, S], BF16, OVA + 16384)
    OVB = OVA + 24576
    sb("QKVW", [128, 3072], BF16, OVB)
    sb("XB", [128, 2048], F32, OVB + 6144)
    sb("XN", [128, 1024], F32, OVB + 14336)
    sb("XNT", [128, 1024], BF16, OVB + 18432)
    sb("RT", [128, 384], F32, OVB + 20480)
    sb("TRG", [128, 256], F32, OVB + 22016)
    sb("WLD", [128, 4096], F32, OVB + 23040)
    sb("SQS", [128, 256], BF16, OVB + 39424)
    assert OVB + 39936 <= TOP
    sb("PT", [128, 8192], BF16, OVB)
    sb("WLB", [128, 4096], F32, OVB + 16384)
    sb("WDQA0", [128, 1024], I8, OVB + 16384)
    sb("WDQA1", [128, 1024], I8, OVB + 24576)
    sb("TL", [128, 1024], F32, OVB + 32768)
    sb("TG", [128, 1024], F32, OVB + 36864)
    assert OVB + 40960 <= TOP
    o = OVA
    sb("H", [128, 4096], F32, o); o += 16384
    sb("HQT", [128, 4096], BF16, o); o += 8192
    sb("WD", [128, 8192], BF16, o); o += 16384
    sb("XN2", [128, 1024], F32, o); o += 4096
    sb("ATX", [128, 1024], BF16, o); o += 2048
    sb("QS1", [128, 1024], F32, o); o += 4096
    sb("QS2", [128, 1024], I32, o); o += 4096
    sb("YO", [128, 3072], F32, o); o += 12288
    sb("CT", [128, 12], F32, o); o += 64
    assert o <= TOP, o

    _psg = nc.psum_tensor("PS", [128, 4096], F32)
    PS = _psg.__enter__()

    P = Prog()
    sem_names = [
        "sCstG", "sCstS", "sXld0", "sXld1", "sTRG0", "sTRG1", "sWA0", "sWA1",
        "sSabs", "sVacc",
        "sSsq", "sVrec", "sSrs0", "sVxn",
        "sTtr", "sSxnt", "sTqkv", "sVrope", "sTqk2", "sSqk", "sSv",
        "sVnorm", "sTnorm", "sVbp", "sSbp", "sTbp",
        "sTsc", "sSexp", "sTpv", "sTden", "sSatt", "sSden", "sTrd", "sVrd",
        "sWB0", "sWB1", "sWBdone", "sDQ0", "sDQ1",
        "sX2ld", "sWDl0", "sWDl1", "sWDl2", "sWDl3", "sWDl4", "sWDl5", "sWDl6", "sWDl7",
        "sVaq", "sTat2", "sSat2", "sTgu", "sSgl", "sVh", "sVq",
        "sThq", "sShq", "sTpj", "sTdn", "sVout", "sOut0", "sOut1",
        "chain_vector", "chain_scalar", "chain_tensor", "chain_sync", "chain_gpsimd",
        "sVln", "sVsh", "sSqp", "sVqd",
    ]
    sems = {}
    for nm in sem_names:
        sems[nm] = nc.semaphore(nm).__enter__()
        P.sem(nm)
    P._h = sems

    KF, QF, VT = A["KF"], A["QF"], A["VT"]
    ATT, WO, QKVW = A["ATT"], A["WO"], A["QKVW"]
    XB, XN, XNT, RT, TRG, WLD, SQS = (A["XB"], A["XN"], A["XNT"], A["RT"],
                                      A["TRG"], A["WLD"], A["SQS"])
    MEAN, RS, RDEN = A["MEAN"], A["RS"], A["RDEN"]
    IDB, IDF = A["IDB"], A["IDF"]
    ONESB, ONESF, ONESROW = A["ONESB"], A["ONESF"], A["ONESROW"]
    NEGB, SC1, SC2, DEN, PAP = A["NEGB"], A["SC1"], A["SC2"], A["DEN"], A["PAP"]
    PT, WLB, TL, TG = A["PT"], A["WLB"], A["TL"], A["TG"]
    WDQA = (A["WDQA0"], A["WDQA1"])
    WG, WU = A["WG"], A["WU"]
    H, HQT, WD = A["H"], A["HQT"], A["WD"]
    XN2, ATX = A["XN2"], A["ATX"]
    QS1, QS2, YO, CT = A["QS1"], A["QS2"], A["YO"], A["CT"]

    for d in range(8):
        P.op("gpsimd", lambda E, d=d: E.dma_start(
            out=QKVW[:, 384 * d:384 * (d + 1)],
            in_=qkvT_d[128 * d:128 * (d + 1), :]), inc="sCstG", n=16)
    P.op("gpsimd", lambda E: E.dma_start(out=WO[:, :], in_=woT_d), inc="sCstG", n=16)
    for dst, src in [(IDB, idb_d), (IDF, idf_d), (ONESB, onesb_d),
                     (ONESF, onesf_d), (ONESROW, onesrow_d)]:
        P.op("sync", lambda E, d=dst, s=src: E.dma_start(out=d[:, :], in_=s),
             inc="sCstS", n=16)

    def p1_sync_tile(t):
        P.wait("sync", "sVxn", t - 1)
        P.op("sync", lambda E, t=t: E.dma_start(
            out=XB[:, 1024 * (t % 2):1024 * (t % 2 + 1)],
            in_=x_d[128 * t:128 * (t + 1), :]), inc="sXld%d" % (t % 2), n=16)
        P.wait("sync", "sVrope", t - 1)
        P.op("sync", lambda E, t=t: E.dma_start(
            out=TRG[:, 128 * (t % 2):128 * (t % 2 + 1)],
            in_=trig_d[128 * t:128 * (t + 1), :]), inc="sTRG%d" % (t % 2), n=16)

    def wa_src(j):
        m, jj = j // 16, j % 16
        rb, h = jj // 2, jj % 2
        if m == 0:
            return wgT_d[128 * rb:128 * (rb + 1), 2048 * h:2048 * (h + 1)]
        if m == 1:
            return wuT_d[128 * rb:128 * (rb + 1), 2048 * h:2048 * (h + 1)]
        return wdT_d.rearrange("(c p two) n -> c p (two n)", two=2, p=128)[jj]

    def p1_sync_passA(j):
        P.wait("sync", "sSabs", j - 1)
        P.op("sync", lambda E, j=j: E.dma_start(
            out=WLD[:, 2048 * (j % 2):2048 * (j % 2 + 1)], in_=wa_src(j)),
            inc="sWA%d" % (j % 2), n=16)

    def p1_vector(t):
        sl = XB[:, 1024 * (t % 2):1024 * (t % 2 + 1)]
        P.wait("vector", "sXld%d" % (t % 2), 16 * (t // 2 + 1))
        P.opf("vector", lambda E, sl=sl: E.reduce_sum(SC1[:, 0:1], sl, axis=AX.X))
        P.opf("vector", lambda E: E.tensor_scalar(SC1[:, 0:1], SC1[:, 0:1],
                                                  1.0 / 1024.0, None, ALU.mult))
        P.wait("vector", "sSsq", t + 1)
        P.opf("vector", lambda E: E.tensor_scalar(SC1[:, 1:2], SC1[:, 2:3],
                                                  1.0 / 1024.0, None, ALU.mult))
        P.opf("vector", lambda E: E.tensor_tensor(SC1[:, 4:5], SC1[:, 0:1],
                                                  SC1[:, 0:1], ALU.mult))
        P.opf("vector", lambda E: E.tensor_sub(SC1[:, 1:2], SC1[:, 1:2], SC1[:, 4:5]))
        P.opf("vector", lambda E: E.tensor_scalar(SC1[:, 1:2], SC1[:, 1:2],
                                                  1e-5, None, ALU.add))
        P.op("vector", lambda E: E.reciprocal(SC1[:, 2:3], SC1[:, 1:2]), inc="sVrec")
        P.opf("vector", lambda E, t=t: E.tensor_copy(MEAN[:, t:t + 1], SC1[:, 0:1]))
        P.wait("vector", "sSrs0", t + 1)
        P.opf("vector", lambda E, t=t: E.tensor_copy(RS[:, t:t + 1], SC1[:, 3:4]))
        P.op("vector", lambda E, sl=sl, t=t: E.tensor_scalar(
            XN[:, :], sl, MEAN[:, t:t + 1], RS[:, t:t + 1],
            ALU.subtract, ALU.mult), inc="sVxn")

    def p1_rope(t):
        P.wait("vector", "sTqkv", t + 1)
        P.wait("vector", "sTRG%d" % (t % 2), 16 * (t // 2 + 1))
        cosv = TRG[:, 128 * (t % 2):128 * (t % 2) + 64]
        sinv = TRG[:, 128 * (t % 2) + 64:128 * (t % 2) + 128]
        for base, off in ((0, 0), (512, 128)):
            ev = PS[:, base:base + 127:2]
            od = PS[:, base + 1:base + 128:2]
            dst = RT[:, off:off + 128]
            scr = RT[:, 256:320]
            last = (base == 512)
            P.opf("vector", lambda E, ev=ev, cosv=cosv, dst=dst: E.tensor_tensor(
                dst[:, 0:127:2], ev, cosv, ALU.mult))
            P.opf("vector", lambda E, od=od, sinv=sinv, scr=scr: E.tensor_tensor(
                scr, od, sinv, ALU.mult))
            P.op("vector", lambda E, dst=dst, scr=scr: E.tensor_sub(
                dst[:, 0:127:2], dst[:, 0:127:2], scr))
            P.opf("vector", lambda E, ev=ev, sinv=sinv, dst=dst: E.tensor_tensor(
                dst[:, 1:128:2], ev, sinv, ALU.mult))
            P.opf("vector", lambda E, od=od, cosv=cosv, scr=scr: E.tensor_tensor(
                scr, od, cosv, ALU.mult))
            P.op("vector", lambda E, dst=dst, scr=scr: E.tensor_add(
                dst[:, 1:128:2], dst[:, 1:128:2], scr),
                inc=("sVrope" if last else None))

    def p1_scalar(t):
        sl = XB[:, 1024 * (t % 2):1024 * (t % 2 + 1)]
        P.wait("scalar", "sXld%d" % (t % 2), 16 * (t // 2 + 1))
        P.wait("scalar", "sVrec", t)
        P.wait("scalar", "sTtr", t)
        P.op("scalar", lambda E, sl=sl: E.activation(
            XN[:, :], sl, AF.Square, accum_out=SC1[:, 2:3]), inc="sSsq")
        P.wait("scalar", "sVrec", t + 1)
        P.wait("scalar", "sVxn", t)
        P.op("scalar", lambda E: E.activation(SC1[:, 3:4], SC1[:, 2:3], AF.Sqrt),
             inc="sSrs0")
        P.wait("scalar", "sTtr", t + 1)
        P.op("scalar", lambda E: E.copy(XNT[:, 0:512], PS[:, 3072:3584]))
        P.op("scalar", lambda E: E.copy(XNT[:, 512:1024], PS[:, 3584:4096]),
             inc="sSxnt")
        P.wait("scalar", "sTqkv", t + 1)
        P.op("scalar", lambda E, t=t: E.copy(VT[:, 128 * t:128 * (t + 1)],
                                             PS[:, 1024:1152]), inc="sSv")
        if t >= 1:
            P.wait("scalar", "sTqk2", t)
            P.op("scalar", lambda E, t=t: E.copy(
                QF[:, 128 * (t - 1):128 * t], PS[:, 1536:1664]))
            P.op("scalar", lambda E, t=t: E.copy(
                KF[:, 128 * (t - 1):128 * t], PS[:, 1664:1792]), inc="sSqk")

    def passA_chunk(j):
        P.wait("scalar", "sWA%d" % (j % 2), 16 * (j // 2 + 1))
        P.wait("scalar", "sVacc", j - 1)
        P.op("scalar", lambda E, j=j: E.activation(
            WLD[:, 2048 * (j % 2):2048 * (j % 2 + 1)],
            WLD[:, 2048 * (j % 2):2048 * (j % 2 + 1)], AF.Abs,
            accum_out=PAP[:, j % 2:j % 2 + 1]), inc="sSabs")
        acc = A["ACCG"] if j < 16 else (A["ACCU"] if j < 32 else A["ACCD"])
        P.wait("vector", "sSabs", j + 1)
        if j % 16 == 0:
            P.op("vector", lambda E, acc=acc, j=j: E.tensor_copy(
                acc[:, :], PAP[:, j % 2:j % 2 + 1]), inc="sVacc")
        else:
            P.op("vector", lambda E, acc=acc, j=j: E.tensor_add(
                acc[:, :], acc[:, :], PAP[:, j % 2:j % 2 + 1]), inc="sVacc")
        P.wait("vector", "sVacc", P.c("sVacc"))

    def p1_tensor(t):
        if t >= 1:
            P.wait("tensor", "sVrope", t)
            P.wait("tensor", "sSqk", t - 1)
            P.op("tensor", lambda E: E.transpose(PS[:, 1536:1664], RT[:, 0:128],
                                                 IDF[:, :]))
            P.op("tensor", lambda E: E.transpose(PS[:, 1664:1792], RT[:, 128:256],
                                                 IDF[:, :]), inc="sTqk2")
        P.wait("tensor", "sVxn", t + 1)
        P.wait("tensor", "sSxnt", t)
        for d in range(8):
            base = 3072 + 128 * d
            P.op("tensor", lambda E, d=d, base=base: E.transpose(
                PS[:, base:base + 128], XN[:, 128 * d:128 * (d + 1)], IDF[:, :]),
                inc=("sTtr" if d == 7 else None))
        P.wait("tensor", "sSxnt", t + 1)
        P.wait("tensor", "sVrope", t)
        P.wait("tensor", "sSv", t)
        for d in range(8):
            lhsT = XNT[:, 128 * d:128 * (d + 1)]
            st, sp = (d == 0), (d == 7)
            P.op("tensor", lambda E, lhsT=lhsT, d=d, st=st, sp=sp: E.matmul(
                PS[:, 0:128], lhsT, QKVW[:, 384 * d:384 * d + 128],
                start=st, stop=sp))
            P.op("tensor", lambda E, lhsT=lhsT, d=d, st=st, sp=sp: E.matmul(
                PS[:, 512:640], lhsT, QKVW[:, 384 * d + 128:384 * d + 256],
                start=st, stop=sp))
            P.op("tensor", lambda E, lhsT=lhsT, d=d, st=st, sp=sp: E.matmul(
                PS[:, 1024:1152], lhsT, QKVW[:, 384 * d + 256:384 * d + 384],
                start=st, stop=sp), inc=("sTqkv" if sp else None))


    P.wait("tensor", "sCstG", P.c("sCstG"))
    P.wait("tensor", "sCstS", P.c("sCstS"))
    for t in range(NT):
        p1_sync_tile(t)
        p1_vector(t)
        p1_rope(t)
        p1_scalar(t)
        p1_tensor(t)
        for j in range(min(NWA, (3 * t) // 2), min(NWA, (3 * (t + 1)) // 2)):
            p1_sync_passA(j)
            passA_chunk(j)
    P.wait("tensor", "sVrope", NT)
    P.wait("tensor", "sSqk", NT - 1)
    P.op("tensor", lambda E: E.transpose(PS[:, 1536:1664], RT[:, 0:128], IDF[:, :]))
    P.op("tensor", lambda E: E.transpose(PS[:, 1664:1792], RT[:, 128:256],
                                         IDF[:, :]), inc="sTqk2")
    P.wait("scalar", "sTqk2", NT)
    P.op("scalar", lambda E: E.copy(QF[:, 128 * (NT - 1):128 * NT],
                                    PS[:, 1536:1664]))
    P.op("scalar", lambda E: E.copy(KF[:, 128 * (NT - 1):128 * NT],
                                    PS[:, 1664:1792]), inc="sSqk")
    for j in range((3 * NT) // 2, NWA):
        p1_sync_passA(j)
        passA_chunk(j)

    P.wait("vector", "sSqk", NT)
    P.wait("tensor", "sSqk", NT)
    P.wait("tensor", "sSv", NT)
    P.wait("tensor", "sSxnt", NT)
    for w, src in ((0, KF), (1, QF)):
        for c in range(NT):
            P.wait("vector", "sTnorm", 32 * w + c - 1)
            P.op("vector", lambda E, src=src, c=c: E.tensor_tensor(
                SQS[:, 128 * (c % 2):128 * (c % 2 + 1)],
                src[:, 128 * c:128 * (c + 1)], src[:, 128 * c:128 * (c + 1)],
                ALU.mult), inc="sVnorm")
            P.wait("tensor", "sVnorm", 32 * w + c + 1)
            if w == 1 and c == 0:
                P.wait("tensor", "sVbp", 1)
            P.op("tensor", lambda E, c=c: E.matmul(
                PS[0:1, 128 * c:128 * (c + 1)], ONESB[:, :],
                SQS[:, 128 * (c % 2):128 * (c % 2 + 1)], start=True, stop=True),
                inc="sTnorm")
        P.wait("vector", "sTnorm", 32 * (w + 1))
        P.op("vector", lambda E, w=w: E.reduce_max(
            SC2[0:1, w:w + 1], PS[0:1, 0:4096], axis=AX.X), inc="sVbp")
        P.wait("vector", "sVbp", P.c("sVbp"))
    P.op("vector", lambda E: E.tensor_tensor(SC2[0:1, 2:3], SC2[0:1, 0:1],
                                             SC2[0:1, 1:2], ALU.mult), inc="sVbp")
    P.wait("vector", "sVbp", P.c("sVbp"))
    P.wait("scalar", "sVbp", P.c("sVbp"))
    P.op("scalar", lambda E: E.activation(SC2[0:1, 3:4], SC2[0:1, 2:3], AF.Sqrt),
         inc="sSbp")
    P.wait("vector", "sSbp", P.c("sSbp"))
    P.opf("vector", lambda E: E.tensor_scalar(SC2[0:1, 4:5], SC2[0:1, 3:4],
                                              INV_SQRT_HD, 1.0, ALU.mult, ALU.add))
    P.op("vector", lambda E: E.tensor_scalar(SC2[0:1, 4:5], SC2[0:1, 4:5],
                                             -1.0, None, ALU.mult), inc="sVbp")
    P.wait("vector", "sVbp", P.c("sVbp"))
    VBP1 = P.c("sVbp")
    P.wait("tensor", "sVacc", NWA)
    P.wait("tensor", "sVbp", VBP1)
    for i, acc in enumerate(("ACCG", "ACCU", "ACCD")):
        P.op("tensor", lambda E, i=i, acc=acc: E.matmul(
            PS[0:1, 2048 + 16 * i:2049 + 16 * i], ONESF[:, :], A[acc][:, :],
            start=True, stop=True), inc="sTbp")
    P.wait("vector", "sTbp", P.c("sTbp"))
    for i in range(3):
        P.op("vector", lambda E, i=i: E.tensor_scalar(
            SC2[0:1, 5 + i:6 + i], PS[0:1, 2048 + 16 * i:2049 + 16 * i],
            1.0 / N_ELEM_W, 1e-5, ALU.mult, ALU.max), inc="sVbp")
    P.wait("tensor", "sVbp", P.c("sVbp"))
    for i, col in enumerate((4, 5, 6, 7)):
        P.op("tensor", lambda E, col=col, i=i: E.matmul(
            PS[:, 2560 + 16 * i:2561 + 16 * i], ONESROW[:, :],
            SC2[0:1, col:col + 1], start=True, stop=True), inc="sTbp")
    P.wait("vector", "sTbp", P.c("sTbp"))
    P.op("vector", lambda E: E.tensor_copy(NEGB[:, :], PS[:, 2560:2561]))
    for i, (sw, mw) in enumerate((("SWG", "MWG127n"), ("SWU", "MWU127n"),
                                  ("SWD", "MWD127n"))):
        src = PS[:, 2576 + 16 * i:2577 + 16 * i]
        P.opf("vector", lambda E, sw=sw, src=src: E.reciprocal(A[sw][:, :], src))
        P.opf("vector", lambda E, mw=mw, src=src: E.tensor_scalar(
            A[mw][:, :], src, -1.0 / 127.0, None, ALU.mult))
    P.op("vector", lambda E: E.tensor_copy(SC1[:, 0:1], SC1[:, 0:1]), inc="sVbp")
    BP_DONE = P.c("sVbp")

    P.wait("sync", "sVbp", BP_DONE)

    def wb_src(u):
        if u < 16:
            rb, h = (u % 16) // 2, u % 2
            return wgT_d[128 * rb:128 * (rb + 1), 2048 * h:2048 * (h + 1)]
        if u < 32:
            uu = u - 16
            rb, h = uu // 2, uu % 2
            return wuT_d[128 * rb:128 * (rb + 1), 2048 * h:2048 * (h + 1)]
        c = u - 32
        return wdT_d[128 * c:128 * (c + 1), :]

    for u in range(NWB):
        P.wait("sync", "sWBdone", u - 1)
        if u >= 34:
            c_prev = u - 34
            P.wait("sync", "sDQ%d" % (c_prev % 2), 16 * (c_prev // 2 + 1))
        ncols = 2048 if u < 32 else 1024
        P.op("sync", lambda E, u=u, ncols=ncols: E.dma_start(
            out=WLB[:, 2048 * (u % 2):2048 * (u % 2) + ncols], in_=wb_src(u)),
            inc="sWB%d" % (u % 2), n=16)
        if u >= 32:
            c = u - 32
            P.wait("sync", "sWBdone", u + 1)
            P.op("sync", lambda E, c=c: E.dma_start(
                out=wdq_d[128 * c:128 * (c + 1), :], in_=WDQA[c % 2][:, :]),
                inc="sDQ%d" % (c % 2), n=16)

    def passB_unit(u):
        ncols = 2048 if u < 32 else 1024
        sl = WLB[:, 2048 * (u % 2):2048 * (u % 2) + ncols]
        sw = A["SWG"] if u < 16 else (A["SWU"] if u < 32 else A["SWD"])
        P.wait("vector", "sWB%d" % (u % 2), 16 * (u // 2 + 1))
        for half in range(ncols // 1024):
            hsl = sl[:, 1024 * half:1024 * (half + 1)]
            P.op("vector", lambda E, hsl=hsl, sw=sw: E.tensor_scalar(
                TL[:, :], hsl, sw[:, :], -0.5, ALU.mult, ALU.is_lt))
            P.op("vector", lambda E, hsl=hsl, sw=sw: E.tensor_scalar(
                TG[:, :], hsl, sw[:, :], 0.5, ALU.mult, ALU.is_gt))
            if u < 32:
                dstW = WG if u < 16 else WU
                cw = u % 16
                rb, hh = cw // 2, cw % 2
                dst = dstW[:, 4096 * rb + 2048 * hh + 1024 * half:
                           4096 * rb + 2048 * hh + 1024 * (half + 1)]
                last = (half == ncols // 1024 - 1)
                P.op("vector", lambda E, dst=dst: E.tensor_sub(dst, TG[:, :], TL[:, :]),
                     inc=("sWBdone" if last else None))
            else:
                c = u - 32
                if c >= 2:
                    P.wait("vector", "sDQ%d" % (c % 2), 16 * (c // 2))
                P.op("vector", lambda E, c=c: E.tensor_sub(
                    WDQA[c % 2][:, :], TG[:, :], TL[:, :]), inc="sWBdone")

    P.wait("tensor", "sVbp", BP_DONE)
    P.wait("scalar", "sVbp", BP_DONE)

    def a2_tensor_scores(i, g):
        P.wait("tensor", "sSexp", 8 * i + g - 1)
        for k in range(4):
            c = 4 * g + k
            bank = 2 * (g % 2) + k // 2
            off = 512 * bank + 256 * (k % 2)
            P.op("tensor", lambda E, c=c, off=off, i=i: E.matmul(
                PS[:, off:off + 256], KF[:, 128 * c:128 * (c + 1)],
                QF[:, 256 * i:256 * (i + 1)], start=True, stop=True),
                inc=("sTsc" if k == 3 else None))

    def a2_tensor_pv(i, g):
        P.wait("tensor", "sSexp", 8 * i + g + 1)
        if g == 0:
            P.wait("tensor", "sSatt", i)
        for k in range(4):
            c = 4 * g + k
            P.op("tensor", lambda E, c=c: E.matmul(
                PS[:, 2048:2304], VT[:, 128 * c:128 * (c + 1)],
                PT[:, 256 * c:256 * (c + 1)], start=(c == 0), stop=(c == 31)),
                inc=("sTpv" if k == 3 else None))

    for i in range(QI):
        for g in range(8):
            a2_tensor_scores(i, g)
            if g >= 1:
                a2_tensor_pv(i, g - 1)
        a2_tensor_pv(i, 7)
        P.wait("tensor", "sSden", i)
        for c in range(32):
            P.op("tensor", lambda E, c=c: E.matmul(
                PS[0:1, 2560:2816], ONESB[:, :], PT[:, 256 * c:256 * (c + 1)],
                start=(c == 0), stop=(c == 31)), inc=("sTden" if c == 31 else None))
        P.wait("tensor", "sSden", i + 1)
        P.wait("tensor", "sVrd", i)
        P.op("tensor", lambda E: E.transpose(PS[:, 3072:3073], DEN[0:1, 0:128],
                                             IDF[0:1, 0:1]))
        P.op("tensor", lambda E: E.transpose(PS[:, 3073:3074], DEN[0:1, 128:256],
                                             IDF[0:1, 0:1]), inc="sTrd")
        for g in range(8):
            if g == 0:
                P.wait("scalar", "sTden", i)
            P.wait("scalar", "sTsc", 8 * i + g + 1)
            src = PS[:, 1024 * (g % 2):1024 * (g % 2) + 1024]
            P.op("scalar", lambda E, src=src, g=g: E.activation(
                PT[:, 1024 * g:1024 * (g + 1)], src, AF.Exp,
                bias=NEGB[:, :], scale=INV_SQRT_HD), inc="sSexp")
        P.wait("scalar", "sTpv", 8 * (i + 1))
        P.op("scalar", lambda E, i=i: E.copy(ATT[:, 256 * i:256 * (i + 1)],
                                             PS[:, 2048:2304]), inc="sSatt")
        P.wait("scalar", "sTden", i + 1)
        P.op("scalar", lambda E: E.copy(DEN[0:1, :], PS[0:1, 2560:2816]),
             inc="sSden")
        P.wait("vector", "sTrd", i + 1)
        P.op("vector", lambda E, i=i: E.reciprocal(RDEN[:, 2 * i:2 * i + 2],
                                                   PS[:, 3072:3074]), inc="sVrd")
        P.wait("vector", "sVrd", P.c("sVrd"))
        for u in range(4 * i, 4 * i + 4):
            passB_unit(u)

    DQ_ALL = 16 * 32
    P.wait("sync", "sTden", QI)
    P.wait("sync", "sDQ0", 16 * 16)
    P.wait("sync", "sDQ1", 16 * 16)
    P.wait("sync", "sWBdone", NWB)
    P.wait("tensor", "sVrd", QI)
    P.wait("tensor", "sSatt", QI)
    P.wait("tensor", "sWBdone", NWB)
    P.wait("vector", "sDQ0", 16 * 16)
    P.wait("vector", "sDQ1", 16 * 16)
    P.wait("scalar", "sVrd", QI)

    P.wait("gpsimd", "sDQ0", 16 * 16)
    P.wait("gpsimd", "sDQ1", 16 * 16)
    P.wait("gpsimd", "sTden", QI)
    for t in range(NT + 2):
        if t < NT:
            if t >= 3:
                P.wait("sync", "sOut%d" % ((t - 3) % 2), 16 * ((t - 3) // 2 + 1))
            P.op("sync", lambda E, t=t: E.dma_start(
                out=YO[:, 1024 * (t % 3):1024 * (t % 3 + 1)],
                in_=x_d[128 * t:128 * (t + 1), :]), inc="sX2ld", n=16)
        if t >= 2:
            to = t - 2
            P.wait("sync", "sVout", 2 * to + 2)
            P.op("sync", lambda E, to=to: E.dma_start(
                out=out_d[128 * to:128 * (to + 1), :],
                in_=YO[:, 1024 * (to % 3):1024 * (to % 3 + 1)]),
                inc="sOut%d" % (to % 2), n=16)
        if 1 <= t <= NT:
            td = t - 1
            for c in range(32):
                P.wait("gpsimd", "sTdn", 32 * td + c - 7)
                P.op("gpsimd", lambda E, c=c: E.dma_start(
                    out=WD[:, 1024 * (c % 8):1024 * (c % 8 + 1)],
                    in_=wdq_d[128 * c:128 * (c + 1), :]),
                    inc="sWDl%d" % (c % 8), n=16)
    P.wait("sync", "sOut0", 16 * (NT // 2))
    P.wait("sync", "sOut1", 16 * (NT // 2))
    if dbg:
        dbg_specs = [
            ("NEGB", [128, 1], F32), ("RDEN", [128, NT], F32),
            ("MEAN", [128, NT], F32), ("RS", [128, NT], F32),
            ("SWG", [128, 1], F32), ("SWU", [128, 1], F32), ("SWD", [128, 1], F32),
            ("MWG127n", [128, 1], F32), ("MWD127n", [128, 1], F32),
            ("CT", [128, 8], F32), ("DEN", [1, 256], F32),
            ("ATT", [128, S], BF16), ("KF", [128, S], BF16),
            ("QF", [128, S], BF16), ("VT", [128, S], BF16),
            ("WG", [128, 8 * INNER], BF16),
            ("HQT", [128, 4096], BF16), ("ATX", [128, 1024], BF16),
            ("H", [128, 4096], F32), ("XN2", [128, 1024], F32),
            ("YO", [128, 2048], F32), ("SC2", [1, 8], F32),
        ]
        for nm, shp, dt in dbg_specs:
            od = nc.dram_tensor("dbg_" + nm, shp, dt, kind="ExternalOutput").ap()
            P.op("sync", lambda E, od=od, nm=nm: E.dma_start(out=od, in_=A[nm][:, :]),
                 inc="sOut0", n=16)
        P.wait("sync", "sOut0", 16 * (NT // 2) + 16 * len(dbg_specs))

    def b2_front_tensor(t):
        s = t % 2
        P.wait("tensor", "sVout", 2 * (t - 1))
        if t >= 1:
            P.wait("tensor", "sVh", 8 * (t - 1) + 6)
            P.wait("tensor", "sSgl", 8 * (t - 1) + 6)
        for b in range(2):
            P.wait("tensor", "sVaq", 2 * t + b + 1)
            if b == 1:
                P.wait("tensor", "sSat2", 2 * t + 1)
            for k in range(4):
                d = 4 * b + k
                P.op("tensor", lambda E, d=d, k=k: E.transpose(
                    PS[:, 3072 + 128 * k:3072 + 128 * (k + 1)],
                    XN2[:, 128 * d:128 * (d + 1)], IDF[:, :]),
                    inc=("sTat2" if k == 3 else None))
        if t >= 1:
            P.wait("tensor", "sSat2", 2 * t + 2)
            for b in range(8):
                P.wait("tensor", "sVq", 8 * (t - 1) + b + 1)
                if b >= 1:
                    P.wait("tensor", "sShq", 8 * (t - 1) + b)
                for k in range(4):
                    c = 4 * b + k
                    P.op("tensor", lambda E, c=c, k=k: E.transpose(
                        PS[:, 3072 + 128 * k:3072 + 128 * (k + 1)],
                        H[:, 128 * c:128 * (c + 1)], IDF[:, :]),
                        inc=("sThq" if k == 3 else None))
        P.wait("tensor", "sSat2", 2 * (t + 1))
        for e in range(8):
            se = (e + 2) % 4
            P.wait("tensor", "sVh", 8 * t + e - 3)
            P.wait("tensor", "sSgl", 8 * t + e - 3)
            gcol, ucol = 1024 * se, 1024 * se + 512
            wcol = 2048 * (e // 4) + 512 * (e % 4)
            for d in range(8):
                st, sp = (d == 0), (d == 7)
                P.op("tensor", lambda E, d=d, st=st, sp=sp, gcol=gcol, wcol=wcol:
                     E.matmul(PS[:, gcol:gcol + 512], ATX[:, 128 * d:128 * (d + 1)],
                              WG[:, 4096 * d + wcol:4096 * d + wcol + 512],
                              start=st, stop=sp))
                P.op("tensor", lambda E, d=d, st=st, sp=sp, ucol=ucol, wcol=wcol:
                     E.matmul(PS[:, ucol:ucol + 512], ATX[:, 128 * d:128 * (d + 1)],
                              WU[:, 4096 * d + wcol:4096 * d + wcol + 512],
                              start=st, stop=sp), inc=("sTgu" if sp else None))

    def b2_last_hqt():
        t = NT
        P.wait("tensor", "sSat2", 2 * NT)
        P.wait("tensor", "sVout", 2 * NT - 2)
        for b in range(8):
            P.wait("tensor", "sVq", 8 * (t - 1) + b + 1)
            if b >= 1:
                P.wait("tensor", "sShq", 8 * (t - 1) + b)
            for k in range(4):
                c = 4 * b + k
                P.op("tensor", lambda E, c=c, k=k: E.transpose(
                    PS[:, 3072 + 128 * k:3072 + 128 * (k + 1)],
                    H[:, 128 * c:128 * (c + 1)], IDF[:, :]),
                    inc=("sThq" if k == 3 else None))

    def b2_tail_tensor(t):
        s = t % 2
        P.wait("tensor", "sVh", min(8 * (t + 1) + 6, 8 * NT))
        P.wait("tensor", "sSgl", min(8 * (t + 1) + 6, 8 * NT))
        P.wait("tensor", "sVout", 2 * t - 1)
        P.op("tensor", lambda E, t=t: E.matmul(
            PS[:, 2048:2560], ATT[:, 128 * t:128 * (t + 1)], WO[:, 0:512],
            start=True, stop=True))
        P.op("tensor", lambda E, t=t: E.matmul(
            PS[:, 2560:3072], ATT[:, 128 * t:128 * (t + 1)], WO[:, 512:1024],
            start=True, stop=True), inc="sTpj")
        P.wait("tensor", "sShq", 8 * (t + 1))
        P.wait("tensor", "sVout", 2 * t)
        for c in range(32):
            P.wait("tensor", "sWDl%d" % (c % 8), 16 * (4 * t + c // 8 + 1))
            st, sp = (c == 0), (c == 31)
            P.op("tensor", lambda E, c=c, st=st, sp=sp: E.matmul(
                PS[:, 3072:3584], HQT[:, 128 * c:128 * (c + 1)],
                WD[:, 1024 * (c % 8):1024 * (c % 8) + 512], start=st, stop=sp))
            P.op("tensor", lambda E, c=c, st=st, sp=sp: E.matmul(
                PS[:, 3584:4096], HQT[:, 128 * c:128 * (c + 1)],
                WD[:, 1024 * (c % 8) + 512:1024 * (c % 8 + 1)], start=st, stop=sp),
                inc="sTdn")

    qs_cnt = {"s": 0, "v": 0}

    def quant_scalar_sub(z_slice, scale_ap, wait_sem, wait_val):
        g = qs_cnt["s"]; qs_cnt["s"] += 1
        sl = 512 * (g % 2)
        P.wait("scalar", wait_sem, wait_val)
        P.wait("scalar", "sVqd", g - 1)
        P.op("scalar", lambda E, z_slice=z_slice, scale_ap=scale_ap, sl=sl:
             E.activation(QS1[:, sl:sl + 512], z_slice, AF.Identity, scale=scale_ap))
        P.op("scalar", lambda E, z_slice=z_slice, scale_ap=scale_ap, sl=sl:
             E.activation(QS2[:, sl:sl + 512], z_slice, AF.Identity, scale=scale_ap),
             inc="sSqp")

    def quant_vector_sub(z_slice, inc_last=None):
        g = qs_cnt["v"]; qs_cnt["v"] += 1
        sl = 512 * (g % 2)
        P.wait("vector", "sSqp", g + 1)
        P.op("vector", lambda E, sl=sl: E.scalar_tensor_tensor(
            QS1[:, sl:sl + 512], QS2[:, sl:sl + 512], -1.0, QS1[:, sl:sl + 512],
            ALU.mult, ALU.add))
        P.op("vector", lambda E, z_slice=z_slice, sl=sl: E.scalar_tensor_tensor(
            z_slice, QS1[:, sl:sl + 512], 0.5, QS2[:, sl:sl + 512],
            ALU.is_ge, ALU.add))
        P.op("vector", lambda E, z_slice=z_slice, sl=sl: E.scalar_tensor_tensor(
            z_slice, QS1[:, sl:sl + 512], -0.5, z_slice,
            ALU.is_lt, ALU.subtract), inc=inc_last)
        P.op("vector", lambda E: E.tensor_copy(SC1[:, 5:6], SC1[:, 5:6]), inc="sVqd")

    def b2_prep_vector(t):
        s3 = t % 3
        P.wait("vector", "sX2ld", 16 * (t + 1))
        P.wait("vector", "sTat2", 2 * t)
        P.op("vector", lambda E, t=t, s3=s3: E.tensor_scalar(
            XN2[:, :], YO[:, 1024 * s3:1024 * (s3 + 1)],
            MEAN[:, t:t + 1], RS[:, t:t + 1], ALU.subtract, ALU.mult))
        P.opf("vector", lambda E: E.reduce_max(SC1[:, 0:1], XN2[:, :], axis=AX.X,
                                               apply_absolute_value=True))
        P.opf("vector", lambda E: E.tensor_scalar(SC1[:, 0:1], SC1[:, 0:1],
                                                  1e-5, None, ALU.max))
        P.opf("vector", lambda E: E.reciprocal(SC1[:, 1:2], SC1[:, 0:1]))
        P.opf("vector", lambda E, s3=s3: E.tensor_scalar(
            CT[:, 4 * s3:4 * s3 + 1], SC1[:, 1:2], 127.0, None, ALU.mult))
        P.opf("vector", lambda E, s3=s3: E.tensor_tensor(
            CT[:, 4 * s3 + 1:4 * s3 + 2], SC1[:, 0:1], A["MWG127n"][:, :], ALU.mult))
        P.opf("vector", lambda E, s3=s3: E.tensor_tensor(
            CT[:, 4 * s3 + 2:4 * s3 + 3], SC1[:, 0:1], A["MWU127n"][:, :], ALU.mult))
        P.op("vector", lambda E: E.tensor_copy(SC1[:, 6:7], SC1[:, 6:7]), inc="sVln")
        for k in range(2):
            quant_vector_sub(XN2[:, 512 * k:512 * (k + 1)], inc_last="sVaq")

    def b2_eighths_vector(t):
        s3 = t % 3
        for e in range(8):
            se = (e + 2) % 4
            P.wait("vector", "sSgl", 8 * t + e + 1)
            ucol = 1024 * se + 512
            P.op("vector", lambda E, e=e, ucol=ucol, s3=s3: E.scalar_tensor_tensor(
                H[:, 512 * e:512 * (e + 1)], PS[:, ucol:ucol + 512],
                CT[:, 4 * s3 + 2:4 * s3 + 3], H[:, 512 * e:512 * (e + 1)],
                ALU.mult, ALU.mult), inc="sVh")
            P.opf("vector", lambda E, e=e: E.reduce_max(
                SC1[:, 4:5], H[:, 512 * e:512 * (e + 1)], axis=AX.X,
                apply_absolute_value=True))
            if e == 0:
                P.opf("vector", lambda E: E.tensor_copy(SC1[:, 2:3], SC1[:, 4:5]))
            else:
                P.opf("vector", lambda E: E.tensor_tensor(
                    SC1[:, 2:3], SC1[:, 2:3], SC1[:, 4:5], ALU.max))

    def b2_quant_vector_scales(t):
        s3 = t % 3
        P.opf("vector", lambda E: E.tensor_scalar(SC1[:, 2:3], SC1[:, 2:3],
                                                  1e-5, None, ALU.max))
        P.opf("vector", lambda E: E.reciprocal(SC1[:, 3:4], SC1[:, 2:3]))
        P.opf("vector", lambda E: E.tensor_scalar(SC1[:, 3:4], SC1[:, 3:4],
                                                  127.0, None, ALU.mult))
        P.opf("vector", lambda E, s3=s3: E.tensor_tensor(
            CT[:, 4 * s3 + 3:4 * s3 + 4], SC1[:, 2:3], A["MWD127n"][:, :], ALU.mult))
        P.op("vector", lambda E: E.tensor_copy(SC1[:, 7:8], SC1[:, 7:8]), inc="sVsh")

    def b2_quant_vector_fixups(t):
        P.wait("vector", "sThq", 8 * t)
        for k in range(8):
            quant_vector_sub(H[:, 512 * k:512 * (k + 1)], inc_last="sVq")

    def b2_tail_vector(t):
        s3 = t % 3
        P.wait("vector", "sTpj", t + 1)
        P.op("vector", lambda E, t=t, s3=s3: E.scalar_tensor_tensor(
            YO[:, 1024 * s3:1024 * (s3 + 1)], PS[:, 2048:3072],
            RDEN[:, t:t + 1], YO[:, 1024 * s3:1024 * (s3 + 1)],
            ALU.mult, ALU.add), inc="sVout")
        P.wait("vector", "sTdn", 32 * (t + 1))
        P.op("vector", lambda E, t=t, s3=s3: E.scalar_tensor_tensor(
            YO[:, 1024 * s3:1024 * (s3 + 1)], PS[:, 3072:4096],
            CT[:, 4 * s3 + 3:4 * s3 + 4], YO[:, 1024 * s3:1024 * (s3 + 1)],
            ALU.mult, ALU.add), inc="sVout")

    def b2_scalar(t):
        s = t % 3
        P.wait("scalar", "sTgu", 8 * t)
        for b in range(2):
            P.wait("scalar", "sTat2", 2 * t + b + 1)
            P.op("scalar", lambda E, b=b: E.copy(
                ATX[:, 512 * b:512 * (b + 1)], PS[:, 3072:3584]), inc="sSat2")
        if t >= 1:
            P.wait("scalar", "sTdn", 32 * (t - 1))
            for b in range(8):
                P.wait("scalar", "sThq", 8 * (t - 1) + b + 1)
                P.op("scalar", lambda E, b=b: E.copy(
                    HQT[:, 512 * b:512 * (b + 1)], PS[:, 3072:3584]), inc="sShq")
        P.wait("scalar", "sVq", 8 * t)
        for e in range(8):
            se = (e + 2) % 4
            P.wait("scalar", "sTgu", 8 * t + e + 1)
            gcol = 1024 * se
            P.op("scalar", lambda E, e=e, gcol=gcol, s=s: E.activation(
                H[:, 512 * e:512 * (e + 1)], PS[:, gcol:gcol + 512], AF.Silu,
                scale=CT[:, 4 * s + 1:4 * s + 2]), inc="sSgl")
        if t + 1 < NT:
            s3n = (t + 1) % 3
            for k in range(2):
                quant_scalar_sub(XN2[:, 512 * k:512 * (k + 1)],
                                 CT[:, 4 * s3n:4 * s3n + 1], "sVln", t + 2)
        for k in range(8):
            quant_scalar_sub(H[:, 512 * k:512 * (k + 1)], SC1[:, 3:4],
                             "sVsh", t + 1)

    def b2_last_scalar_hq():
        t = NT
        P.wait("scalar", "sTdn", 32 * (t - 1))
        for b in range(8):
            P.wait("scalar", "sThq", 8 * (t - 1) + b + 1)
            P.op("scalar", lambda E, b=b: E.copy(
                HQT[:, 512 * b:512 * (b + 1)], PS[:, 3072:3584]), inc="sShq")

    P.op("vector", lambda E: E.memset(SC1[:, 4:8], 0.0))
    for k in range(2):
        quant_scalar_sub(XN2[:, 512 * k:512 * (k + 1)], CT[:, 0:1], "sVln", 1)
    b2_prep_vector(0)
    for t in range(NT + 1):
        if t < NT:
            b2_front_tensor(t)
            b2_scalar(t)
            b2_eighths_vector(t)
            b2_quant_vector_scales(t)
            if t + 1 < NT:
                b2_prep_vector(t + 1)
            b2_quant_vector_fixups(t)
            if t >= 1:
                b2_tail_vector(t - 1)
        else:
            b2_last_hqt()
            b2_last_scalar_hq()
            b2_tail_vector(NT - 1)
        if t >= 1:
            b2_tail_tensor(t - 1)

    with nc.Block() as block:
        @block.sync
        def _(E):
            for fn in P.streams["sync"]:
                fn(E)

        @block.tensor
        def _(E):
            for fn in P.streams["tensor"]:
                fn(E)

        @block.vector
        def _(E):
            for fn in P.streams["vector"]:
                fn(E)

        @block.scalar
        def _(E):
            for fn in P.streams["scalar"]:
                fn(E)

        @block.gpsimd
        def _(E):
            for fn in P.streams["gpsimd"]:
                fn(E)

    return nc


_NC_CACHE = None


def _get_nc():
    global _NC_CACHE
    if _NC_CACHE is None:
        _NC_CACHE = build_nc()
    return _NC_CACHE


def _host_constants():
    inv_freq = (1.0 / (np.float32(10000.0) **
                       (np.arange(0, HDIM, 2, dtype=np.float32) / np.float32(HDIM))))
    inv_freq = inv_freq.astype(np.float32)
    tpos = np.arange(S - 1, dtype=np.float32)
    ang = (tpos[:, None] * inv_freq[None, :]).astype(np.float32)
    cos = np.cos(ang).astype(np.float32)
    sin = np.sin(ang).astype(np.float32)
    trig = np.zeros((S, 128), np.float32)
    trig[0, 0:64] = 1.0
    trig[1:, 0:64] = cos
    trig[1:, 64:128] = sin
    idb = np.eye(128, dtype=BF16_NP)
    idf = np.eye(128, dtype=np.float32)
    onesb = np.ones((128, 1), BF16_NP)
    onesf = np.ones((128, 1), np.float32)
    onesrow = np.ones((1, 128), np.float32)
    return trig, idb, idf, onesb, onesf, onesrow


def kernel(L0_attn_x, L0_attn_wq, L0_attn_wk, L0_attn_wv, L0_attn_wo,
           L0_ffn_w_gate, L0_ffn_w_up, L0_ffn_w_down):
    x = np.ascontiguousarray(np.asarray(L0_attn_x, dtype=np.float32))
    wq = np.asarray(L0_attn_wq, np.float32)
    wk = np.asarray(L0_attn_wk, np.float32)
    wv = np.asarray(L0_attn_wv, np.float32)
    wo = np.asarray(L0_attn_wo, np.float32)
    qkvT = np.ascontiguousarray(
        np.concatenate([wq.T, wk.T, wv.T], axis=1)).astype(np.float32)
    woT = np.ascontiguousarray(wo.T)
    wgT = np.ascontiguousarray(np.asarray(L0_ffn_w_gate, np.float32).T)
    wuT = np.ascontiguousarray(np.asarray(L0_ffn_w_up, np.float32).T)
    wdT = np.ascontiguousarray(np.asarray(L0_ffn_w_down, np.float32).T)
    trig, idb, idf, onesb, onesf, onesrow = _host_constants()

    nc = _get_nc()
    in_maps = []
    for b in range(B):
        in_maps.append({
            "x": np.ascontiguousarray(x[b]),
            "qkvT": qkvT, "woT": woT,
            "wgT": wgT, "wuT": wuT, "wdT": wdT,
            "trig": trig, "idb": idb, "idf": idf,
            "onesb": onesb, "onesf": onesf, "onesrow": onesrow,
        })
    res = run_bass_kernel_spmd(nc, in_maps, core_ids=list(range(B)))
    out = np.stack([np.asarray(res.results[b]["out"], dtype=np.float32)
                    for b in range(B)], axis=0)
    return out
